# version 08
# baseline (speedup 1.0000x reference)
import contextlib
import numpy as np
import ml_dtypes
import concourse.bass as bass
import concourse.mybir as mybir
from concourse.bass_utils import run_bass_kernel_spmd

F32 = mybir.dt.float32
BF16 = mybir.dt.bfloat16
F16 = mybir.dt.float16
AF = mybir.ActivationFunctionType
ALU = mybir.AluOpType
AX = mybir.AxisListType
AP = bass.AP

MAGIC = 12582912.0
Q15 = 32767.0

SEQ, T, C, H, HS, II = 2, 2048, 1024, 16, 64, 3072
TOK = SEQ * T
CCH = C // 128
ICH = II // 128
NTT = TOK // 128
NCORE = 8

ENGINES = ("sync", "vector", "scalar", "tensor", "gpsimd")


class Ctx:
    def __init__(self, block, sems, nc=None, es=None):
        self.block = block
        self.sem = sems
        self.cnt = {k: 0 for k in sems}
        self._nc = nc
        self._es = es

    def mksem(self, name):
        h = self._es.enter_context(self._nc.semaphore(name))
        self.sem[name] = h
        self.cnt[name] = 0
        return name

    def tok(self, s):
        return (s, self.cnt[s])

    def wt(self, eng, token):
        s, v = token
        if v > 0:
            eng.wait_ge(self.sem[s], v)

    def b(self, instr, s, amt=1):
        self.cnt[s] += amt
        instr.then_inc(self.sem[s], amt)
        return self.cnt[s]

    def b16(self, instr, s):
        return self.b(instr, s, 16)

    def w(self, eng, s, val=None):
        if val is None:
            val = self.cnt[s]
        if val > 0:
            eng.wait_ge(self.sem[s], val)

    def barrier(self):
        snap = dict(self.cnt)
        for ename in ENGINES:
            def body(e, snap=snap):
                for s, v in snap.items():
                    if v > 0:
                        e.wait_ge(self.sem[s], v)
            getattr(self.block, ename)(body)


def build_nc():
    nc = bass.Bass("TRN2", target_bir_lowering=False)

    x_d = nc.dram_tensor("x", [TOK, C], F32, kind="ExternalInput").ap()
    wqkT_d = nc.dram_tensor("wqkT", [H, C, 192], F32, kind="ExternalInput").ap()
    wpT_d = nc.dram_tensor("wpT", [C, C], F32, kind="ExternalInput").ap()
    bproj_d = nc.dram_tensor("bproj", [1, C], F32, kind="ExternalInput").ap()
    wg_d = nc.dram_tensor("wg", [II, C], F32, kind="ExternalInput").ap()
    wgT_d = nc.dram_tensor("wgT", [C, II], F32, kind="ExternalInput").ap()
    wu_d = nc.dram_tensor("wu", [II, C], F32, kind="ExternalInput").ap()
    wuT_d = nc.dram_tensor("wuT", [C, II], F32, kind="ExternalInput").ap()
    wd_d = nc.dram_tensor("wd", [C, II], F32, kind="ExternalInput").ap()
    wdT_d = nc.dram_tensor("wdT", [II, C], F32, kind="ExternalInput").ap()
    identb_d = nc.dram_tensor("identb_in", [128, 128], BF16, kind="ExternalInput").ap()
    identf_d = nc.dram_tensor("identf_in", [128, 128], F32, kind="ExternalInput").ap()
    maskneg_d = nc.dram_tensor("maskneg_in", [128, 128], F32, kind="ExternalInput").ap()
    ones1_d = nc.dram_tensor("ones1_in", [1, 128], BF16, kind="ExternalInput").ap()
    out_d = nc.dram_tensor("out", [TOK, C], F32, kind="ExternalOutput").ap()

    wqkb_d = nc.dram_tensor("wqkb", [H, C, 192], BF16).ap()
    wpb_d = nc.dram_tensor("wpb", [C, C], BF16).ap()
    srg_d = nc.dram_tensor("srg", [64, II], F32).ap()
    sru_d = nc.dram_tensor("sru", [64, II], F32).ap()
    srd_d = nc.dram_tensor("srd", [192, C], F32).ap()
    wgq_d = nc.dram_tensor("wgq", [C, II], BF16).ap()
    wuq_d = nc.dram_tensor("wuq", [C, II], BF16).ap()
    wdq_d = nc.dram_tensor("wdq", [II, C], BF16).ap()
    qk_sp = nc.dram_tensor("qk_sp", [SEQ * 8 * 2 * 128, T], BF16).ap()
    v_sp = nc.dram_tensor("v_sp", [SEQ * 128, 16 * C], BF16).ap()
    ao_sp = nc.dram_tensor("ao_sp", [TOK, C], F16).ap()
    g_sp = nc.dram_tensor("g_sp", [II, TOK], BF16).ap()
    u_sp = nc.dram_tensor("u_sp", [II, TOK], BF16).ap()
    pt_sp = nc.dram_tensor("pt_sp", [II, TOK], BF16).ap()
    pu_sp = nc.dram_tensor("pu_sp", [II, TOK], BF16).ap()
    cc_in = [nc.dram_tensor(f"cc_in{i}", [1, 16], F32).ap() for i in range(4)]
    cc_out = [nc.dram_tensor(f"cc_out{i}", [1, 16], F32).ap() for i in range(4)]
    sc_d = nc.dram_tensor("sc_d", [4, 16], F32).ap()

    al = nc.sbuf_tensor

    with contextlib.ExitStack() as _es1:
        xnT = _es1.enter_context(al("xnT", [128, CCH * TOK], BF16))
        xt0 = _es1.enter_context(al("xt0", [128, C], F32))
        xt1 = _es1.enter_context(al("xt1", [128, C], F32))
        stats = _es1.enter_context(al("stats", [128, 16], F32))
        macc = _es1.enter_context(al("macc", [128, 416], F32))
        wql0 = _es1.enter_context(al("wql0", [128, 1024], BF16))
        wql1 = _es1.enter_context(al("wql1", [128, 1024], BF16))
        gath = _es1.enter_context(al("gath", [1, 128], F32))
        scal = _es1.enter_context(al("scal", [1, 16], F32))
        scalg = _es1.enter_context(al("scalg", [1, 16], F32))
        sres = _es1.enter_context(al("sres", [1, 16], F32))
        identb = _es1.enter_context(al("identb_sb", [128, 128], BF16))
        identf = _es1.enter_context(al("identf_sb", [128, 128], F32))
        maskneg = _es1.enter_context(al("maskneg_sb", [128, 128], F32))
        ones1 = _es1.enter_context(al("ones1_sb", [1, 128], BF16))
        bproj = _es1.enter_context(al("bproj_sb", [1, C], F32))
        bprojb = _es1.enter_context(al("bprojb", [1, C], BF16))
        bc = _es1.enter_context(al("bc", [128, 8], F32))
        stg0 = _es1.enter_context(al("stg0", [128, 1024], BF16))
        stg1 = _es1.enter_context(al("stg1", [128, 1024], BF16))
        stgh0 = _es1.enter_context(al("stgh0", [128, 512], F16))
        stgh1 = _es1.enter_context(al("stgh1", [128, 512], F16))
        outs0 = _es1.enter_context(al("outs0", [128, 512], F32))
        outs1 = _es1.enter_context(al("outs1", [128, 512], F32))
        ps0 = _es1.enter_context(nc.psum_tensor("ps0", [128, 512], F32))
        ps1 = _es1.enter_context(nc.psum_tensor("ps1", [128, 512], F32))
        scps = _es1.enter_context(nc.psum_tensor("scps", [128, 2048], F32))
        ps2 = scps[:, 0:512]
        ps3 = scps[:, 512:1024]
        ps4 = scps[:, 1024:1536]
        ps5 = scps[:, 1536:2048]
        ps6 = _es1.enter_context(nc.psum_tensor("ps6", [128, 1024], BF16))
        ps7 = _es1.enter_context(nc.psum_tensor("ps7", [128, 512], F32))
        smT = _es1.enter_context(nc.semaphore("smT"))
        smV = _es1.enter_context(nc.semaphore("smV"))
        smS = _es1.enter_context(nc.semaphore("smS"))
        smG = _es1.enter_context(nc.semaphore("smG"))
        smCC = _es1.enter_context(nc.semaphore("smCC"))
        block = _es1.enter_context(nc.Block())
        ctx = Ctx(block, dict(T=smT, V=smV, S=smS, G=smG, CC=smCC),
                  nc=nc, es=_es1)
        for _snm in ("dP0", "dWL", "dWS", "dA0", "dA1", "dAS", "dB0", "dB1",
                     "dBS0", "dBS1", "dX0", "dX1", "dMG", "dCC", "dVW",
                     "dQW", "dVS0", "dVS1", "dPW", "dVB", "dQK", "dPS0",
                     "dPS1", "dFW0", "dFW1", "dFS0", "dFS1", "dG0", "dG1",
                     "dTS0", "dTS1", "dR0", "dR1", "dW50", "dW51", "dAO0",
                     "dAO1", "dOUT0", "dOUT1", "dTG0", "dTG1",
                     "dRX0", "dRX1", "dRX2", "dA2", "dB2", "dBS2"):
            ctx.mksem(_snm)
        B = block
        sbanks = [ps2, ps3, ps4, ps5]

        def p0(sync):
            ctx.b16(sync.dma_start(out=identb[:, :], in_=identb_d), "dP0")
            ctx.b16(sync.dma_start(out=identf[:, :], in_=identf_d), "dP0")
            ctx.b16(sync.dma_start(out=maskneg[:, :], in_=maskneg_d), "dP0")
            ctx.b16(sync.dma_start(out=ones1[:, :], in_=ones1_d), "dP0")
            ctx.b16(sync.dma_start(out=bproj[:, :], in_=bproj_d), "dP0")
        B.sync(p0)

        def p0b(s):
            ctx.w(s, "dP0")
            ctx.b(s.activation(bprojb[:, :], bproj[:, :], AF.Copy,
                               bias=0.0, scale=1.0), "S")
        B.scalar(p0b)
        ctx.barrier()

        ev_bc = [0]

        def collective(idx, post_math, bslots):
            def go(g):
                ctx.w(g, "V")
                ctx.b16(g.dma_start(out=cc_in[idx], in_=scal[:, :]), "dCC")
                ctx.w(g, "dCC")
                ctx.b(g.collective_compute(
                    "AllReduce", ALU.max,
                    replica_groups=[list(range(NCORE))],
                    ins=[cc_in[idx]], outs=[cc_out[idx]]), "CC")
                ctx.w(g, "CC")
                ctx.b16(g.dma_start(out=scalg[:, :], in_=cc_out[idx]), "dCC")
            B.gpsimd(go)

            def pm(v):
                ctx.w(v, "dCC")
                post_math(v)
                ctx.b(v.drain(), "V")
            B.vector(pm)

            def bounce(g, idx=idx):
                ctx.w(g, "V")
                ctx.b16(g.dma_start(out=sc_d[idx:idx + 1, :],
                                    in_=sres[:, :]), "dCC")
                ctx.w(g, "dCC")
                for (bcol, cix) in bslots:
                    src = AP(sc_d.tensor, idx * 16 + cix, [[0, 128], [1, 1]])
                    ctx.b16(g.dma_start(
                        out=bc[:, bcol:bcol + 1], in_=src), "dCC")
                ctx.w(g, "dCC")
                g.sem_inc(ctx.sem["G"], 1)
                ctx.cnt["G"] += 1
                ev_bc[0] = ctx.cnt["G"]
            B.gpsimd(bounce)

        def c1_math(v):
            v.tensor_scalar(sres[0:1, 0:1], scalg[0:1, 0:1],
                            1.0 / Q15, 1e-12, ALU.mult, ALU.max)
            v.drain()
            v.reciprocal(sres[0:1, 1:2], sres[0:1, 0:1])

        with contextlib.ExitStack() as _es2:
            wsA0 = _es2.enter_context(al("wsA0", [128, II], F32))
            wsA1 = _es2.enter_context(al("wsA1", [128, II], F32))
            wsA2 = _es2.enter_context(al("wsA2", [128, II], F32))
            wsB0 = _es2.enter_context(al("wsB0", [128, 512], F32))
            wsB1 = _es2.enter_context(al("wsB1", [128, 512], F32))
            wsB2 = _es2.enter_context(al("wsB2", [128, 512], F32))
            sxp0 = _es2.enter_context(al("sxp0", [128, 512], F32))
            sxp1 = _es2.enter_context(al("sxp1", [128, 512], F32))
            sxp2 = _es2.enter_context(al("sxp2", [128, 512], F32))
            rxp0 = _es2.enter_context(al("rxp0", [128, 512], F32))
            rxp1 = _es2.enter_context(al("rxp1", [128, 512], F32))
            rxp2 = _es2.enter_context(al("rxp2", [128, 512], F32))
            srn = _es2.enter_context(al("srn", [128, 192], F32))
            srev = _es2.enter_context(al("srev", [128, 256], F32))
            wqb0 = _es2.enter_context(al("wqb0", [128, 512], BF16))
            wqb1 = _es2.enter_context(al("wqb1", [128, 512], BF16))
            wqb2 = _es2.enter_context(al("wqb2", [128, 512], BF16))
            pqb = _es2.enter_context(al("pqb", [128, 512], F32))
            wkst = _es2.enter_context(al("wkst", [128, 1536], F32))
            wkbf = _es2.enter_context(al("wkbf", [128, 1536], BF16))
            xsq = _es2.enter_context(al("xsq", [128, C], F32))
            xnf = _es2.enter_context(al("xnf", [128, C], F32))

            sched = []
            s_base = ctx.cnt["S"]
            s_off = [0]
            asbar = {}

            def cast_gen():
                specs = [("qkv", h) for h in range(H)] + \
                        [("wp", ccb) for ccb in range(CCH)]
                evW = [0]
                evSt = [0]
                prev_prom = [None]
                for (kind, idx) in specs:
                    def ld(g, kind=kind, idx=idx, pp=prev_prom[0]):
                        if pp is not None:
                            ctx.w(g, "S", pp)
                        if kind == "qkv":
                            src = AP(wqkT_d.tensor, idx * C * 192,
                                     [[192, 128], [192 * 128, 8], [1, 192]])
                            evW[0] = ctx.b16(
                                g.dma_start(out=wkst[:, :], in_=src), "dWL")
                        else:
                            evW[0] = ctx.b16(g.dma_start(
                                out=wkst[:, 0:1024],
                                in_=wpT_d[idx * 128:(idx + 1) * 128, :]),
                                "dWL")
                    B.gpsimd(ld)
                    myW = evW[0]
                    mySt = evSt[0]
                    s_off[0] += 1
                    prom = s_base + s_off[0]

                    def cast(s, kind=kind, myW=myW, mySt=mySt, prom=prom):
                        ctx.w(s, "dWL", myW)
                        ctx.w(s, "dWS", mySt)
                        if kind == "qkv":
                            qv = AP(wkst, 0, [[1536, 128], [192, 8], [1, 64]])
                            qo = AP(wkbf, 0, [[1536, 128], [192, 8], [1, 64]])
                            s.activation(qo, qv, AF.Copy, bias=0.0,
                                         scale=0.125)
                            kv = AP(wkst, 64,
                                    [[1536, 128], [192, 8], [1, 128]])
                            ko = AP(wkbf, 64,
                                    [[1536, 128], [192, 8], [1, 128]])
                            got = ctx.b(s.activation(ko, kv, AF.Copy,
                                                     bias=0.0, scale=1.0),
                                        "S")
                        else:
                            got = ctx.b(s.activation(
                                wkbf[:, 0:1024], wkst[:, 0:1024], AF.Copy,
                                bias=0.0, scale=1.0), "S")
                        assert got == prom
                    sched.append(cast)
                    prev_prom[0] = prom

                    def st(g, kind=kind, idx=idx, prom=prom):
                        ctx.w(g, "S", prom)
                        if kind == "qkv":
                            dst = AP(wqkb_d.tensor, idx * C * 192,
                                     [[192, 128], [192 * 128, 8], [1, 192]])
                            evSt[0] = ctx.b16(
                                g.dma_start(out=dst, in_=wkbf[:, :]), "dWS")
                        else:
                            evSt[0] = ctx.b16(g.dma_start(
                                out=wpb_d[idx * 128:(idx + 1) * 128, :],
                                in_=wkbf[:, 0:1024]), "dWS")
                    B.gpsimd(st)
                    yield

            def A_gen():
                specs = []
                for (nat_d, sr_d, nrc, icols, nblk, srstride) in (
                        (wg_d, srg_d, ICH, C, 32, II),
                        (wu_d, sru_d, ICH, C, 32, II),
                        (wd_d, srd_d, CCH, II, 96, C)):
                    for rc in range(nrc):
                        specs.append((nat_d, sr_d, rc, icols, nblk,
                                      srstride))
                nA = len(specs)
                wsl = [wsA0, wsA1, wsA2]
                evV, evT, evG = {}, {}, {}

                def emit_ld(g):
                    (nat_d, sr_d, rc, icols, nblk, srstride) = specs[g]
                    ws = wsl[g % 3]

                    def ld(sync, g=g, rc=rc, ws=ws, icols=icols,
                           nat_d=nat_d):
                        if g >= 3:
                            ctx.w(sync, "V", evV[g - 3])
                        evV[(g, "ld")] = ctx.b16(sync.dma_start(
                            out=ws[:, 0:icols],
                            in_=nat_d[rc * 128:(rc + 1) * 128, :]),
                            ("dA0", "dA1", "dA2")[g % 3])
                    B.sync(ld)

                def emit_rest(g):
                    (nat_d, sr_d, rc, icols, nblk, srstride) = specs[g]
                    ws = wsl[g % 3]

                    def red(v, g=g, ws=ws, nblk=nblk):
                        ctx.w(v, ("dA0", "dA1", "dA2")[g % 3],
                              evV[(g, "ld")])
                        if g >= 1:
                            ctx.w(v, "T", evT[g - 1])
                        src = AP(ws, 0, [[II, 128], [32, nblk], [1, 32]])
                        v.reduce_max(srn[:, 0:nblk], src, AX.X,
                                     apply_absolute_value=True)
                        v.drain()
                        v.tensor_scalar(srn[:, 0:nblk], srn[:, 0:nblk],
                                        1.0 / 7.0, 1e-12, ALU.mult, ALU.max)
                        v.drain()
                        v.reciprocal(srn[:, nblk:2 * nblk], srn[:, 0:nblk])
                        evV[g] = ctx.b(v.drain(), "V")
                    B.vector(red)

                    def tr(t, g=g, nblk=nblk):
                        ctx.w(t, "V", evV[g])
                        if g >= 1:
                            ctx.w(t, "V", evV[(g - 1, "e")])
                        if nblk <= 64:
                            ins = t.transpose(ps5[0:2 * nblk, 0:128],
                                              srn[:, 0:2 * nblk],
                                              identf[:, :])
                        else:
                            t.transpose(ps5[0:nblk, 0:128], srn[:, 0:nblk],
                                        identf[:, :])
                            ins = t.transpose(ps5[0:nblk, 128:256],
                                              srn[:, nblk:2 * nblk],
                                              identf[:, :])
                        evT[g] = ctx.b(ins, "T")
                    B.tensor(tr)

                    def vev(v, g=g, nblk=nblk):
                        ctx.w(v, "T", evT[g])
                        if g >= 1:
                            ctx.w(v, "dAS", evG[g - 1])
                        if nblk <= 64:
                            ins = v.tensor_copy(srev[0:2 * nblk, 0:128],
                                                ps5[0:2 * nblk, 0:128])
                        else:
                            v.tensor_copy(srev[0:nblk, 0:128],
                                          ps5[0:nblk, 0:128])
                            ins = v.tensor_copy(srev[0:nblk, 128:256],
                                                ps5[0:nblk, 128:256])
                        evV[(g, "e")] = ctx.b(ins, "V")
                    B.vector(vev)

                    def st(gp, g=g, rc=rc, nblk=nblk, srstride=srstride,
                           sr_d=sr_d):
                        ctx.w(gp, "V", evV[(g, "e")])
                        if nblk <= 64:
                            dst = AP(sr_d.tensor, rc * 128,
                                     [[srstride, 2 * nblk], [1, 128]])
                            evG[g] = ctx.b16(
                                gp.dma_start(out=dst,
                                             in_=srev[0:2 * nblk, 0:128]),
                                "dAS")
                        else:
                            d1 = AP(sr_d.tensor, rc * 128,
                                    [[srstride, nblk], [1, 128]])
                            ctx.b16(gp.dma_start(out=d1,
                                                 in_=srev[0:nblk, 0:128]),
                                    "dAS")
                            d2 = AP(sr_d.tensor, nblk * srstride + rc * 128,
                                    [[srstride, nblk], [1, 128]])
                            evG[g] = ctx.b16(
                                gp.dma_start(out=d2,
                                             in_=srev[0:nblk, 128:256]),
                                "dAS")
                    B.gpsimd(st)

                for g in range(nA + 2):
                    if g < nA:
                        emit_ld(g)
                    if g >= 2:
                        emit_rest(g - 2)
                        if g - 2 == 23:
                            asbar[0] = ctx.cnt["dAS"]
                        elif g - 2 == 47:
                            asbar[1] = ctx.cnt["dAS"]
                        elif g - 2 == 55:
                            asbar[2] = ctx.cnt["dAS"]
                    yield

            def B_gen():
                specs = []
                for w, (wT_src, sr_d, wq_dst, nrc, ocols, rstride) in \
                        enumerate(((wgT_d, srg_d, wgq_d, CCH, II, 32),
                                   (wuT_d, sru_d, wuq_d, CCH, II, 32),
                                   (wdT_d, srd_d, wdq_d, ICH, C, 96))):
                    for rc in range(nrc):
                        for it in range(ocols // 512):
                            specs.append((w, wT_src, sr_d, wq_dst, rc, it,
                                          ocols, rstride))
                nB = len(specs)
                wsl = [wsB0, wsB1, wsB2]
                sxl = [sxp0, sxp1, sxp2]
                rxl = [rxp0, rxp1, rxp2]
                wql = [wqb0, wqb1, wqb2]
                evV, evG = {}, {}

                def emit_ld(k):
                    (w, wT_src, sr_d, wq_dst, rc, it, ocols, rstride) = \
                        specs[k]
                    ws, sx, rx_ = wsl[k % 3], sxl[k % 3], rxl[k % 3]

                    def ld(sync, w=w, rc=rc, it=it, ws=ws, sx=sx,
                           k=k, ocols=ocols, sr_d=sr_d, wT_src=wT_src):
                        ctx.w(sync, "dAS", asbar[w])
                        if k >= 3:
                            ctx.w(sync, "V", evV[k - 3])
                        dsm = ("dB0", "dB1", "dB2")[k % 3]
                        ctx.b16(sync.dma_start(
                            out=ws[:, :],
                            in_=wT_src[rc * 128:(rc + 1) * 128,
                                       it * 512:(it + 1) * 512]), dsm)
                        soff = 4 * rc * ocols + it * 512
                        evV[(k, "ld")] = ctx.b16(
                            sync.dma_start(out=sx[:, :], in_=AP(
                                sr_d.tensor, soff,
                                [[ocols, 4], [0, 32], [1, 512]])), dsm)
                    B.sync(ld)

                    def ldr(g, w=w, rc=rc, it=it, rx_=rx_, k=k,
                            ocols=ocols, rstride=rstride, sr_d=sr_d):
                        ctx.w(g, "dAS", asbar[w])
                        if k >= 3:
                            ctx.w(g, "V", evV[k - 3])
                        soff = 4 * rc * ocols + it * 512
                        evV[(k, "ldr")] = ctx.b16(
                            g.dma_start(out=rx_[:, :], in_=AP(
                                sr_d.tensor, rstride * ocols + soff,
                                [[ocols, 4], [0, 32], [1, 512]])),
                            ("dRX0", "dRX1", "dRX2")[k % 3])
                    B.gpsimd(ldr)

                def emit_rest(k):
                    (w, wT_src, sr_d, wq_dst, rc, it, ocols, rstride) = \
                        specs[k]
                    ws, sx, rx_ = wsl[k % 3], sxl[k % 3], rxl[k % 3]
                    wq = wql[k % 3]

                    def qz(v, ws=ws, sx=sx, rx_=rx_, wq=wq, k=k):
                        ctx.w(v, ("dB0", "dB1", "dB2")[k % 3],
                              evV[(k, "ld")])
                        ctx.w(v, ("dRX0", "dRX1", "dRX2")[k % 3],
                              evV[(k, "ldr")])
                        if k >= 3:
                            ctx.w(v, ("dBS0", "dBS1", "dBS2")[(k - 3) % 3],
                                  evG[k - 3])
                        v.drain()
                        v.tensor_mul(pqb[:, :], ws[:, :], rx_[:, :])
                        v.drain()
                        v.tensor_single_scalar(pqb[:, :], pqb[:, :],
                                               MAGIC, ALU.add)
                        v.drain()
                        evV[k] = ctx.b(v.scalar_tensor_tensor(
                            wq[:, :], pqb[:, :], -MAGIC, sx[:, :],
                            ALU.add, ALU.mult), "V")
                    B.vector(qz)

                    def st(g, rc=rc, it=it, wq=wq, k=k, wq_dst=wq_dst):
                        ctx.w(g, "V", evV[k])
                        evG[k] = ctx.b16(g.dma_start(
                            out=wq_dst[rc * 128:(rc + 1) * 128,
                                       it * 512:(it + 1) * 512],
                            in_=wq[:, :]),
                            ("dBS0", "dBS1", "dBS2")[k % 3])
                    B.gpsimd(st)

                for k in range(nB + 2):
                    if k < nB:
                        emit_ld(k)
                    if k >= 2:
                        emit_rest(k - 2)
                    yield

            cg = cast_gen()
            ag = A_gen()
            bg = B_gen()

            ev_xn, ev_tp, ev_ev = {}, {}, {}
            for tt in range(NTT):
                xt = xt0 if tt % 2 == 0 else xt1

                def ld(sync, tt=tt, xt=xt):
                    if tt >= 2:
                        ctx.w(sync, "V", ev_xn[tt - 2])
                    ev_xn[(tt, "ld")] = ctx.b16(sync.dma_start(
                        out=xt[:, :], in_=x_d[tt * 128:(tt + 1) * 128, :]),
                        "dX0" if tt % 2 == 0 else "dX1")
                B.sync(ld)

                def st1(v, tt=tt, xt=xt):
                    ctx.w(v, "dX0" if tt % 2 == 0 else "dX1",
                          ev_xn[(tt, "ld")])
                    v.tensor_mul(xsq[:, :], xt[:, :], xt[:, :])
                    v.drain()
                    v.reduce_sum(stats[:, 0:1], xt[:, :], AX.X)
                    v.reduce_sum(stats[:, 1:2], xsq[:, :], AX.X)
                    v.drain()
                    v.tensor_scalar(stats[:, 2:3], stats[:, 0:1],
                                    1.0 / C, None, ALU.mult)
                    v.drain()
                    v.tensor_mul(stats[:, 3:4], stats[:, 2:3], stats[:, 2:3])
                    v.drain()
                    v.scalar_tensor_tensor(stats[:, 4:5], stats[:, 1:2],
                                           1.0 / C, stats[:, 3:4],
                                           ALU.mult, ALU.subtract)
                    v.drain()
                    v.tensor_single_scalar(stats[:, 4:5], stats[:, 4:5],
                                           1e-5, ALU.add)
                    v.drain()
                    v.reciprocal(stats[:, 5:6], stats[:, 4:5])
                    ctx.b(v.drain(), "V")
                B.vector(st1)

                vcap = ctx.cnt["V"]
                s_off[0] += 1
                prom = s_base + s_off[0]

                def sq(s, vcap=vcap, prom=prom):
                    s.wait_ge(ctx.sem["V"], vcap)
                    got = ctx.b(s.activation(stats[:, 6:7], stats[:, 5:6],
                                             AF.Sqrt, bias=0.0, scale=1.0),
                                "S")
                    assert got == prom
                sched.append(sq)

                def st3(v, prom=prom):
                    ctx.w(v, "S", prom)
                    v.tensor_scalar(stats[:, 7:8], stats[:, 2:3], -1.0, None,
                                    ALU.mult)
                    ctx.b(v.drain(), "V")
                B.vector(st3)

                def xn_act(v, tt=tt, xt=xt):
                    if tt >= 1:
                        ctx.w(v, "T", ev_tp[tt - 1])
                    ev_xn[tt] = ctx.b(v.tensor_scalar(
                        xnf[:, :], xt[:, :], stats[:, 7:8], stats[:, 6:7],
                        ALU.add, ALU.mult), "V")
                B.vector(xn_act)

                def m1p(v, tt=tt):
                    v.drain()
                    ctx.b(v.reduce_max(macc[:, tt:tt + 1], xnf[:, :], AX.X,
                                       apply_absolute_value=True), "V")
                B.vector(m1p)

                next(ag, None)

                def tp(t, tt=tt):
                    ctx.w(t, "V", ev_xn[tt])
                    if tt >= 1:
                        ctx.w(t, "V", ev_ev[tt - 1])
                    for bk in range(2):
                        pb = ps3 if bk == 0 else ps4
                        for kx in range(4):
                            ci = 4 * bk + kx
                            ins = t.transpose(pb[:, kx * 128:(kx + 1) * 128],
                                              xnf[:, ci * 128:(ci + 1) * 128],
                                              identf[:, :])
                    ev_tp[tt] = ctx.b(ins, "T")
                B.tensor(tp)

                def ev(v, tt=tt):
                    ctx.w(v, "T", ev_tp[tt])
                    for bk in range(2):
                        dst = AP(xnT, (4 * bk) * TOK + tt * 128,
                                 [[CCH * TOK, 128], [TOK, 4], [1, 128]])
                        ins = v.tensor_copy(dst, AP(
                            scps, 512 + bk * 512,
                            [[2048, 128], [128, 4], [1, 128]]))
                    ev_ev[tt] = ctx.b(ins, "V")
                B.vector(ev)

                next(ag, None)
                next(cg, None)

            for _ in cg:
                pass
            for _ in ag:
                pass

            def ln_sbody(s):
                for fn in sched:
                    fn(s)
            B.scalar(ln_sbody)

            def m1fin(v):
                v.drain()
                v.reduce_max(stats[:, 8:9], macc[:, 0:NTT], AX.X)
                ctx.b(v.drain(), "V")
            B.vector(m1fin)

            def m1g(g):
                ctx.w(g, "V")
                ctx.b16(g.dma_start(out=gath[:, :], in_=stats[:, 8:9]), "dMG")
            B.gpsimd(m1g)

            def m1f2(v):
                ctx.w(v, "dMG")
                v.memset(scal[:, :], 0.0)
                v.memset(sres[:, :], 0.0)
                v.drain()
                v.reduce_max(scal[0:1, 0:1], gath[0:1, :], AX.X)
                ctx.b(v.drain(), "V")
            B.vector(m1f2)
            ctx.barrier()

            collective(0, c1_math, [(0, 1)])

            with contextlib.ExitStack() as _es3:
                wvw = _es3.enter_context(al("wvw", [128, 2 * CCH * 512], BF16))
                wqkp = _es3.enter_context(al("wqkp", [128, CCH * 256], BF16))
                qkev = _es3.enter_context(al("qkev", [128, 512], BF16))
                qkev2 = _es3.enter_context(al("qkev2", [128, 512], BF16))

                def ldv(sync):
                    for cc in range(CCH):
                        for vg in range(2):
                            src = AP(wqkb_d.tensor,
                                     (8 * vg) * C * 192 + (cc * 128) * 192 + 128,
                                     [[192, 128], [C * 192, 8], [1, 64]])
                            ctx.b16(sync.dma_start(
                                out=wvw[:, (cc * 2 + vg) * 512:(cc * 2 + vg + 1) * 512],
                                in_=src), "dVW")
                B.sync(ldv)

                ev_vev, ev_vst = {}, {}
                k = 0
                for sq in range(SEQ):
                    for vg in range(2):
                        for tt in range(16):
                            pb = ps0 if k % 2 == 0 else ps1

                            def mm(t, sq=sq, vg=vg, tt=tt, pb=pb, k=k):
                                ctx.w(t, "dVW", 256)
                                if k >= 2:
                                    ctx.w(t, "V", ev_vev[k - 2])
                                gt = sq * 16 + tt
                                for cc in range(CCH):
                                    ins = t.matmul(
                                        pb[:, :],
                                        xnT[:, cc * TOK + gt * 128:cc * TOK + (gt + 1) * 128],
                                        wvw[:, (cc * 2 + vg) * 512:(cc * 2 + vg + 1) * 512],
                                        start=(cc == 0), stop=(cc == CCH - 1))
                                ctx.b(ins, "T")
                            B.tensor(mm)

                            def ev(v, pb=pb, k=k):
                                ctx.w(v, "T")
                                qv = qkev if k % 2 == 0 else qkev2
                                if k >= 2:
                                    ctx.w(v, "dVS0" if k % 2 == 0 else "dVS1",
                                          ev_vst[k - 2])
                                ev_vev[k] = ctx.b(v.tensor_copy(qv[:, :], pb[:, :]),
                                                  "V")
                            B.vector(ev)

                            def st(g, sq=sq, vg=vg, tt=tt, k=k):
                                ctx.w(g, "V", ev_vev[k])
                                qv = qkev if k % 2 == 0 else qkev2
                                dst = AP(v_sp.tensor,
                                         sq * 128 * (16 * C) + tt * 1024 + vg * 512,
                                         [[16 * C, 128], [1, 512]])
                                ev_vst[k] = ctx.b16(
                                    g.dma_start(out=dst, in_=qv[:, :]),
                                    "dVS0" if k % 2 == 0 else "dVS1")
                            B.gpsimd(st)
                            next(bg, None)
                            k += 1

                ev_qkev, ev_qkst = {}, {}
                kk = 0
                for sq in range(SEQ):
                    for pr in range(8):
                        def ldqk(sync, sq=sq, pr=pr):
                            ctx.w(sync, "T")
                            for cc in range(CCH):
                                src = AP(wqkb_d.tensor,
                                         (2 * pr) * C * 192 + (cc * 128) * 192,
                                         [[192, 128], [C * 192, 2], [1, 128]])
                                ctx.b16(sync.dma_start(
                                    out=wqkp[:, cc * 256:(cc + 1) * 256],
                                    in_=src), "dQW")
                            ev_qkev[(sq, pr, "ld")] = ctx.cnt["dQW"]
                        B.sync(ldqk)
                        next(bg, None)

                        for tr in range(4):
                            def mm(t, sq=sq, pr=pr, tr=tr, kk=kk):
                                ctx.w(t, "dQW", ev_qkev[(sq, pr, "ld")])
                                if kk >= 1:
                                    ctx.w(t, "V", ev_qkev[kk - 1])
                                else:
                                    ctx.w(t, "V")
                                gt = sq * 4 + tr
                                for cc in range(CCH):
                                    rhs = xnT[:, cc * TOK + gt * 512:cc * TOK + (gt + 1) * 512]
                                    s_ = (cc == 0)
                                    p_ = (cc == CCH - 1)
                                    t.matmul(ps0[0:64, :],
                                             wqkp[:, cc * 256:cc * 256 + 64],
                                             rhs, start=s_, stop=p_,
                                             tile_position=(0, 0))
                                    t.matmul(ps2[64:128, :],
                                             wqkp[:, cc * 256 + 128:cc * 256 + 192],
                                             rhs, start=s_, stop=p_,
                                             tile_position=(0, 64))
                                    t.matmul(ps1[0:64, :],
                                             wqkp[:, cc * 256 + 64:cc * 256 + 128],
                                             rhs, start=s_, stop=p_,
                                             tile_position=(0, 0))
                                    ins = t.matmul(ps3[64:128, :],
                                                   wqkp[:, cc * 256 + 192:cc * 256 + 256],
                                                   rhs, start=s_, stop=p_,
                                                   tile_position=(0, 64))
                                ctx.b(ins, "T")
                            B.tensor(mm)

                            def ev(v, kk=kk):
                                ctx.w(v, "T")
                                if kk >= 1:
                                    ctx.w(v, "dVS1" if kk % 2 == 0 else "dVS0",
                                          ev_qkst[kk - 1])
                                else:
                                    ctx.w(v, "dVS0")
                                    ctx.w(v, "dVS1")
                                v.tensor_copy(qkev[0:64, :], ps0[0:64, :])
                                v.tensor_copy(qkev[64:128, :], ps2[64:128, :])
                                v.tensor_copy(qkev2[0:64, :], ps1[0:64, :])
                                ev_qkev[kk] = ctx.b(
                                    v.tensor_copy(qkev2[64:128, :],
                                                  ps3[64:128, :]), "V")
                            B.vector(ev)

                            def st(g, sq=sq, pr=pr, tr=tr, kk=kk):
                                ctx.w(g, "V", ev_qkev[kk])
                                base = ((sq * 8 + pr) * 2) * 128 * T
                                dq = AP(qk_sp.tensor, base + tr * 512,
                                        [[T, 128], [1, 512]])
                                dk = AP(qk_sp.tensor, base + 128 * T + tr * 512,
                                        [[T, 128], [1, 512]])
                                dsm = "dVS0" if kk % 2 == 0 else "dVS1"
                                ctx.b16(g.dma_start(out=dq, in_=qkev[:, :]), dsm)
                                ev_qkst[kk] = ctx.b16(
                                    g.dma_start(out=dk, in_=qkev2[:, :]), dsm)
                            B.gpsimd(st)
                            next(bg, None)
                            kk += 1

                for _ in bg:
                    pass
                ctx.barrier()

        _esP4b = contextlib.ExitStack()
        ub0 = _esP4b.enter_context(al("ub0", [128, C], F32))
        ub1 = _esP4b.enter_context(al("ub1", [128, C], F32))
        ev_i2 = {}
        rx_lams = []
        RX0 = ctx.cnt["S"]
        for j in range(NTT):
            ub = ub0 if j % 2 == 0 else ub1

            def mk_i1(j=j, ub=ub, vprev=ev_i2.get(j - 2, 0)):
                def f(s):
                    if j == 0:
                        ctx.w(s, "G", ev_bc[0])
                    if vprev > 0:
                        s.wait_ge(ctx.sem["V"], vprev)
                    got = ctx.b(s.activation(
                        ub[:, :], xnT[:, j * 1024:(j + 1) * 1024],
                        AF.Copy, bias=MAGIC, scale=bc[:, 0:1]), "S")
                    assert got == RX0 + j + 1
                return f
            rx_lams.append(mk_i1())

            def i2(v, j=j, ub=ub):
                ctx.w(v, "S", RX0 + j + 1)
                ev_i2[j] = ctx.b(v.tensor_single_scalar(
                    xnT[:, j * 1024:(j + 1) * 1024], ub[:, :],
                    -MAGIC, ALU.add), "V")
            B.vector(i2)

        def rx_sbody(s):
            for f in rx_lams:
                f(s)
        B.scalar(rx_sbody)
        ctx.barrier()
        _esP4b.close()

        def ffn_units():
            ev_mm, ev_ev2, ev_st2 = {}, {}, {}
            k = 0
            for gw in range(2 * ICH):
                wq_src = wgq_d if gw < ICH else wuq_d
                sp_dst = g_sp if gw < ICH else u_sp
                ic = gw % ICH
                wql = wql0 if gw % 2 == 0 else wql1

                def ldw(sync, gw=gw, ic=ic, wql=wql, wq_src=wq_src):
                    if gw >= 2:
                        ctx.w(sync, "T", ev_mm[(gw - 2, 7)])
                    dsm = "dFW0" if gw % 2 == 0 else "dFW1"
                    for cc in range(CCH):
                        ctx.b16(sync.dma_start(
                            out=wql[:, cc * 128:(cc + 1) * 128],
                            in_=wq_src[cc * 128:(cc + 1) * 128,
                                       ic * 128:(ic + 1) * 128]), dsm)
                    ev_mm[(gw, "ld")] = ctx.cnt[dsm]
                B.sync(ldw)

                for t8 in range(8):
                    mcol = 32 + k if k < 8 * ICH else 224 + (k - 8 * ICH)

                    def mm(t, gw=gw, t8=t8, k=k, wql=wql):
                        ctx.w(t, "dFW0" if gw % 2 == 0 else "dFW1",
                              ev_mm[(gw, "ld")])
                        if k >= 1:
                            ctx.w(t, "V", ev_ev2[k - 1])
                        for cc in range(CCH):
                            ins = t.matmul(
                                ps0[:, :], wql[:, cc * 128:(cc + 1) * 128],
                                xnT[:, cc * TOK + t8 * 512:cc * TOK + (t8 + 1) * 512],
                                start=(cc == 0), stop=(cc == CCH - 1))
                        ev_mm[(gw, t8)] = ctx.b(ins, "T")
                    B.tensor(mm)

                    def ev2(v, gw=gw, t8=t8, k=k, mcol=mcol):
                        ctx.w(v, "T", ev_mm[(gw, t8)])
                        sg = stg0 if k % 2 == 0 else stg1
                        if k >= 2:
                            ctx.w(v, "dFS0" if k % 2 == 0 else "dFS1",
                                  ev_st2[k - 2])
                        v.tensor_copy(sg[:, 0:512], ps0[:, :])
                        ev_ev2[k] = ctx.b(v.reduce_max(
                            macc[:, mcol:mcol + 1],
                            ps0[:, :], AX.X,
                            apply_absolute_value=True), "V")
                    B.vector(ev2)

                    def st2(g, ic=ic, t8=t8, k=k, sp_dst=sp_dst):
                        ctx.w(g, "V", ev_ev2[k])
                        sg = stg0 if k % 2 == 0 else stg1
                        dst = AP(sp_dst.tensor,
                                 ic * 128 * TOK + t8 * 512,
                                 [[TOK, 128], [1, 512]])
                        ev_st2[k] = ctx.b16(
                            g.dma_start(out=dst, in_=sg[:, 0:512]),
                            "dFS0" if k % 2 == 0 else "dFS1")
                    B.gpsimd(st2)
                    k += 1
                    yield

        ffng = ffn_units()

        with contextlib.ExitStack() as _es4:
            v_sb = _es4.enter_context(al("v_sb", [128, 16 * C], BF16))
            atT = _es4.enter_context(al("atT", [128, 8 * T], BF16))
            qp = _es4.enter_context(al("qp", [128, T], BF16))
            kp = _es4.enter_context(al("kp", [128, T], BF16))
            Pa0 = _es4.enter_context(al("Pa0", [128, T], BF16))
            Pb0 = _es4.enter_context(al("Pb0", [128, T], BF16))
            Pa1 = _es4.enter_context(al("Pa1", [128, T], BF16))
            Pb1 = _es4.enter_context(al("Pb1", [128, T], BF16))
            PT0 = _es4.enter_context(al("PT0", [128, 1024], BF16))
            PT1 = _es4.enter_context(al("PT1", [128, 1024], BF16))
            cmx = _es4.enter_context(al("cmx", [128, 32], F32))
            dnp = _es4.enter_context(al("dnp", [128, 32], F32))
            wp_sb = _es4.enter_context(al("wp_sb", [128, CCH * 1024], BF16))
            def ld_wp(sync):
                for ccb in range(CCH):
                    ctx.b16(sync.dma_start(
                        out=wp_sb[:, ccb * 1024:(ccb + 1) * 1024],
                        in_=wpb_d[ccb * 128:(ccb + 1) * 128, :]), "dPW")
            B.sync(ld_wp)

            for sq in range(SEQ):
                def ldv2(sync, sq=sq):
                    ctx.w(sync, "T")
                    src = AP(v_sp.tensor, sq * 128 * 16 * C,
                             [[16 * C, 128], [1, 16 * C]])
                    ctx.b16(sync.dma_start(out=v_sb[:, :], in_=src), "dVB")
                B.sync(ldv2)
                ev_vb = ctx.cnt["dVB"]

                for pr in range(8):
                    def ldqk2(sync, sq=sq, pr=pr):
                        ctx.w(sync, "T")
                        base = ((sq * 8 + pr) * 2) * 128 * T
                        ctx.b16(sync.dma_start(
                            out=qp[:, :],
                            in_=AP(qk_sp.tensor, base,
                                   [[T, 128], [1, T]])), "dQK")
                        ctx.b16(sync.dma_start(
                            out=kp[:, :],
                            in_=AP(qk_sp.tensor, base + 128 * T,
                                   [[T, 128], [1, T]])), "dQK")
                    B.sync(ldqk2)
                    ev_qk2 = ctx.cnt["dQK"]

                    S0 = ctx.cnt["S"]

                    def sprom(qt, hh):
                        return S0 + 2 * qt + hh + 1 if qt >= 0 else S0

                    s_lams = []
                    ev_nrm = {}
                    ev_tp = {}
                    nrm_pend = []
                    for qt in range(17):
                      if qt < 16:
                        nsr = qt // 4 + 1
                        last_n = (qt % 4 + 1) * 128
                        for hh in range(2):
                            b0 = 64 * hh
                            Ph = (Pa0 if hh == 0 else Pb0) if qt % 2 == 0 \
                                else (Pa1 if hh == 0 else Pb1)
                            slot = (qt % 2) * 2 + hh
                            if nsr <= 2:
                                scbase = 0 if hh == 0 else 1024
                                bankfree = sprom(qt - 1, hh)
                            else:
                                scbase = 0
                                bankfree = sprom(qt, hh - 1) if hh == 1 \
                                    else sprom(qt - 1, 1)
                            qw = (qt + 1) * 128

                            def mm(t, b0=b0, qt=qt, nsr=nsr, last_n=last_n,
                                   ev_qk2=ev_qk2, scbase=scbase,
                                   bankfree=bankfree):
                                ctx.w(t, "dQK", ev_qk2)
                                ctx.w(t, "S", bankfree)
                                for sr in range(nsr):
                                    n = 512 if sr < nsr - 1 else last_n
                                    ins = t.matmul(
                                        scps[:, scbase + sr * 512:scbase + sr * 512 + n],
                                        qp[b0:b0 + 64, qt * 128:(qt + 1) * 128],
                                        kp[b0:b0 + 64, sr * 512:sr * 512 + n],
                                        start=True, stop=True)
                                ctx.b(ins, "T")
                            B.tensor(mm)

                            def msk(v, qt=qt, hh=hh, qw=qw, scbase=scbase,
                                    slot=slot):
                                ctx.w(v, "T")
                                ctx.w(v, "S", sprom(qt - 2, hh))
                                v.tensor_add(scps[:, scbase + qw - 128:scbase + qw],
                                             scps[:, scbase + qw - 128:scbase + qw],
                                             maskneg[:, :])
                                v.drain()
                                ctx.b(v.reduce_max(
                                    cmx[:, slot * 8 + 6:slot * 8 + 7],
                                    scps[:, scbase:scbase + qw],
                                    AX.X, negate=True), "V")
                            B.vector(msk)

                            vev_cap = ctx.cnt["V"]
                            tev_cap = ev_tp.get(qt - 2, 0)

                            def ex(s, qt=qt, hh=hh, qw=qw, Ph=Ph, slot=slot,
                                   scbase=scbase, vev=vev_cap, tev=tev_cap):
                                if vev > 0:
                                    s.wait_ge(ctx.sem["V"], vev)
                                if tev > 0:
                                    s.wait_ge(ctx.sem["T"], tev)
                                ins = s.activation(
                                    Ph[:, 0:qw],
                                    scps[:, scbase:scbase + qw], AF.Exp,
                                    bias=cmx[:, slot * 8 + 6:slot * 8 + 7],
                                    scale=1.0,
                                    accum_out=dnp[:, slot * 8:slot * 8 + 1])
                                got = ctx.b(ins, "S")
                                assert got == sprom(qt, hh), (got, qt, hh)
                            s_lams.append(ex)

                            nrm_pend.append((qt, Ph, hh, slot))

                      if qt >= 1:
                        while nrm_pend and nrm_pend[0][0] == qt - 1:
                            (nqt, nPh, nhh, nslot) = nrm_pend.pop(0)

                            def nrm(v, qt=nqt, Ph=nPh, hh=nhh, slot=nslot):
                                ctx.w(v, "S", sprom(qt, hh))
                                v.reciprocal(dnp[:, slot * 8 + 7:slot * 8 + 8],
                                             dnp[:, slot * 8:slot * 8 + 1])
                                v.drain()
                                ev_nrm[(qt, hh)] = ctx.b(v.tensor_scalar(
                                    Ph[:, 0:(qt + 1) * 128],
                                    Ph[:, 0:(qt + 1) * 128],
                                    dnp[:, slot * 8 + 7:slot * 8 + 8],
                                    None, ALU.mult), "V")
                            B.vector(nrm)
                        pqt = qt - 1
                        nbat = (pqt + 4) // 4
                        PhA = Pa0 if pqt % 2 == 0 else Pa1
                        PhB = Pb0 if pqt % 2 == 0 else Pb1
                        for bt in range(nbat):
                            scs = list(range(bt * 4, min(pqt + 1, bt * 4 + 4)))
                            PTb = PT0 if bt % 2 == 0 else PT1

                            def tp(t, pqt=pqt, scs=scs, PhA=PhA, PhB=PhB):
                                ctx.w(t, "V")
                                for i, sc in enumerate(scs):
                                    for hh in range(2):
                                        Ph = PhA if hh == 0 else PhB
                                        ins = t.transpose(
                                            ps6[:, (i * 2 + hh) * 128:(i * 2 + hh + 1) * 128],
                                            Ph[:, sc * 128:(sc + 1) * 128],
                                            identb[:, :])
                                ev_tp[pqt] = ctx.b(ins, "T")
                            B.tensor(tp)

                            def pev(v, PTb=PTb, scs=scs):
                                ctx.w(v, "T")
                                n = len(scs) * 256
                                ctx.b(v.tensor_copy(PTb[:, 0:n], ps6[:, 0:n]),
                                      "V")
                            B.vector(pev)

                            def av(t, pr=pr, pqt=pqt, PTb=PTb, scs=scs,
                                   ev_vb=ev_vb, S0=S0):
                                ctx.w(t, "dVB", ev_vb)
                                if pqt == 0 and pr == 0:
                                    ctx.w(t, "S", S0)
                                ctx.w(t, "V")
                                reg = (pqt % 4) * 128
                                for i, sc in enumerate(scs):
                                    for hh in range(2):
                                        h = 2 * pr + hh
                                        avb = ps7 if hh == 0 else ps1
                                        vcol = sc * 1024 + (h // 8) * 512 + (h % 8) * 64
                                        ins = t.matmul(
                                            avb[64 * hh:64 * hh + 64,
                                                reg:reg + 128],
                                            v_sb[:, vcol:vcol + 64],
                                            PTb[:, (i * 2 + hh) * 128:(i * 2 + hh) * 128 + 128],
                                            start=(sc == 0), stop=(sc == pqt),
                                            tile_position=(0, 64 * hh))
                                ev_tp[pqt] = ctx.b(ins, "T")
                            B.tensor(av)

                        if pqt % 4 == 3:
                            qr = pqt // 4

                            def aev(v, pr=pr, qr=qr):
                                ctx.w(v, "T")
                                v.tensor_copy(
                                    atT[0:64, pr * T + qr * 512:pr * T + qr * 512 + 512],
                                    ps7[0:64, :])
                                ctx.b(v.tensor_copy(
                                    atT[64:128, pr * T + qr * 512:pr * T + qr * 512 + 512],
                                    ps1[64:128, :]), "V")
                            B.vector(aev)

                      next(ffng, None)

                    def sbody(s, s_lams=s_lams):
                        for f in s_lams:
                            f(s)
                    B.scalar(sbody)
                    assert ctx.cnt["S"] == S0 + 32

                ev_pst = {}
                kk = 0
                prj_lams = []
                PS0 = ctx.cnt["S"]
                for tt in range(16):
                    for ch in range(2):
                        pb = ps0 if kk % 2 == 0 else ps1

                        def mm(t, sq=sq, tt=tt, ch=ch, pb=pb, kk=kk):
                            ctx.w(t, "dPW", 128)
                            ctx.w(t, "V")
                            if kk >= 2:
                                ctx.w(t, "S", PS0 + kk - 1)
                            t.matmul(pb[:, :], ones1[0:1, :],
                                     bprojb[0:1, ch * 512:(ch + 1) * 512],
                                     start=True, stop=False)
                            for pr in range(8):
                                ins = t.matmul(
                                    pb[:, :],
                                    atT[:, pr * T + tt * 128:pr * T + (tt + 1) * 128],
                                    wp_sb[:, pr * 1024 + ch * 512:pr * 1024 + ch * 512 + 512],
                                    start=False, stop=(pr == 7))
                            ctx.b(ins, "T")
                        B.tensor(mm)

                        def mk_pev(kk=kk, pb=pb,
                                   tcap=None, pstprev=None):
                            def f(s, tcap=ctx.cnt["T"],
                                  pstprev=ev_pst.get(
                                      kk - 2,
                                      ctx.cnt["dPS0" if kk % 2 == 0
                                              else "dPS1"])):
                                s.wait_ge(ctx.sem["T"], tcap)
                                sh = stgh0 if kk % 2 == 0 else stgh1
                                ctx.w(s, "dPS0" if kk % 2 == 0 else "dPS1",
                                      pstprev)
                                got = ctx.b(s.activation(
                                    sh[:, :], pb[:, :], AF.Copy,
                                    bias=0.0, scale=1.0), "S")
                                assert got == PS0 + kk + 1
                            return f
                        prj_lams.append(mk_pev())

                        def pst(g, sq=sq, tt=tt, ch=ch, kk=kk):
                            ctx.w(g, "S", PS0 + kk + 1)
                            sh = stgh0 if kk % 2 == 0 else stgh1
                            dst = AP(ao_sp.tensor,
                                     (sq * 16 + tt) * 128 * C + ch * 512,
                                     [[C, 128], [1, 512]])
                            ev_pst[kk] = ctx.b16(
                                g.dma_start(out=dst, in_=sh[:, :]),
                                "dPS0" if kk % 2 == 0 else "dPS1")
                        B.gpsimd(pst)
                        kk += 1

                def prj_sbody(s, prj_lams=prj_lams):
                    for f in prj_lams:
                        f(s)
                B.scalar(prj_sbody)
            ctx.barrier()

        with contextlib.ExitStack() as _es5:
            ub2 = _es5.enter_context(al("ub2", [128, C], F32))
            ub6 = _es5.enter_context(al("ub6", [128, C], F32))
            ub7 = _es5.enter_context(al("ub7", [128, C], F32))
            stgc0 = _es5.enter_context(al("stgc0", [128, 1024], BF16))
            stgc1 = _es5.enter_context(al("stgc1", [128, 1024], BF16))
            ub3 = _es5.enter_context(al("ub3", [128, C], F32))
            ub4 = _es5.enter_context(al("ub4", [128, C], F32))
            ub5 = _es5.enter_context(al("ub5", [128, C], F32))
            ldt0 = _es5.enter_context(al("ldt0", [128, 1024], BF16))
            ldt1 = _es5.enter_context(al("ldt1", [128, 1024], BF16))
            ldt2 = _es5.enter_context(al("ldt2", [128, 1024], BF16))
            ldt3 = _es5.enter_context(al("ldt3", [128, 1024], BF16))
            rob = _es5.enter_context(al("rob", [128, ICH * 512], BF16))
            rob2 = _es5.enter_context(al("rob2", [128, ICH * 512], BF16))
            def mfin(col0, ncols, slot, clear):
                def f1(v):
                    v.drain()
                    v.reduce_max(stats[:, 9:10], macc[:, col0:col0 + ncols],
                                 AX.X)
                    ctx.b(v.drain(), "V")
                B.vector(f1)

                def f2(g):
                    ctx.w(g, "V")
                    ctx.b16(g.dma_start(out=gath[:, :], in_=stats[:, 9:10]),
                            "dMG")
                B.gpsimd(f2)

                def f3(v, slot=slot, clear=clear):
                    ctx.w(v, "dMG")
                    if clear:
                        v.memset(scal[:, :], 0.0)
                        v.drain()
                    v.reduce_max(scal[0:1, slot:slot + 1], gath[0:1, :], AX.X)
                    ctx.b(v.drain(), "V")
                B.vector(f3)

            mfin(32, 192, 0, True)
            ctx.barrier()

            def c2_math(v):
                v.tensor_mul(sres[0:1, 10:11], sres[0:1, 0:1], scalg[0:1, 0:1])
                v.drain()
                v.tensor_scalar(sres[0:1, 2:3], sres[0:1, 10:11],
                                1.0 / Q15, 1e-12, ALU.mult, ALU.max)
                v.drain()
                v.reciprocal(sres[0:1, 11:12], sres[0:1, 2:3])
                v.drain()
                v.tensor_mul(sres[0:1, 3:4], sres[0:1, 0:1], sres[0:1, 11:12])

            collective(1, c2_math, [(1, 3), (2, 2)])

            ntile = ICH * 4
            ev_ld3, ev_v1, ev_v2, ev_st3 = {}, {}, {}, {}
            sig_lams = []
            S8 = ctx.cnt["S"]
            for k in range(ntile):
                next(ffng, None)
                next(ffng, None)
                ic, tc = divmod(k, 4)
                gt_ = ldt0 if k % 2 == 0 else ldt1
                uu = ub2 if k % 2 == 0 else ub3
                rg_ = ub4 if k % 2 == 0 else ub5
                ss = ub6 if k % 2 == 0 else ub7

                def ld3(sync, ic=ic, tc=tc, gt_=gt_, k=k):
                    if k >= 2:
                        ctx.w(sync, "V", ev_v2.get(k - 2, 0))
                    dsm = "dG0" if k % 2 == 0 else "dG1"
                    ctx.b16(sync.dma_start(
                        out=gt_[:, :],
                        in_=AP(g_sp.tensor, ic * 128 * TOK + tc * 1024,
                               [[TOK, 128], [1, 1024]])), dsm)
                    ev_ld3[k] = ctx.cnt[dsm]
                B.sync(ld3)

                def v1f(v, k=k, gt_=gt_, uu=uu, rg_=rg_):
                    if k == 0:
                        ctx.w(v, "G", ev_bc[0])
                    ctx.w(v, "dG0" if k % 2 == 0 else "dG1", ev_ld3[k])
                    if k >= 2:
                        ctx.w(v, "S", S8 + k - 1)
                    v.tensor_scalar(uu[:, :], gt_[:, :], bc[:, 1:2], MAGIC,
                                    ALU.mult, ALU.add)
                    v.drain()
                    ev_v1[k] = ctx.b(v.tensor_single_scalar(
                        rg_[:, :], uu[:, :], -MAGIC, ALU.add), "V")
                B.vector(v1f)

                vcap1 = ctx.cnt["V"]
                vcap2 = ev_v2.get(k - 2, 0)
                sig_lams.append((vcap1, vcap2, k, rg_, ss))

                def v2a(v, k=k, uu=uu, ss=ss):
                    ctx.w(v, "S", S8 + k + 1)
                    v.tensor_scalar(uu[:, :], ss[:, :], 65536.0, MAGIC,
                                    ALU.mult, ALU.add)
                    v.drain()
                    ctx.b(v.tensor_scalar(ss[:, :], uu[:, :], -MAGIC, 65535.0,
                                          ALU.add, ALU.min), "V")
                B.vector(v2a)

                def g2f(g, k=k, ss=ss, rg_=rg_):
                    ctx.w(g, "V")
                    if k >= 2:
                        ctx.w(g, "dTG0" if k % 2 == 0 else "dTG1",
                              ev_st3.get(k - 2, 0))
                    sg = stgc0 if k % 2 == 0 else stgc1
                    ctx.b(g.tensor_mul(sg[:, :], rg_[:, :], ss[:, :]), "G")
                B.gpsimd(g2f)

                def v2b(v, k=k):
                    ctx.w(v, "G")
                    sg = stgc0 if k % 2 == 0 else stgc1
                    ev_v2[k] = ctx.b(v.reduce_max(
                        macc[:, 32 + k:33 + k], sg[:, :], AX.X,
                        apply_absolute_value=True), "V")
                B.vector(v2b)

                def st3(g, k=k, ic=ic, tc=tc):
                    ctx.w(g, "V", ev_v2[k])
                    sg = stgc0 if k % 2 == 0 else stgc1
                    ev_st3[k] = ctx.b16(g.dma_start(
                        out=AP(pt_sp.tensor, ic * 128 * TOK + tc * 1024,
                               [[TOK, 128], [1, 1024]]),
                        in_=sg[:, :]), "dTG0" if k % 2 == 0 else "dTG1")
                B.gpsimd(st3)

            def sig_body(s):
                for (vcap1, vcap2, k, rg_, ss) in sig_lams:
                    s.wait_ge(ctx.sem["V"], vcap1)
                    if vcap2 > 0:
                        s.wait_ge(ctx.sem["V"], vcap2)
                    got = ctx.b(s.activation(ss[:, :], rg_[:, :], AF.Sigmoid,
                                             bias=0.0, scale=bc[:, 2:3]), "S")
                    assert got == S8 + k + 1
            B.scalar(sig_body)
            mfin(32, ntile, 1, True)

            for _ in ffng:
                pass
            mfin(224, 192, 0, False)
            ctx.barrier()

            def c3_math(v):
                v.tensor_mul(sres[0:1, 10:11], sres[0:1, 0:1], scalg[0:1, 0:1])
                v.drain()
                v.tensor_scalar(sres[0:1, 4:5], sres[0:1, 10:11],
                                1.0 / Q15, 1e-12, ALU.mult, ALU.max)
                v.drain()
                v.reciprocal(sres[0:1, 11:12], sres[0:1, 4:5])
                v.drain()
                v.tensor_mul(sres[0:1, 5:6], sres[0:1, 0:1], sres[0:1, 11:12])
                v.tensor_scalar(sres[0:1, 12:13], sres[0:1, 2:3],
                                1.0 / 65536.0, None, ALU.mult)
                v.drain()
                v.tensor_mul(sres[0:1, 10:11], sres[0:1, 12:13],
                             scalg[0:1, 1:2])
                v.drain()
                v.tensor_scalar(sres[0:1, 6:7], sres[0:1, 10:11],
                                1.0 / Q15, 1e-12, ALU.mult, ALU.max)
                v.drain()
                v.reciprocal(sres[0:1, 11:12], sres[0:1, 6:7])
                v.drain()
                v.tensor_mul(sres[0:1, 7:8], sres[0:1, 12:13],
                             sres[0:1, 11:12])

            collective(2, c3_math, [(3, 5), (4, 7)])

            wdq_tot = [0]

            def wdpre(sync):
                for ic in range(ICH):
                    wdq_tot[0] = ctx.b16(sync.dma_start(
                        out=xnT[:, ic * 1024:(ic + 1) * 1024],
                        in_=wdq_d[ic * 128:(ic + 1) * 128, :]), "dW50")
            B.sync(wdpre)

            ev_ld4, ev_v3, ev_v4, ev_st4 = {}, {}, {}, {}
            p9_lams = []
            S9 = ctx.cnt["S"]
            for k in range(ntile):
                ic, tc = divmod(k, 4)
                ut_ = ldt0 if k % 2 == 0 else ldt1
                ptt = ldt2 if k % 2 == 0 else ldt3
                uu = ub2 if k % 2 == 0 else ub3
                ru_ = ub4 if k % 2 == 0 else ub5
                rt_ = ub6 if k % 2 == 0 else ub7

                def ld4(sync, ic=ic, tc=tc, ut_=ut_, ptt=ptt, k=k):
                    if k >= 2:
                        ctx.w(sync, "V", ev_v4.get(k - 2, 0))
                    dsm = "dG0" if k % 2 == 0 else "dG1"
                    ctx.b16(sync.dma_start(
                        out=ut_[:, :],
                        in_=AP(u_sp.tensor, ic * 128 * TOK + tc * 1024,
                               [[TOK, 128], [1, 1024]])), dsm)
                    ctx.b16(sync.dma_start(
                        out=ptt[:, :],
                        in_=AP(pt_sp.tensor, ic * 128 * TOK + tc * 1024,
                               [[TOK, 128], [1, 1024]])), dsm)
                    ev_ld4[k] = ctx.cnt[dsm]
                B.sync(ld4)

                def mk_s3(k=k, ut_=ut_, uu=uu,
                          tok=("dG0" if k % 2 == 0 else "dG1", 0),
                          vprev=ev_v4.get(k - 2, 0)):
                    def f(s, ldv=ev_ld4[k]):
                        if k == 0:
                            ctx.w(s, "G", ev_bc[0])
                        ctx.w(s, "dG0" if k % 2 == 0 else "dG1", ldv)
                        if vprev > 0:
                            s.wait_ge(ctx.sem["V"], vprev)
                        got = ctx.b(s.activation(
                            uu[:, :], ut_[:, :], AF.Copy,
                            bias=MAGIC, scale=bc[:, 3:4]), "S")
                        assert got == S9 + 2 * k + 1
                    return f
                p9_lams.append(mk_s3())

                def v3f(v, k=k, uu=uu, ru_=ru_):
                    ctx.w(v, "S", S9 + 2 * k + 1)
                    ev_v3[k] = ctx.b(v.tensor_single_scalar(
                        ru_[:, :], uu[:, :], -MAGIC, ALU.add), "V")
                B.vector(v3f)

                def mk_s4(k=k, ptt=ptt, uu=uu, vv=None):
                    def f(s, vv=ev_v3[k]):
                        s.wait_ge(ctx.sem["V"], vv)
                        got = ctx.b(s.activation(
                            uu[:, :], ptt[:, :], AF.Copy,
                            bias=MAGIC, scale=bc[:, 4:5]), "S")
                        assert got == S9 + 2 * k + 2
                    return f
                p9_lams.append(mk_s4())

                def v4a(v, k=k, uu=uu, rt_=rt_):
                    ctx.w(v, "S", S9 + 2 * k + 2)
                    ctx.b(v.tensor_single_scalar(rt_[:, :], uu[:, :],
                                                 -MAGIC, ALU.add), "V")
                B.vector(v4a)

                def g4f(g, k=k, ru_=ru_, rt_=rt_):
                    ctx.w(g, "V")
                    if k >= 2:
                        ctx.w(g, "dTG0" if k % 2 == 0 else "dTG1",
                              ev_st4.get(k - 2, 0))
                    else:
                        ctx.w(g, "dTG0" if k % 2 == 0 else "dTG1")
                    sg = stgc0 if k % 2 == 0 else stgc1
                    ctx.b(g.tensor_mul(sg[:, :], ru_[:, :], rt_[:, :]), "G")
                B.gpsimd(g4f)

                def v4b(v, k=k):
                    ctx.w(v, "G")
                    sg = stgc0 if k % 2 == 0 else stgc1
                    ev_v4[k] = ctx.b(v.reduce_max(
                        macc[:, 32 + k:33 + k], sg[:, :], AX.X,
                        apply_absolute_value=True), "V")
                B.vector(v4b)

                def st4(g, k=k, ic=ic, tc=tc):
                    ctx.w(g, "V", ev_v4[k])
                    sg = stgc0 if k % 2 == 0 else stgc1
                    ev_st4[k] = ctx.b16(g.dma_start(
                        out=AP(pu_sp.tensor, ic * 128 * TOK + tc * 1024,
                               [[TOK, 128], [1, 1024]]),
                        in_=sg[:, :]), "dTG0" if k % 2 == 0 else "dTG1")
                B.gpsimd(st4)

            def p9_sbody(s):
                for f in p9_lams:
                    f(s)
            B.scalar(p9_sbody)
            mfin(32, ntile, 0, True)
            ctx.barrier()

            def c4_math(v):
                v.tensor_mul(sres[0:1, 12:13], sres[0:1, 6:7], sres[0:1, 4:5])
                v.drain()
                v.tensor_mul(sres[0:1, 10:11], sres[0:1, 12:13],
                             scalg[0:1, 0:1])
                v.drain()
                v.tensor_scalar(sres[0:1, 8:9], sres[0:1, 10:11],
                                1.0 / Q15, 1e-12, ALU.mult, ALU.max)
                v.drain()
                v.reciprocal(sres[0:1, 11:12], sres[0:1, 8:9])
                v.drain()
                v.tensor_mul(sres[0:1, 9:10], sres[0:1, 12:13],
                             sres[0:1, 11:12])

            collective(3, c4_math, [(5, 9), (6, 8)])

            ev_ro, ev_x10, ev_ao, ev_dn, ev_cmb, ev_ost = {}, {}, {}, {}, {}, {}
            ev_ost_last = {0: 0, 1: 0}
            p10_lams = []
            S10 = ctx.cnt["S"]
            for tr8 in range(8):
                robc = rob if tr8 % 2 == 0 else rob2
                for ic in range(ICH):
                    kr = tr8 * ICH + ic
                    ldt = ldt0 if kr % 2 == 0 else ldt1
                    uu = ub2 if kr % 2 == 0 else ub3

                    def ld5(sync, tr8=tr8, ic=ic, ldt=ldt, kr=kr):
                        if kr >= 2:
                            ctx.w(sync, "S", ev_ro.get((kr - 2, "s"), 0))
                        if tr8 >= 2:
                            ctx.w(sync, "T", ev_dn.get((tr8 - 2, 1), 0))
                        dsm = "dR0" if kr % 2 == 0 else "dR1"
                        ctx.b16(sync.dma_start(
                            out=ldt[:, 0:512],
                            in_=AP(pu_sp.tensor, ic * 128 * TOK + tr8 * 512,
                                   [[TOK, 128], [1, 512]])), dsm)
                        ev_ro[(kr, "d")] = ctx.cnt[dsm]
                    B.sync(ld5)

                    def mk_s5(kr=kr, ldt=ldt, uu=uu,
                              vprev=ev_ro.get((kr - 2, "v"), 0)):
                        def f(s, ldv=ev_ro[(kr, "d")]):
                            if kr == 0:
                                ctx.w(s, "G", ev_bc[0])
                            ctx.w(s, "dR0" if kr % 2 == 0 else "dR1", ldv)
                            if vprev > 0:
                                s.wait_ge(ctx.sem["V"], vprev)
                            got = ctx.b(s.activation(
                                uu[:, 0:512], ldt[:, 0:512], AF.Copy,
                                bias=MAGIC, scale=bc[:, 5:6]), "S")
                            assert got == S10 + kr + 1
                        return f
                    p10_lams.append(mk_s5())
                    ev_ro[(kr, "s")] = S10 + kr + 1

                    def v5f(v, kr=kr, ic=ic, uu=uu, robc=robc):
                        ctx.w(v, "S", ev_ro[(kr, "s")])
                        ev_ro[(kr, "v")] = ctx.b(v.tensor_single_scalar(
                            robc[:, ic * 512:(ic + 1) * 512], uu[:, 0:512],
                            -MAGIC, ALU.add), "V")
                    B.vector(v5f)

                ro_done = ctx.cnt["V"]

                for half in range(2):
                    banks = [ps0, ps1, ps2, ps3]

                    def merged(tr8=tr8, half=half, ro_done=ro_done,
                               banks=banks, robc=robc):
                        def mm5(t, tr8=tr8, half=half, robc=robc):
                            ctx.w(t, "dW50", wdq_tot[0])
                            ctx.w(t, "V")
                            for ic in range(ICH):
                                for tj in range(2):
                                    tt = half * 2 + tj
                                    for ch in range(2):
                                        ins = t.matmul(
                                            banks[tj * 2 + ch][:, :],
                                            robc[:, ic * 512 + tt * 128:ic * 512 + (tt + 1) * 128],
                                            xnT[:, ic * 1024 + ch * 512:ic * 1024 + (ch + 1) * 512],
                                            start=(ic == 0),
                                            stop=(ic == ICH - 1))
                            ev_dn[(tr8, half)] = ctx.b(ins, "T")
                        B.tensor(mm5)
                    merged()

                    for tj in range(2):
                        tt = half * 2 + tj
                        gtt = tr8 * 4 + tt
                        xt = xt0 if gtt % 2 == 0 else xt1

                        def ldx(sync, gtt=gtt, xt=xt, tr8=tr8, half=half,
                                tj=tj):
                            if ev_cmb:
                                ctx.w(sync, "V")
                            dsm = "dX0" if gtt % 2 == 0 else "dX1"
                            ctx.b16(sync.dma_start(
                                out=xt[:, :],
                                in_=x_d[gtt * 128:(gtt + 1) * 128, :]), dsm)
                            ev_x10[gtt] = (dsm, ctx.cnt[dsm])
                        B.sync(ldx)

                        for ch in range(2):
                            sh = stgh0 if ch == 0 else stgh1
                            ou = outs0 if ch == 0 else outs1

                            def lda2(sync, gtt=gtt, ch=ch, sh=sh):
                                ctx.w(sync, "V")
                                dsm = "dAO0" if ch == 0 else "dAO1"
                                ctx.b16(sync.dma_start(
                                    out=sh[:, :],
                                    in_=AP(ao_sp.tensor,
                                           gtt * 128 * C + ch * 512,
                                           [[C, 128], [1, 512]])), dsm)
                                ev_ao[(gtt, ch)] = (dsm, ctx.cnt[dsm])
                            B.sync(lda2)

                            def cmb2(v, tr8=tr8, half=half, tj=tj, ch=ch,
                                     gtt=gtt, xt=xt, sh=sh, ou=ou):
                                ctx.w(v, "T", ev_dn[(tr8, half)])
                                ctx.wt(v, ev_ao[(gtt, ch)])
                                ctx.wt(v, ev_x10[gtt])
                                ctx.w(v, "dOUT0" if ch == 0 else "dOUT1",
                                      ev_ost_last[ch])
                                bank = [ps0, ps1, ps2, ps3][tj * 2 + ch]
                                ev_cmb[(gtt, ch)] = ctx.b(
                                    v.scalar_tensor_tensor(
                                        ou[:, :], bank[:, :], bc[:, 6:7],
                                        sh[:, :], ALU.mult, ALU.add), "V")
                            B.vector(cmb2)

                            def gadd(g, gtt=gtt, ch=ch, ou=ou, xt=xt):
                                ctx.w(g, "V", ev_cmb[(gtt, ch)])
                                ev_cmb[(gtt, ch, "g")] = ctx.b(g.tensor_add(
                                    ou[:, :], ou[:, :],
                                    xt[:, ch * 512:(ch + 1) * 512]), "G")
                            B.gpsimd(gadd)

                            def ost(g, gtt=gtt, ch=ch, ou=ou):
                                ctx.w(g, "G", ev_cmb[(gtt, ch, "g")])
                                ev_ost[(gtt, ch)] = ctx.b16(g.dma_start(
                                    out=out_d[gtt * 128:(gtt + 1) * 128,
                                              ch * 512:(ch + 1) * 512],
                                    in_=ou[:, :]),
                                    "dOUT0" if ch == 0 else "dOUT1")
                                ev_ost_last[ch] = ev_ost[(gtt, ch)]
                            B.gpsimd(ost)

            def p10_sbody(s):
                for f in p10_lams:
                    f(s)
            B.scalar(p10_sbody)
            ctx.barrier()

    return nc


_NC_CACHE = [None]


def kernel(L0_attn_x, L0_attn_w_qkv, L0_attn_w_proj, L0_attn_b_proj,
           L0_ffn_w_gate, L0_ffn_w_up, L0_ffn_w_down):
    x = np.ascontiguousarray(np.asarray(L0_attn_x, dtype=np.float32))
    wqkv = np.ascontiguousarray(np.asarray(L0_attn_w_qkv, dtype=np.float32))
    wproj = np.ascontiguousarray(np.asarray(L0_attn_w_proj, dtype=np.float32))
    bproj = np.ascontiguousarray(np.asarray(L0_attn_b_proj, dtype=np.float32))
    wg = np.ascontiguousarray(np.asarray(L0_ffn_w_gate, dtype=np.float32))
    wu = np.ascontiguousarray(np.asarray(L0_ffn_w_up, dtype=np.float32))
    wd = np.ascontiguousarray(np.asarray(L0_ffn_w_down, dtype=np.float32))

    wqkT = np.ascontiguousarray(wqkv.transpose(0, 2, 1))
    wpT = np.ascontiguousarray(wproj.T)
    wgT = np.ascontiguousarray(wg.T)
    wuT = np.ascontiguousarray(wu.T)
    wdT = np.ascontiguousarray(wd.T)

    identb = np.eye(128, dtype=ml_dtypes.bfloat16)
    identf = np.eye(128, dtype=np.float32)
    maskneg = np.triu(np.full((128, 128), -1.0e30, dtype=np.float32), 1)
    ones1 = np.ones((1, 128), dtype=ml_dtypes.bfloat16)

    if _NC_CACHE[0] is None:
        _NC_CACHE[0] = build_nc()
    nc = _NC_CACHE[0]

    shared = dict(wqkT=wqkT, wpT=wpT, bproj=bproj.reshape(1, C),
                  wg=wg, wgT=wgT, wu=wu, wuT=wuT, wd=wd, wdT=wdT,
                  identb_in=identb, identf_in=identf,
                  maskneg_in=maskneg, ones1_in=ones1)
    in_maps = []
    for c in range(NCORE):
        m = dict(shared)
        m["x"] = np.ascontiguousarray(
            x[2 * c:2 * c + 2].reshape(TOK, C))
        in_maps.append(m)

    res = run_bass_kernel_spmd(nc, in_maps, core_ids=list(range(NCORE)))
    out = np.concatenate(
        [r["out"].reshape(2, T, C) for r in res.results], axis=0)
    return out.astype(np.float32)


# version 09
# speedup vs baseline: 1.0396x; 1.0396x over previous
import contextlib
import numpy as np
import ml_dtypes
import concourse.bass as bass
import concourse.mybir as mybir
from concourse.bass_utils import run_bass_kernel_spmd

F32 = mybir.dt.float32
BF16 = mybir.dt.bfloat16
F16 = mybir.dt.float16
AF = mybir.ActivationFunctionType
ALU = mybir.AluOpType
AX = mybir.AxisListType
AP = bass.AP

MAGIC = 12582912.0
Q15 = 32767.0

SEQ, T, C, H, HS, II = 2, 2048, 1024, 16, 64, 3072
TOK = SEQ * T
CCH = C // 128
ICH = II // 128
NTT = TOK // 128
NCORE = 8

ENGINES = ("sync", "vector", "scalar", "tensor", "gpsimd")


class Ctx:
    def __init__(self, block, sems, nc=None, es=None):
        self.block = block
        self.sem = sems
        self.cnt = {k: 0 for k in sems}
        self._nc = nc
        self._es = es

    def mksem(self, name):
        h = self._es.enter_context(self._nc.semaphore(name))
        self.sem[name] = h
        self.cnt[name] = 0
        return name

    def tok(self, s):
        return (s, self.cnt[s])

    def wt(self, eng, token):
        s, v = token
        if v > 0:
            eng.wait_ge(self.sem[s], v)

    def b(self, instr, s, amt=1):
        self.cnt[s] += amt
        instr.then_inc(self.sem[s], amt)
        return self.cnt[s]

    def b16(self, instr, s):
        return self.b(instr, s, 16)

    def w(self, eng, s, val=None):
        if val is None:
            val = self.cnt[s]
        if val > 0:
            eng.wait_ge(self.sem[s], val)

    def barrier(self):
        snap = dict(self.cnt)
        for ename in ENGINES:
            def body(e, snap=snap):
                for s, v in snap.items():
                    if v > 0:
                        e.wait_ge(self.sem[s], v)
            getattr(self.block, ename)(body)


def build_nc():
    nc = bass.Bass("TRN2", target_bir_lowering=False)

    x_d = nc.dram_tensor("x", [TOK, C], F32, kind="ExternalInput").ap()
    wqkT_d = nc.dram_tensor("wqkT", [H, C, 192], F32, kind="ExternalInput").ap()
    wpT_d = nc.dram_tensor("wpT", [C, C], F32, kind="ExternalInput").ap()
    bproj_d = nc.dram_tensor("bproj", [1, C], F32, kind="ExternalInput").ap()
    wg_d = nc.dram_tensor("wg", [II, C], F32, kind="ExternalInput").ap()
    wgT_d = nc.dram_tensor("wgT", [C, II], F32, kind="ExternalInput").ap()
    wu_d = nc.dram_tensor("wu", [II, C], F32, kind="ExternalInput").ap()
    wuT_d = nc.dram_tensor("wuT", [C, II], F32, kind="ExternalInput").ap()
    wd_d = nc.dram_tensor("wd", [C, II], F32, kind="ExternalInput").ap()
    wdT_d = nc.dram_tensor("wdT", [II, C], F32, kind="ExternalInput").ap()
    identb_d = nc.dram_tensor("identb_in", [128, 128], BF16, kind="ExternalInput").ap()
    identf_d = nc.dram_tensor("identf_in", [128, 128], F32, kind="ExternalInput").ap()
    maskneg_d = nc.dram_tensor("maskneg_in", [128, 128], F32, kind="ExternalInput").ap()
    ones1_d = nc.dram_tensor("ones1_in", [1, 128], BF16, kind="ExternalInput").ap()
    out_d = nc.dram_tensor("out", [TOK, C], F32, kind="ExternalOutput").ap()

    wqkb_d = nc.dram_tensor("wqkb", [H, C, 192], BF16).ap()
    wpb_d = nc.dram_tensor("wpb", [C, C], BF16).ap()
    srg_d = nc.dram_tensor("srg", [64, II], F32).ap()
    sru_d = nc.dram_tensor("sru", [64, II], F32).ap()
    srd_d = nc.dram_tensor("srd", [192, C], F32).ap()
    wgq_d = nc.dram_tensor("wgq", [C, II], BF16).ap()
    wuq_d = nc.dram_tensor("wuq", [C, II], BF16).ap()
    wdq_d = nc.dram_tensor("wdq", [II, C], BF16).ap()
    qk_sp = nc.dram_tensor("qk_sp", [SEQ * 8 * 2 * 128, T], BF16).ap()
    v_sp = nc.dram_tensor("v_sp", [SEQ * 128, 16 * C], BF16).ap()
    ao_sp = nc.dram_tensor("ao_sp", [TOK, C], F16).ap()
    g_sp = nc.dram_tensor("g_sp", [II, TOK], BF16).ap()
    u_sp = nc.dram_tensor("u_sp", [II, TOK], BF16).ap()
    pt_sp = nc.dram_tensor("pt_sp", [II, TOK], BF16).ap()
    pu_sp = nc.dram_tensor("pu_sp", [II, TOK], BF16).ap()
    cc_in = [nc.dram_tensor(f"cc_in{i}", [1, 16], F32).ap() for i in range(4)]
    cc_out = [nc.dram_tensor(f"cc_out{i}", [1, 16], F32).ap() for i in range(4)]
    sc_d = nc.dram_tensor("sc_d", [4, 16], F32).ap()

    al = nc.sbuf_tensor

    with contextlib.ExitStack() as _es1:
        xnT = _es1.enter_context(al("xnT", [128, CCH * TOK], BF16))
        xt0 = _es1.enter_context(al("xt0", [128, C], F32))
        xt1 = _es1.enter_context(al("xt1", [128, C], F32))
        stats = _es1.enter_context(al("stats", [128, 16], F32))
        macc = _es1.enter_context(al("macc", [128, 416], F32))
        wql0 = _es1.enter_context(al("wql0", [128, 1024], BF16))
        wql1 = _es1.enter_context(al("wql1", [128, 1024], BF16))
        gath = _es1.enter_context(al("gath", [1, 128], F32))
        scal = _es1.enter_context(al("scal", [1, 16], F32))
        scalg = _es1.enter_context(al("scalg", [1, 16], F32))
        sres = _es1.enter_context(al("sres", [1, 16], F32))
        identb = _es1.enter_context(al("identb_sb", [128, 128], BF16))
        identf = _es1.enter_context(al("identf_sb", [128, 128], F32))
        maskneg = _es1.enter_context(al("maskneg_sb", [128, 128], F32))
        ones1 = _es1.enter_context(al("ones1_sb", [1, 128], BF16))
        bproj = _es1.enter_context(al("bproj_sb", [1, C], F32))
        bprojb = _es1.enter_context(al("bprojb", [1, C], BF16))
        bc = _es1.enter_context(al("bc", [128, 8], F32))
        stg0 = _es1.enter_context(al("stg0", [128, 1024], BF16))
        stg1 = _es1.enter_context(al("stg1", [128, 1024], BF16))
        stgh0 = _es1.enter_context(al("stgh0", [128, 512], F16))
        stgh1 = _es1.enter_context(al("stgh1", [128, 512], F16))
        outs0 = _es1.enter_context(al("outs0", [128, 512], F32))
        outs1 = _es1.enter_context(al("outs1", [128, 512], F32))
        ps0 = _es1.enter_context(nc.psum_tensor("ps0", [128, 512], F32))
        ps1 = _es1.enter_context(nc.psum_tensor("ps1", [128, 512], F32))
        scps = _es1.enter_context(nc.psum_tensor("scps", [128, 2048], F32))
        ps2 = scps[:, 0:512]
        ps3 = scps[:, 512:1024]
        ps4 = scps[:, 1024:1536]
        ps5 = scps[:, 1536:2048]
        ps6 = _es1.enter_context(nc.psum_tensor("ps6", [128, 1024], BF16))
        ps7 = _es1.enter_context(nc.psum_tensor("ps7", [128, 512], F32))
        smT = _es1.enter_context(nc.semaphore("smT"))
        smV = _es1.enter_context(nc.semaphore("smV"))
        smS = _es1.enter_context(nc.semaphore("smS"))
        smG = _es1.enter_context(nc.semaphore("smG"))
        smCC = _es1.enter_context(nc.semaphore("smCC"))
        block = _es1.enter_context(nc.Block())
        ctx = Ctx(block, dict(T=smT, V=smV, S=smS, G=smG, CC=smCC),
                  nc=nc, es=_es1)
        for _snm in ("dP0", "dWL", "dWS", "dA0", "dA1", "dAS", "dB0", "dB1",
                     "dBS0", "dBS1", "dX0", "dX1", "dMG", "dCC", "dVW",
                     "dQW", "dVS0", "dVS1", "dPW", "dVB", "dQK", "dPS0",
                     "dPS1", "dFW0", "dFW1", "dFS0", "dFS1", "dG0", "dG1",
                     "dTS0", "dTS1", "dR0", "dR1", "dW50", "dW51", "dAO0",
                     "dAO1", "dOUT0", "dOUT1", "dTG0", "dTG1",
                     "dRX0", "dRX1", "dRX2", "dA2", "dB2", "dBS2"):
            ctx.mksem(_snm)
        B = block
        sbanks = [ps2, ps3, ps4, ps5]

        def p0(sync):
            ctx.b16(sync.dma_start(out=identb[:, :], in_=identb_d), "dP0")
            ctx.b16(sync.dma_start(out=identf[:, :], in_=identf_d), "dP0")
            ctx.b16(sync.dma_start(out=maskneg[:, :], in_=maskneg_d), "dP0")
            ctx.b16(sync.dma_start(out=ones1[:, :], in_=ones1_d), "dP0")
            ctx.b16(sync.dma_start(out=bproj[:, :], in_=bproj_d), "dP0")
        B.sync(p0)

        def p0b(s):
            ctx.w(s, "dP0")
            ctx.b(s.activation(bprojb[:, :], bproj[:, :], AF.Copy,
                               bias=0.0, scale=1.0), "S")
        B.scalar(p0b)
        ctx.barrier()

        ev_bc = [0]

        def collective(idx, post_math, bslots):
            def go(g):
                ctx.w(g, "V")
                ctx.b16(g.dma_start(out=cc_in[idx], in_=scal[:, :]), "dCC")
                ctx.w(g, "dCC")
                ctx.b(g.collective_compute(
                    "AllReduce", ALU.max,
                    replica_groups=[list(range(NCORE))],
                    ins=[cc_in[idx]], outs=[cc_out[idx]]), "CC")
                ctx.w(g, "CC")
                ctx.b16(g.dma_start(out=scalg[:, :], in_=cc_out[idx]), "dCC")
            B.gpsimd(go)

            def pm(v):
                ctx.w(v, "dCC")
                post_math(v)
                ctx.b(v.drain(), "V")
            B.vector(pm)

            def bounce(g, idx=idx):
                ctx.w(g, "V")
                ctx.b16(g.dma_start(out=sc_d[idx:idx + 1, :],
                                    in_=sres[:, :]), "dCC")
                ctx.w(g, "dCC")
                for (bcol, cix) in bslots:
                    src = AP(sc_d.tensor, idx * 16 + cix, [[0, 128], [1, 1]])
                    ctx.b16(g.dma_start(
                        out=bc[:, bcol:bcol + 1], in_=src), "dCC")
                ctx.w(g, "dCC")
                g.sem_inc(ctx.sem["G"], 1)
                ctx.cnt["G"] += 1
                ev_bc[0] = ctx.cnt["G"]
            B.gpsimd(bounce)

        def c1_math(v):
            v.tensor_scalar(sres[0:1, 0:1], scalg[0:1, 0:1],
                            1.0 / Q15, 1e-12, ALU.mult, ALU.max)
            v.drain()
            v.reciprocal(sres[0:1, 1:2], sres[0:1, 0:1])

        with contextlib.ExitStack() as _es2:
            wsA0 = _es2.enter_context(al("wsA0", [128, II], F32))
            wsA1 = _es2.enter_context(al("wsA1", [128, II], F32))
            wsA2 = _es2.enter_context(al("wsA2", [128, II], F32))
            wsB0 = _es2.enter_context(al("wsB0", [128, 512], F32))
            wsB1 = _es2.enter_context(al("wsB1", [128, 512], F32))
            wsB2 = _es2.enter_context(al("wsB2", [128, 512], F32))
            sxp0 = _es2.enter_context(al("sxp0", [128, 512], F32))
            sxp1 = _es2.enter_context(al("sxp1", [128, 512], F32))
            sxp2 = _es2.enter_context(al("sxp2", [128, 512], F32))
            rxp0 = _es2.enter_context(al("rxp0", [128, 512], F32))
            rxp1 = _es2.enter_context(al("rxp1", [128, 512], F32))
            rxp2 = _es2.enter_context(al("rxp2", [128, 512], F32))
            srn = _es2.enter_context(al("srn", [128, 192], F32))
            srev = _es2.enter_context(al("srev", [128, 256], F32))
            wqb0 = _es2.enter_context(al("wqb0", [128, 512], BF16))
            wqb1 = _es2.enter_context(al("wqb1", [128, 512], BF16))
            wqb2 = _es2.enter_context(al("wqb2", [128, 512], BF16))
            pqb = _es2.enter_context(al("pqb", [128, 512], F32))
            wkst = _es2.enter_context(al("wkst", [128, 1536], F32))
            wkbf = _es2.enter_context(al("wkbf", [128, 1536], BF16))
            xsq = _es2.enter_context(al("xsq", [128, C], F32))
            xnf = _es2.enter_context(al("xnf", [128, C], F32))

            sched = []
            s_base = ctx.cnt["S"]
            s_off = [0]
            asbar = {}

            def cast_gen():
                specs = [("qkv", h) for h in range(H)] + \
                        [("wp", ccb) for ccb in range(CCH)]
                evW = [0]
                evSt = [0]
                prev_prom = [None]
                for (kind, idx) in specs:
                    def ld(g, kind=kind, idx=idx, pp=prev_prom[0]):
                        if pp is not None:
                            ctx.w(g, "S", pp)
                        if kind == "qkv":
                            src = AP(wqkT_d.tensor, idx * C * 192,
                                     [[192, 128], [192 * 128, 8], [1, 192]])
                            evW[0] = ctx.b16(
                                g.dma_start(out=wkst[:, :], in_=src), "dWL")
                        else:
                            evW[0] = ctx.b16(g.dma_start(
                                out=wkst[:, 0:1024],
                                in_=wpT_d[idx * 128:(idx + 1) * 128, :]),
                                "dWL")
                    B.gpsimd(ld)
                    myW = evW[0]
                    mySt = evSt[0]
                    s_off[0] += 1
                    prom = s_base + s_off[0]

                    def cast(s, kind=kind, myW=myW, mySt=mySt, prom=prom):
                        ctx.w(s, "dWL", myW)
                        ctx.w(s, "dWS", mySt)
                        if kind == "qkv":
                            qv = AP(wkst, 0, [[1536, 128], [192, 8], [1, 64]])
                            qo = AP(wkbf, 0, [[1536, 128], [192, 8], [1, 64]])
                            s.activation(qo, qv, AF.Copy, bias=0.0,
                                         scale=0.125)
                            kv = AP(wkst, 64,
                                    [[1536, 128], [192, 8], [1, 128]])
                            ko = AP(wkbf, 64,
                                    [[1536, 128], [192, 8], [1, 128]])
                            got = ctx.b(s.activation(ko, kv, AF.Copy,
                                                     bias=0.0, scale=1.0),
                                        "S")
                        else:
                            got = ctx.b(s.activation(
                                wkbf[:, 0:1024], wkst[:, 0:1024], AF.Copy,
                                bias=0.0, scale=1.0), "S")
                        assert got == prom
                    sched.append(cast)
                    prev_prom[0] = prom

                    def st(g, kind=kind, idx=idx, prom=prom):
                        ctx.w(g, "S", prom)
                        if kind == "qkv":
                            dst = AP(wqkb_d.tensor, idx * C * 192,
                                     [[192, 128], [192 * 128, 8], [1, 192]])
                            evSt[0] = ctx.b16(
                                g.dma_start(out=dst, in_=wkbf[:, :]), "dWS")
                        else:
                            evSt[0] = ctx.b16(g.dma_start(
                                out=wpb_d[idx * 128:(idx + 1) * 128, :],
                                in_=wkbf[:, 0:1024]), "dWS")
                    B.gpsimd(st)
                    yield

            def A_gen():
                specs = []
                for (nat_d, sr_d, nrc, icols, nblk, srstride) in (
                        (wg_d, srg_d, ICH, C, 32, II),
                        (wu_d, sru_d, ICH, C, 32, II),
                        (wd_d, srd_d, CCH, II, 96, C)):
                    for rc in range(nrc):
                        specs.append((nat_d, sr_d, rc, icols, nblk,
                                      srstride))
                nA = len(specs)
                wsl = [wsA0, wsA1, wsA2]
                evV, evT, evG = {}, {}, {}

                def emit_ld(g):
                    (nat_d, sr_d, rc, icols, nblk, srstride) = specs[g]
                    ws = wsl[g % 3]

                    def ld(sync, g=g, rc=rc, ws=ws, icols=icols,
                           nat_d=nat_d):
                        if g >= 3:
                            ctx.w(sync, "V", evV[g - 3])
                        evV[(g, "ld")] = ctx.b16(sync.dma_start(
                            out=ws[:, 0:icols],
                            in_=nat_d[rc * 128:(rc + 1) * 128, :]),
                            ("dA0", "dA1", "dA2")[g % 3])
                    B.sync(ld)

                def emit_rest(g):
                    (nat_d, sr_d, rc, icols, nblk, srstride) = specs[g]
                    ws = wsl[g % 3]

                    def red(v, g=g, ws=ws, nblk=nblk):
                        ctx.w(v, ("dA0", "dA1", "dA2")[g % 3],
                              evV[(g, "ld")])
                        if g >= 1:
                            ctx.w(v, "T", evT[g - 1])
                        src = AP(ws, 0, [[II, 128], [32, nblk], [1, 32]])
                        v.reduce_max(srn[:, 0:nblk], src, AX.X,
                                     apply_absolute_value=True)
                        v.drain()
                        v.tensor_scalar(srn[:, 0:nblk], srn[:, 0:nblk],
                                        1.0 / 7.0, 1e-12, ALU.mult, ALU.max)
                        v.drain()
                        v.reciprocal(srn[:, nblk:2 * nblk], srn[:, 0:nblk])
                        evV[g] = ctx.b(v.drain(), "V")
                    B.vector(red)

                    def tr(t, g=g, nblk=nblk):
                        ctx.w(t, "V", evV[g])
                        if g >= 1:
                            ctx.w(t, "V", evV[(g - 1, "e")])
                        if nblk <= 64:
                            ins = t.transpose(ps5[0:2 * nblk, 0:128],
                                              srn[:, 0:2 * nblk],
                                              identf[:, :])
                        else:
                            t.transpose(ps5[0:nblk, 0:128], srn[:, 0:nblk],
                                        identf[:, :])
                            ins = t.transpose(ps5[0:nblk, 128:256],
                                              srn[:, nblk:2 * nblk],
                                              identf[:, :])
                        evT[g] = ctx.b(ins, "T")
                    B.tensor(tr)

                    def vev(v, g=g, nblk=nblk):
                        ctx.w(v, "T", evT[g])
                        if g >= 1:
                            ctx.w(v, "dAS", evG[g - 1])
                        if nblk <= 64:
                            ins = v.tensor_copy(srev[0:2 * nblk, 0:128],
                                                ps5[0:2 * nblk, 0:128])
                        else:
                            v.tensor_copy(srev[0:nblk, 0:128],
                                          ps5[0:nblk, 0:128])
                            ins = v.tensor_copy(srev[0:nblk, 128:256],
                                                ps5[0:nblk, 128:256])
                        evV[(g, "e")] = ctx.b(ins, "V")
                    B.vector(vev)

                    def st(gp, g=g, rc=rc, nblk=nblk, srstride=srstride,
                           sr_d=sr_d):
                        ctx.w(gp, "V", evV[(g, "e")])
                        if nblk <= 64:
                            dst = AP(sr_d.tensor, rc * 128,
                                     [[srstride, 2 * nblk], [1, 128]])
                            evG[g] = ctx.b16(
                                gp.dma_start(out=dst,
                                             in_=srev[0:2 * nblk, 0:128]),
                                "dAS")
                        else:
                            d1 = AP(sr_d.tensor, rc * 128,
                                    [[srstride, nblk], [1, 128]])
                            ctx.b16(gp.dma_start(out=d1,
                                                 in_=srev[0:nblk, 0:128]),
                                    "dAS")
                            d2 = AP(sr_d.tensor, nblk * srstride + rc * 128,
                                    [[srstride, nblk], [1, 128]])
                            evG[g] = ctx.b16(
                                gp.dma_start(out=d2,
                                             in_=srev[0:nblk, 128:256]),
                                "dAS")
                    B.gpsimd(st)

                for g in range(nA + 2):
                    if g < nA:
                        emit_ld(g)
                    if g >= 2:
                        emit_rest(g - 2)
                        if g - 2 == 23:
                            asbar[0] = ctx.cnt["dAS"]
                        elif g - 2 == 47:
                            asbar[1] = ctx.cnt["dAS"]
                        elif g - 2 == 55:
                            asbar[2] = ctx.cnt["dAS"]
                    yield

            def B_gen():
                specs = []
                for w, (wT_src, sr_d, wq_dst, nrc, ocols, rstride) in \
                        enumerate(((wgT_d, srg_d, wgq_d, CCH, II, 32),
                                   (wuT_d, sru_d, wuq_d, CCH, II, 32),
                                   (wdT_d, srd_d, wdq_d, ICH, C, 96))):
                    for rc in range(nrc):
                        for it in range(ocols // 512):
                            specs.append((w, wT_src, sr_d, wq_dst, rc, it,
                                          ocols, rstride))
                nB = len(specs)
                wsl = [wsB0, wsB1, wsB2]
                sxl = [sxp0, sxp1, sxp2]
                rxl = [rxp0, rxp1, rxp2]
                wql = [wqb0, wqb1, wqb2]
                evV, evG = {}, {}

                def emit_ld(k):
                    (w, wT_src, sr_d, wq_dst, rc, it, ocols, rstride) = \
                        specs[k]
                    ws, sx, rx_ = wsl[k % 3], sxl[k % 3], rxl[k % 3]

                    def ld(sync, w=w, rc=rc, it=it, ws=ws, sx=sx,
                           k=k, ocols=ocols, sr_d=sr_d, wT_src=wT_src):
                        ctx.w(sync, "dAS", asbar[w])
                        if k >= 3:
                            ctx.w(sync, "V", evV[k - 3])
                        dsm = ("dB0", "dB1", "dB2")[k % 3]
                        ctx.b16(sync.dma_start(
                            out=ws[:, :],
                            in_=wT_src[rc * 128:(rc + 1) * 128,
                                       it * 512:(it + 1) * 512]), dsm)
                        soff = 4 * rc * ocols + it * 512
                        evV[(k, "ld")] = ctx.b16(
                            sync.dma_start(out=sx[:, :], in_=AP(
                                sr_d.tensor, soff,
                                [[ocols, 4], [0, 32], [1, 512]])), dsm)
                    B.sync(ld)

                    def ldr(g, w=w, rc=rc, it=it, rx_=rx_, k=k,
                            ocols=ocols, rstride=rstride, sr_d=sr_d):
                        ctx.w(g, "dAS", asbar[w])
                        if k >= 3:
                            ctx.w(g, "V", evV[k - 3])
                        soff = 4 * rc * ocols + it * 512
                        evV[(k, "ldr")] = ctx.b16(
                            g.dma_start(out=rx_[:, :], in_=AP(
                                sr_d.tensor, rstride * ocols + soff,
                                [[ocols, 4], [0, 32], [1, 512]])),
                            ("dRX0", "dRX1", "dRX2")[k % 3])
                    B.gpsimd(ldr)

                def emit_rest(k):
                    (w, wT_src, sr_d, wq_dst, rc, it, ocols, rstride) = \
                        specs[k]
                    ws, sx, rx_ = wsl[k % 3], sxl[k % 3], rxl[k % 3]
                    wq = wql[k % 3]

                    def qz(v, ws=ws, sx=sx, rx_=rx_, wq=wq, k=k):
                        ctx.w(v, ("dB0", "dB1", "dB2")[k % 3],
                              evV[(k, "ld")])
                        ctx.w(v, ("dRX0", "dRX1", "dRX2")[k % 3],
                              evV[(k, "ldr")])
                        if k >= 3:
                            ctx.w(v, ("dBS0", "dBS1", "dBS2")[(k - 3) % 3],
                                  evG[k - 3])
                        v.drain()
                        v.tensor_mul(pqb[:, :], ws[:, :], rx_[:, :])
                        v.drain()
                        v.tensor_single_scalar(pqb[:, :], pqb[:, :],
                                               MAGIC, ALU.add)
                        v.drain()
                        evV[k] = ctx.b(v.scalar_tensor_tensor(
                            wq[:, :], pqb[:, :], -MAGIC, sx[:, :],
                            ALU.add, ALU.mult), "V")
                    B.vector(qz)

                    def st(g, rc=rc, it=it, wq=wq, k=k, wq_dst=wq_dst):
                        ctx.w(g, "V", evV[k])
                        evG[k] = ctx.b16(g.dma_start(
                            out=wq_dst[rc * 128:(rc + 1) * 128,
                                       it * 512:(it + 1) * 512],
                            in_=wq[:, :]),
                            ("dBS0", "dBS1", "dBS2")[k % 3])
                    B.gpsimd(st)

                for k in range(nB + 2):
                    if k < nB:
                        emit_ld(k)
                    if k >= 2:
                        emit_rest(k - 2)
                    yield

            cg = cast_gen()
            ag = A_gen()
            bg = B_gen()

            ev_xn, ev_tp, ev_ev = {}, {}, {}
            for tt in range(NTT):
                xt = xt0 if tt % 2 == 0 else xt1

                def ld(sync, tt=tt, xt=xt):
                    if tt >= 2:
                        ctx.w(sync, "V", ev_xn[tt - 2])
                    ev_xn[(tt, "ld")] = ctx.b16(sync.dma_start(
                        out=xt[:, :], in_=x_d[tt * 128:(tt + 1) * 128, :]),
                        "dX0" if tt % 2 == 0 else "dX1")
                B.sync(ld)

                def st1(v, tt=tt, xt=xt):
                    ctx.w(v, "dX0" if tt % 2 == 0 else "dX1",
                          ev_xn[(tt, "ld")])
                    v.tensor_mul(xsq[:, :], xt[:, :], xt[:, :])
                    v.drain()
                    v.reduce_sum(stats[:, 0:1], xt[:, :], AX.X)
                    v.reduce_sum(stats[:, 1:2], xsq[:, :], AX.X)
                    v.drain()
                    v.tensor_scalar(stats[:, 2:3], stats[:, 0:1],
                                    1.0 / C, None, ALU.mult)
                    v.drain()
                    v.tensor_mul(stats[:, 3:4], stats[:, 2:3], stats[:, 2:3])
                    v.drain()
                    v.scalar_tensor_tensor(stats[:, 4:5], stats[:, 1:2],
                                           1.0 / C, stats[:, 3:4],
                                           ALU.mult, ALU.subtract)
                    v.drain()
                    v.tensor_single_scalar(stats[:, 4:5], stats[:, 4:5],
                                           1e-5, ALU.add)
                    v.drain()
                    v.reciprocal(stats[:, 5:6], stats[:, 4:5])
                    ctx.b(v.drain(), "V")
                B.vector(st1)

                vcap = ctx.cnt["V"]
                s_off[0] += 1
                prom = s_base + s_off[0]

                def sq(s, vcap=vcap, prom=prom):
                    s.wait_ge(ctx.sem["V"], vcap)
                    got = ctx.b(s.activation(stats[:, 6:7], stats[:, 5:6],
                                             AF.Sqrt, bias=0.0, scale=1.0),
                                "S")
                    assert got == prom
                sched.append(sq)

                def st3(v, prom=prom):
                    ctx.w(v, "S", prom)
                    v.tensor_scalar(stats[:, 7:8], stats[:, 2:3], -1.0, None,
                                    ALU.mult)
                    ctx.b(v.drain(), "V")
                B.vector(st3)

                def xn_act(v, tt=tt, xt=xt):
                    if tt >= 1:
                        ctx.w(v, "T", ev_tp[tt - 1])
                    ev_xn[tt] = ctx.b(v.tensor_scalar(
                        xnf[:, :], xt[:, :], stats[:, 7:8], stats[:, 6:7],
                        ALU.add, ALU.mult), "V")
                B.vector(xn_act)

                def m1p(v, tt=tt):
                    v.drain()
                    ctx.b(v.reduce_max(macc[:, tt:tt + 1], xnf[:, :], AX.X,
                                       apply_absolute_value=True), "V")
                B.vector(m1p)

                next(ag, None)

                def tp(t, tt=tt):
                    ctx.w(t, "V", ev_xn[tt])
                    if tt >= 1:
                        ctx.w(t, "V", ev_ev[tt - 1])
                    for bk in range(2):
                        pb = ps3 if bk == 0 else ps4
                        for kx in range(4):
                            ci = 4 * bk + kx
                            ins = t.transpose(pb[:, kx * 128:(kx + 1) * 128],
                                              xnf[:, ci * 128:(ci + 1) * 128],
                                              identf[:, :])
                    ev_tp[tt] = ctx.b(ins, "T")
                B.tensor(tp)

                def ev(v, tt=tt):
                    ctx.w(v, "T", ev_tp[tt])
                    for bk in range(2):
                        dst = AP(xnT, (4 * bk) * TOK + tt * 128,
                                 [[CCH * TOK, 128], [TOK, 4], [1, 128]])
                        ins = v.tensor_copy(dst, AP(
                            scps, 512 + bk * 512,
                            [[2048, 128], [128, 4], [1, 128]]))
                    ev_ev[tt] = ctx.b(ins, "V")
                B.vector(ev)

                next(ag, None)
                next(cg, None)

            for _ in cg:
                pass
            for _ in ag:
                pass

            def ln_sbody(s):
                for fn in sched:
                    fn(s)
            B.scalar(ln_sbody)

            def m1fin(v):
                v.drain()
                v.reduce_max(stats[:, 8:9], macc[:, 0:NTT], AX.X)
                ctx.b(v.drain(), "V")
            B.vector(m1fin)

            def m1g(g):
                ctx.w(g, "V")
                ctx.b16(g.dma_start(out=gath[:, :], in_=stats[:, 8:9]), "dMG")
            B.gpsimd(m1g)

            def m1f2(v):
                ctx.w(v, "dMG")
                v.memset(scal[:, :], 0.0)
                v.memset(sres[:, :], 0.0)
                v.drain()
                v.reduce_max(scal[0:1, 0:1], gath[0:1, :], AX.X)
                ctx.b(v.drain(), "V")
            B.vector(m1f2)
            ctx.barrier()

            collective(0, c1_math, [(0, 1)])

            with contextlib.ExitStack() as _es3:
                wvw = _es3.enter_context(al("wvw", [128, 2 * CCH * 512], BF16))
                wqkp = _es3.enter_context(al("wqkp", [128, CCH * 256], BF16))
                qkev = _es3.enter_context(al("qkev", [128, 512], BF16))
                qkev2 = _es3.enter_context(al("qkev2", [128, 512], BF16))

                def ldv(sync):
                    for cc in range(CCH):
                        for vg in range(2):
                            src = AP(wqkb_d.tensor,
                                     (8 * vg) * C * 192 + (cc * 128) * 192 + 128,
                                     [[192, 128], [C * 192, 8], [1, 64]])
                            ctx.b16(sync.dma_start(
                                out=wvw[:, (cc * 2 + vg) * 512:(cc * 2 + vg + 1) * 512],
                                in_=src), "dVW")
                B.sync(ldv)

                ev_vev, ev_vst = {}, {}
                k = 0
                for sq in range(SEQ):
                    for vg in range(2):
                        for tt in range(16):
                            pb = ps0 if k % 2 == 0 else ps1

                            def mm(t, sq=sq, vg=vg, tt=tt, pb=pb, k=k):
                                ctx.w(t, "dVW", 256)
                                if k >= 2:
                                    ctx.w(t, "V", ev_vev[k - 2])
                                gt = sq * 16 + tt
                                for cc in range(CCH):
                                    ins = t.matmul(
                                        pb[:, :],
                                        xnT[:, cc * TOK + gt * 128:cc * TOK + (gt + 1) * 128],
                                        wvw[:, (cc * 2 + vg) * 512:(cc * 2 + vg + 1) * 512],
                                        start=(cc == 0), stop=(cc == CCH - 1))
                                ctx.b(ins, "T")
                            B.tensor(mm)

                            def ev(v, pb=pb, k=k):
                                ctx.w(v, "T")
                                qv = qkev if k % 2 == 0 else qkev2
                                if k >= 2:
                                    ctx.w(v, "dVS0" if k % 2 == 0 else "dVS1",
                                          ev_vst[k - 2])
                                ev_vev[k] = ctx.b(v.tensor_copy(qv[:, :], pb[:, :]),
                                                  "V")
                            B.vector(ev)

                            def st(g, sq=sq, vg=vg, tt=tt, k=k):
                                ctx.w(g, "V", ev_vev[k])
                                qv = qkev if k % 2 == 0 else qkev2
                                dst = AP(v_sp.tensor,
                                         sq * 128 * (16 * C) + tt * 1024 + vg * 512,
                                         [[16 * C, 128], [1, 512]])
                                ev_vst[k] = ctx.b16(
                                    g.dma_start(out=dst, in_=qv[:, :]),
                                    "dVS0" if k % 2 == 0 else "dVS1")
                            B.gpsimd(st)
                            next(bg, None)
                            k += 1

                ev_qkev, ev_qkst = {}, {}
                kk = 0
                for sq in range(SEQ):
                    for pr in range(8):
                        def ldqk(sync, sq=sq, pr=pr):
                            ctx.w(sync, "T")
                            for cc in range(CCH):
                                src = AP(wqkb_d.tensor,
                                         (2 * pr) * C * 192 + (cc * 128) * 192,
                                         [[192, 128], [C * 192, 2], [1, 128]])
                                ctx.b16(sync.dma_start(
                                    out=wqkp[:, cc * 256:(cc + 1) * 256],
                                    in_=src), "dQW")
                            ev_qkev[(sq, pr, "ld")] = ctx.cnt["dQW"]
                        B.sync(ldqk)
                        next(bg, None)

                        for tr in range(4):
                            def mm(t, sq=sq, pr=pr, tr=tr, kk=kk):
                                ctx.w(t, "dQW", ev_qkev[(sq, pr, "ld")])
                                if kk >= 1:
                                    ctx.w(t, "V", ev_qkev[kk - 1])
                                else:
                                    ctx.w(t, "V")
                                gt = sq * 4 + tr
                                for cc in range(CCH):
                                    rhs = xnT[:, cc * TOK + gt * 512:cc * TOK + (gt + 1) * 512]
                                    s_ = (cc == 0)
                                    p_ = (cc == CCH - 1)
                                    t.matmul(ps0[0:64, :],
                                             wqkp[:, cc * 256:cc * 256 + 64],
                                             rhs, start=s_, stop=p_,
                                             tile_position=(0, 0))
                                    t.matmul(ps2[64:128, :],
                                             wqkp[:, cc * 256 + 128:cc * 256 + 192],
                                             rhs, start=s_, stop=p_,
                                             tile_position=(0, 64))
                                    t.matmul(ps1[0:64, :],
                                             wqkp[:, cc * 256 + 64:cc * 256 + 128],
                                             rhs, start=s_, stop=p_,
                                             tile_position=(0, 0))
                                    ins = t.matmul(ps3[64:128, :],
                                                   wqkp[:, cc * 256 + 192:cc * 256 + 256],
                                                   rhs, start=s_, stop=p_,
                                                   tile_position=(0, 64))
                                ctx.b(ins, "T")
                            B.tensor(mm)

                            def ev(v, kk=kk):
                                ctx.w(v, "T")
                                if kk >= 1:
                                    ctx.w(v, "dVS1" if kk % 2 == 0 else "dVS0",
                                          ev_qkst[kk - 1])
                                else:
                                    ctx.w(v, "dVS0")
                                    ctx.w(v, "dVS1")
                                v.tensor_copy(qkev[0:64, :], ps0[0:64, :])
                                v.tensor_copy(qkev[64:128, :], ps2[64:128, :])
                                v.tensor_copy(qkev2[0:64, :], ps1[0:64, :])
                                ev_qkev[kk] = ctx.b(
                                    v.tensor_copy(qkev2[64:128, :],
                                                  ps3[64:128, :]), "V")
                            B.vector(ev)

                            def st(g, sq=sq, pr=pr, tr=tr, kk=kk):
                                ctx.w(g, "V", ev_qkev[kk])
                                base = ((sq * 8 + pr) * 2) * 128 * T
                                dq = AP(qk_sp.tensor, base + tr * 512,
                                        [[T, 128], [1, 512]])
                                dk = AP(qk_sp.tensor, base + 128 * T + tr * 512,
                                        [[T, 128], [1, 512]])
                                dsm = "dVS0" if kk % 2 == 0 else "dVS1"
                                ctx.b16(g.dma_start(out=dq, in_=qkev[:, :]), dsm)
                                ev_qkst[kk] = ctx.b16(
                                    g.dma_start(out=dk, in_=qkev2[:, :]), dsm)
                            B.gpsimd(st)
                            next(bg, None)
                            kk += 1

                for _ in bg:
                    pass
                ctx.barrier()

        _esP4b = contextlib.ExitStack()
        ub0 = _esP4b.enter_context(al("ub0", [128, C], F32))
        ub1 = _esP4b.enter_context(al("ub1", [128, C], F32))
        ev_i2 = {}
        rx_lams = []
        RX0 = ctx.cnt["S"]
        for j in range(NTT):
            ub = ub0 if j % 2 == 0 else ub1

            def mk_i1(j=j, ub=ub, vprev=ev_i2.get(j - 2, 0)):
                def f(s):
                    if j == 0:
                        ctx.w(s, "G", ev_bc[0])
                    if vprev > 0:
                        s.wait_ge(ctx.sem["V"], vprev)
                    got = ctx.b(s.activation(
                        ub[:, :], xnT[:, j * 1024:(j + 1) * 1024],
                        AF.Copy, bias=MAGIC, scale=bc[:, 0:1]), "S")
                    assert got == RX0 + j + 1
                return f
            rx_lams.append(mk_i1())

            def i2(v, j=j, ub=ub):
                ctx.w(v, "S", RX0 + j + 1)
                ev_i2[j] = ctx.b(v.tensor_single_scalar(
                    xnT[:, j * 1024:(j + 1) * 1024], ub[:, :],
                    -MAGIC, ALU.add), "V")
            B.vector(i2)

        def rx_sbody(s):
            for f in rx_lams:
                f(s)
        B.scalar(rx_sbody)
        ctx.barrier()
        _esP4b.close()

        ffn_state = dict(ke=0, km=0)
        ffn_ev_mm, ffn_ev_ev2, ffn_ev_st2 = {}, {}, {}
        nU = 16 * ICH

        def ffn_pump_ev():
            k = ffn_state["ke"]
            if k >= ffn_state["km"] or k >= nU:
                return
            ffn_state["ke"] += 1
            gw, t8 = divmod(k, 8)
            sp_dst = g_sp if gw < ICH else u_sp
            ic = gw % ICH
            mcol = 32 + k if k < 8 * ICH else 224 + (k - 8 * ICH)

            def ev2(v, gw=gw, t8=t8, k=k, mcol=mcol):
                ctx.w(v, "T", ffn_ev_mm[(gw, t8)])
                sg = stg0 if k % 2 == 0 else stg1
                if k >= 2:
                    ctx.w(v, "dFS0" if k % 2 == 0 else "dFS1",
                          ffn_ev_st2[k - 2])
                v.tensor_copy(sg[:, 0:512], ps0[:, :])
                ffn_ev_ev2[k] = ctx.b(v.reduce_max(
                    macc[:, mcol:mcol + 1],
                    ps0[:, :], AX.X,
                    apply_absolute_value=True), "V")
            B.vector(ev2)

            def st2(g, ic=ic, t8=t8, k=k, sp_dst=sp_dst):
                ctx.w(g, "V", ffn_ev_ev2[k])
                sg = stg0 if k % 2 == 0 else stg1
                dst = AP(sp_dst.tensor,
                         ic * 128 * TOK + t8 * 512,
                         [[TOK, 128], [1, 512]])
                ffn_ev_st2[k] = ctx.b16(
                    g.dma_start(out=dst, in_=sg[:, 0:512]),
                    "dFS0" if k % 2 == 0 else "dFS1")
            B.gpsimd(st2)

        def ffn_pump_mm():
            k = ffn_state["km"]
            if k >= nU or ffn_state["ke"] < k:
                return
            ffn_state["km"] += 1
            gw, t8 = divmod(k, 8)
            wq_src = wgq_d if gw < ICH else wuq_d
            ic = gw % ICH
            wql = wql0 if gw % 2 == 0 else wql1

            if t8 == 0:
                def ldw(sync, gw=gw, ic=ic, wql=wql, wq_src=wq_src):
                    if gw >= 2:
                        ctx.w(sync, "T", ffn_ev_mm[(gw - 2, 7)])
                    dsm = "dFW0" if gw % 2 == 0 else "dFW1"
                    for cc in range(CCH):
                        ctx.b16(sync.dma_start(
                            out=wql[:, cc * 128:(cc + 1) * 128],
                            in_=wq_src[cc * 128:(cc + 1) * 128,
                                       ic * 128:(ic + 1) * 128]), dsm)
                    ffn_ev_mm[(gw, "ld")] = ctx.cnt[dsm]
                B.sync(ldw)

            def mm(t, gw=gw, t8=t8, k=k, wql=wql):
                ctx.w(t, "dFW0" if gw % 2 == 0 else "dFW1",
                      ffn_ev_mm[(gw, "ld")])
                if k >= 1:
                    ctx.w(t, "V", ffn_ev_ev2[k - 1])
                for cc in range(CCH):
                    ins = t.matmul(
                        ps0[:, :], wql[:, cc * 128:(cc + 1) * 128],
                        xnT[:, cc * TOK + t8 * 512:cc * TOK + (t8 + 1) * 512],
                        start=(cc == 0), stop=(cc == CCH - 1))
                ffn_ev_mm[(gw, t8)] = ctx.b(ins, "T")
            B.tensor(mm)


        with contextlib.ExitStack() as _es4:
            v_sb = _es4.enter_context(al("v_sb", [128, 16 * C], BF16))
            atT = _es4.enter_context(al("atT", [128, 8 * T], BF16))
            qp = _es4.enter_context(al("qp", [128, T], BF16))
            kp = _es4.enter_context(al("kp", [128, T], BF16))
            Pa0 = _es4.enter_context(al("Pa0", [128, T], BF16))
            Pb0 = _es4.enter_context(al("Pb0", [128, T], BF16))
            Pa1 = _es4.enter_context(al("Pa1", [128, T], BF16))
            Pb1 = _es4.enter_context(al("Pb1", [128, T], BF16))
            PT0 = _es4.enter_context(al("PT0", [128, 1024], BF16))
            PT1 = _es4.enter_context(al("PT1", [128, 1024], BF16))
            cmx = _es4.enter_context(al("cmx", [128, 32], F32))
            dnp = _es4.enter_context(al("dnp", [128, 32], F32))
            wp_sb = _es4.enter_context(al("wp_sb", [128, CCH * 1024], BF16))
            def ld_wp(sync):
                for ccb in range(CCH):
                    ctx.b16(sync.dma_start(
                        out=wp_sb[:, ccb * 1024:(ccb + 1) * 1024],
                        in_=wpb_d[ccb * 128:(ccb + 1) * 128, :]), "dPW")
            B.sync(ld_wp)

            for sq in range(SEQ):
                def ldv2(sync, sq=sq):
                    ctx.w(sync, "T")
                    src = AP(v_sp.tensor, sq * 128 * 16 * C,
                             [[16 * C, 128], [1, 16 * C]])
                    ctx.b16(sync.dma_start(out=v_sb[:, :], in_=src), "dVB")
                B.sync(ldv2)
                ev_vb = ctx.cnt["dVB"]

                for pr in range(8):
                    def ldqk2(sync, sq=sq, pr=pr):
                        ctx.w(sync, "T")
                        base = ((sq * 8 + pr) * 2) * 128 * T
                        ctx.b16(sync.dma_start(
                            out=qp[:, :],
                            in_=AP(qk_sp.tensor, base,
                                   [[T, 128], [1, T]])), "dQK")
                        ctx.b16(sync.dma_start(
                            out=kp[:, :],
                            in_=AP(qk_sp.tensor, base + 128 * T,
                                   [[T, 128], [1, T]])), "dQK")
                    B.sync(ldqk2)
                    ev_qk2 = ctx.cnt["dQK"]

                    S0 = ctx.cnt["S"]

                    def sprom(qt, hh):
                        return S0 + 2 * qt + hh + 1 if qt >= 0 else S0

                    s_lams = []
                    ev_nrm = {}
                    ev_tp = {}
                    nrm_pend = []
                    for qt in range(17):
                      ffn_pump_ev()
                      if qt < 16:
                        nsr = qt // 4 + 1
                        last_n = (qt % 4 + 1) * 128
                        for hh in range(2):
                            b0 = 64 * hh
                            Ph = (Pa0 if hh == 0 else Pb0) if qt % 2 == 0 \
                                else (Pa1 if hh == 0 else Pb1)
                            slot = (qt % 2) * 2 + hh
                            if nsr <= 2:
                                scbase = 0 if hh == 0 else 1024
                                bankfree = sprom(qt - 1, hh)
                            else:
                                scbase = 0
                                bankfree = sprom(qt, hh - 1) if hh == 1 \
                                    else sprom(qt - 1, 1)
                            qw = (qt + 1) * 128

                            def mm(t, b0=b0, qt=qt, nsr=nsr, last_n=last_n,
                                   ev_qk2=ev_qk2, scbase=scbase,
                                   bankfree=bankfree):
                                ctx.w(t, "dQK", ev_qk2)
                                ctx.w(t, "S", bankfree)
                                for sr in range(nsr):
                                    n = 512 if sr < nsr - 1 else last_n
                                    ins = t.matmul(
                                        scps[:, scbase + sr * 512:scbase + sr * 512 + n],
                                        qp[b0:b0 + 64, qt * 128:(qt + 1) * 128],
                                        kp[b0:b0 + 64, sr * 512:sr * 512 + n],
                                        start=True, stop=True)
                                ctx.b(ins, "T")
                            B.tensor(mm)

                            def msk(v, qt=qt, hh=hh, qw=qw, scbase=scbase,
                                    slot=slot):
                                ctx.w(v, "T")
                                ctx.w(v, "S", sprom(qt - 2, hh))
                                v.tensor_add(scps[:, scbase + qw - 128:scbase + qw],
                                             scps[:, scbase + qw - 128:scbase + qw],
                                             maskneg[:, :])
                                v.drain()
                                ctx.b(v.reduce_max(
                                    cmx[:, slot * 8 + 6:slot * 8 + 7],
                                    scps[:, scbase:scbase + qw],
                                    AX.X, negate=True), "V")
                            B.vector(msk)

                            vev_cap = ctx.cnt["V"]
                            tev_cap = ev_tp.get(qt - 2, 0)

                            def ex(s, qt=qt, hh=hh, qw=qw, Ph=Ph, slot=slot,
                                   scbase=scbase, vev=vev_cap, tev=tev_cap):
                                if vev > 0:
                                    s.wait_ge(ctx.sem["V"], vev)
                                if tev > 0:
                                    s.wait_ge(ctx.sem["T"], tev)
                                ins = s.activation(
                                    Ph[:, 0:qw],
                                    scps[:, scbase:scbase + qw], AF.Exp,
                                    bias=cmx[:, slot * 8 + 6:slot * 8 + 7],
                                    scale=1.0,
                                    accum_out=dnp[:, slot * 8:slot * 8 + 1])
                                got = ctx.b(ins, "S")
                                assert got == sprom(qt, hh), (got, qt, hh)
                            s_lams.append(ex)

                            nrm_pend.append((qt, Ph, hh, slot))

                      if qt >= 1:
                        while nrm_pend and nrm_pend[0][0] == qt - 1:
                            (nqt, nPh, nhh, nslot) = nrm_pend.pop(0)

                            def nrm(v, qt=nqt, Ph=nPh, hh=nhh, slot=nslot):
                                ctx.w(v, "S", sprom(qt, hh))
                                v.reciprocal(dnp[:, slot * 8 + 7:slot * 8 + 8],
                                             dnp[:, slot * 8:slot * 8 + 1])
                                v.drain()
                                ev_nrm[(qt, hh)] = ctx.b(v.tensor_scalar(
                                    Ph[:, 0:(qt + 1) * 128],
                                    Ph[:, 0:(qt + 1) * 128],
                                    dnp[:, slot * 8 + 7:slot * 8 + 8],
                                    None, ALU.mult), "V")
                            B.vector(nrm)
                        pqt = qt - 1
                        nbat = (pqt + 4) // 4
                        PhA = Pa0 if pqt % 2 == 0 else Pa1
                        PhB = Pb0 if pqt % 2 == 0 else Pb1
                        for bt in range(nbat):
                            scs = list(range(bt * 4, min(pqt + 1, bt * 4 + 4)))
                            PTb = PT0 if bt % 2 == 0 else PT1

                            def tp(t, pqt=pqt, scs=scs, PhA=PhA, PhB=PhB):
                                ctx.w(t, "V")
                                for i, sc in enumerate(scs):
                                    for hh in range(2):
                                        Ph = PhA if hh == 0 else PhB
                                        ins = t.transpose(
                                            ps6[:, (i * 2 + hh) * 128:(i * 2 + hh + 1) * 128],
                                            Ph[:, sc * 128:(sc + 1) * 128],
                                            identb[:, :])
                                ev_tp[pqt] = ctx.b(ins, "T")
                            B.tensor(tp)

                            def pev(v, PTb=PTb, scs=scs):
                                ctx.w(v, "T")
                                n = len(scs) * 256
                                ctx.b(v.tensor_copy(PTb[:, 0:n], ps6[:, 0:n]),
                                      "V")
                            B.vector(pev)

                            def av(t, pr=pr, pqt=pqt, PTb=PTb, scs=scs,
                                   ev_vb=ev_vb, S0=S0):
                                ctx.w(t, "dVB", ev_vb)
                                if pqt == 0 and pr == 0:
                                    ctx.w(t, "S", S0)
                                ctx.w(t, "V")
                                reg = (pqt % 4) * 128
                                for i, sc in enumerate(scs):
                                    for hh in range(2):
                                        h = 2 * pr + hh
                                        avb = ps7 if hh == 0 else ps1
                                        vcol = sc * 1024 + (h // 8) * 512 + (h % 8) * 64
                                        ins = t.matmul(
                                            avb[64 * hh:64 * hh + 64,
                                                reg:reg + 128],
                                            v_sb[:, vcol:vcol + 64],
                                            PTb[:, (i * 2 + hh) * 128:(i * 2 + hh) * 128 + 128],
                                            start=(sc == 0), stop=(sc == pqt),
                                            tile_position=(0, 64 * hh))
                                ev_tp[pqt] = ctx.b(ins, "T")
                            B.tensor(av)

                        if pqt % 4 == 3:
                            qr = pqt // 4

                            def aev(v, pr=pr, qr=qr):
                                ctx.w(v, "T")
                                v.tensor_copy(
                                    atT[0:64, pr * T + qr * 512:pr * T + qr * 512 + 512],
                                    ps7[0:64, :])
                                ctx.b(v.tensor_copy(
                                    atT[64:128, pr * T + qr * 512:pr * T + qr * 512 + 512],
                                    ps1[64:128, :]), "V")
                            B.vector(aev)

                      ffn_pump_mm()

                    def sbody(s, s_lams=s_lams):
                        for f in s_lams:
                            f(s)
                    B.scalar(sbody)
                    assert ctx.cnt["S"] == S0 + 32

                ffn_pump_ev()

                ev_pst = {}
                kk = 0
                prj_lams = []
                PS0 = ctx.cnt["S"]
                for tt in range(16):
                    for ch in range(2):
                        pb = ps0 if kk % 2 == 0 else ps1

                        def mm(t, sq=sq, tt=tt, ch=ch, pb=pb, kk=kk):
                            ctx.w(t, "dPW", 128)
                            ctx.w(t, "V")
                            if kk >= 2:
                                ctx.w(t, "S", PS0 + kk - 1)
                            t.matmul(pb[:, :], ones1[0:1, :],
                                     bprojb[0:1, ch * 512:(ch + 1) * 512],
                                     start=True, stop=False)
                            for pr in range(8):
                                ins = t.matmul(
                                    pb[:, :],
                                    atT[:, pr * T + tt * 128:pr * T + (tt + 1) * 128],
                                    wp_sb[:, pr * 1024 + ch * 512:pr * 1024 + ch * 512 + 512],
                                    start=False, stop=(pr == 7))
                            ctx.b(ins, "T")
                        B.tensor(mm)

                        def mk_pev(kk=kk, pb=pb,
                                   tcap=None, pstprev=None):
                            def f(s, tcap=ctx.cnt["T"],
                                  pstprev=ev_pst.get(
                                      kk - 2,
                                      ctx.cnt["dPS0" if kk % 2 == 0
                                              else "dPS1"])):
                                s.wait_ge(ctx.sem["T"], tcap)
                                sh = stgh0 if kk % 2 == 0 else stgh1
                                ctx.w(s, "dPS0" if kk % 2 == 0 else "dPS1",
                                      pstprev)
                                got = ctx.b(s.activation(
                                    sh[:, :], pb[:, :], AF.Copy,
                                    bias=0.0, scale=1.0), "S")
                                assert got == PS0 + kk + 1
                            return f
                        prj_lams.append(mk_pev())

                        def pst(g, sq=sq, tt=tt, ch=ch, kk=kk):
                            ctx.w(g, "S", PS0 + kk + 1)
                            sh = stgh0 if kk % 2 == 0 else stgh1
                            dst = AP(ao_sp.tensor,
                                     (sq * 16 + tt) * 128 * C + ch * 512,
                                     [[C, 128], [1, 512]])
                            ev_pst[kk] = ctx.b16(
                                g.dma_start(out=dst, in_=sh[:, :]),
                                "dPS0" if kk % 2 == 0 else "dPS1")
                        B.gpsimd(pst)
                        kk += 1

                def prj_sbody(s, prj_lams=prj_lams):
                    for f in prj_lams:
                        f(s)
                B.scalar(prj_sbody)
            ctx.barrier()

        with contextlib.ExitStack() as _es5:
            ub2 = _es5.enter_context(al("ub2", [128, C], F32))
            ub6 = _es5.enter_context(al("ub6", [128, C], F32))
            ub7 = _es5.enter_context(al("ub7", [128, C], F32))
            stgc0 = _es5.enter_context(al("stgc0", [128, 1024], BF16))
            stgc1 = _es5.enter_context(al("stgc1", [128, 1024], BF16))
            ub3 = _es5.enter_context(al("ub3", [128, C], F32))
            ub4 = _es5.enter_context(al("ub4", [128, C], F32))
            ub5 = _es5.enter_context(al("ub5", [128, C], F32))
            ldt0 = _es5.enter_context(al("ldt0", [128, 1024], BF16))
            ldt1 = _es5.enter_context(al("ldt1", [128, 1024], BF16))
            ldt2 = _es5.enter_context(al("ldt2", [128, 1024], BF16))
            ldt3 = _es5.enter_context(al("ldt3", [128, 1024], BF16))
            rob = _es5.enter_context(al("rob", [128, ICH * 512], BF16))
            rob2 = _es5.enter_context(al("rob2", [128, ICH * 512], BF16))
            def mfin(col0, ncols, slot, clear):
                def f1(v):
                    v.drain()
                    v.reduce_max(stats[:, 9:10], macc[:, col0:col0 + ncols],
                                 AX.X)
                    ctx.b(v.drain(), "V")
                B.vector(f1)

                def f2(g):
                    ctx.w(g, "V")
                    ctx.b16(g.dma_start(out=gath[:, :], in_=stats[:, 9:10]),
                            "dMG")
                B.gpsimd(f2)

                def f3(v, slot=slot, clear=clear):
                    ctx.w(v, "dMG")
                    if clear:
                        v.memset(scal[:, :], 0.0)
                        v.drain()
                    v.reduce_max(scal[0:1, slot:slot + 1], gath[0:1, :], AX.X)
                    ctx.b(v.drain(), "V")
                B.vector(f3)

            ffn_pump_ev()
            mfin(32, 192, 0, True)
            ctx.barrier()

            def c2_math(v):
                v.tensor_mul(sres[0:1, 10:11], sres[0:1, 0:1], scalg[0:1, 0:1])
                v.drain()
                v.tensor_scalar(sres[0:1, 2:3], sres[0:1, 10:11],
                                1.0 / Q15, 1e-12, ALU.mult, ALU.max)
                v.drain()
                v.reciprocal(sres[0:1, 11:12], sres[0:1, 2:3])
                v.drain()
                v.tensor_mul(sres[0:1, 3:4], sres[0:1, 0:1], sres[0:1, 11:12])

            collective(1, c2_math, [(1, 3), (2, 2)])

            ntile = ICH * 4
            ev_ld3, ev_v1, ev_v2, ev_st3 = {}, {}, {}, {}
            sig_lams = []
            S8 = ctx.cnt["S"]
            for k in range(ntile):
                ffn_pump_ev()
                ffn_pump_mm()
                ic, tc = divmod(k, 4)
                gt_ = ldt0 if k % 2 == 0 else ldt1
                uu = ub2 if k % 2 == 0 else ub3
                rg_ = ub4 if k % 2 == 0 else ub5
                ss = ub6 if k % 2 == 0 else ub7

                def ld3(sync, ic=ic, tc=tc, gt_=gt_, k=k):
                    if k >= 2:
                        ctx.w(sync, "V", ev_v2.get(k - 2, 0))
                    dsm = "dG0" if k % 2 == 0 else "dG1"
                    ctx.b16(sync.dma_start(
                        out=gt_[:, :],
                        in_=AP(g_sp.tensor, ic * 128 * TOK + tc * 1024,
                               [[TOK, 128], [1, 1024]])), dsm)
                    ev_ld3[k] = ctx.cnt[dsm]
                B.sync(ld3)

                def v1f(v, k=k, gt_=gt_, uu=uu, rg_=rg_):
                    if k == 0:
                        ctx.w(v, "G", ev_bc[0])
                    ctx.w(v, "dG0" if k % 2 == 0 else "dG1", ev_ld3[k])
                    if k >= 2:
                        ctx.w(v, "S", S8 + k - 1)
                    v.tensor_scalar(uu[:, :], gt_[:, :], bc[:, 1:2], MAGIC,
                                    ALU.mult, ALU.add)
                    v.drain()
                    ev_v1[k] = ctx.b(v.tensor_single_scalar(
                        rg_[:, :], uu[:, :], -MAGIC, ALU.add), "V")
                B.vector(v1f)

                vcap1 = ctx.cnt["V"]
                vcap2 = ev_v2.get(k - 2, 0)
                sig_lams.append((vcap1, vcap2, k, rg_, ss))

                def v2a(v, k=k, uu=uu, ss=ss):
                    ctx.w(v, "S", S8 + k + 1)
                    v.tensor_scalar(uu[:, :], ss[:, :], 65536.0, MAGIC,
                                    ALU.mult, ALU.add)
                    v.drain()
                    ctx.b(v.tensor_scalar(ss[:, :], uu[:, :], -MAGIC, 65535.0,
                                          ALU.add, ALU.min), "V")
                B.vector(v2a)

                def g2f(g, k=k, ss=ss, rg_=rg_):
                    ctx.w(g, "V")
                    if k >= 2:
                        ctx.w(g, "dTG0" if k % 2 == 0 else "dTG1",
                              ev_st3.get(k - 2, 0))
                    sg = stgc0 if k % 2 == 0 else stgc1
                    ctx.b(g.tensor_mul(sg[:, :], rg_[:, :], ss[:, :]), "G")
                B.gpsimd(g2f)

                def v2b(v, k=k):
                    ctx.w(v, "G")
                    sg = stgc0 if k % 2 == 0 else stgc1
                    ev_v2[k] = ctx.b(v.reduce_max(
                        macc[:, 32 + k:33 + k], sg[:, :], AX.X,
                        apply_absolute_value=True), "V")
                B.vector(v2b)

                def st3(g, k=k, ic=ic, tc=tc):
                    ctx.w(g, "V", ev_v2[k])
                    sg = stgc0 if k % 2 == 0 else stgc1
                    ev_st3[k] = ctx.b16(g.dma_start(
                        out=AP(pt_sp.tensor, ic * 128 * TOK + tc * 1024,
                               [[TOK, 128], [1, 1024]]),
                        in_=sg[:, :]), "dTG0" if k % 2 == 0 else "dTG1")
                B.gpsimd(st3)

            def sig_body(s):
                for (vcap1, vcap2, k, rg_, ss) in sig_lams:
                    s.wait_ge(ctx.sem["V"], vcap1)
                    if vcap2 > 0:
                        s.wait_ge(ctx.sem["V"], vcap2)
                    got = ctx.b(s.activation(ss[:, :], rg_[:, :], AF.Sigmoid,
                                             bias=0.0, scale=bc[:, 2:3]), "S")
                    assert got == S8 + k + 1
            B.scalar(sig_body)
            mfin(32, ntile, 1, True)

            while ffn_state["km"] < nU or ffn_state["ke"] < nU:
                ffn_pump_ev()
                ffn_pump_mm()
            mfin(224, 192, 0, False)
            ctx.barrier()

            def c3_math(v):
                v.tensor_mul(sres[0:1, 10:11], sres[0:1, 0:1], scalg[0:1, 0:1])
                v.drain()
                v.tensor_scalar(sres[0:1, 4:5], sres[0:1, 10:11],
                                1.0 / Q15, 1e-12, ALU.mult, ALU.max)
                v.drain()
                v.reciprocal(sres[0:1, 11:12], sres[0:1, 4:5])
                v.drain()
                v.tensor_mul(sres[0:1, 5:6], sres[0:1, 0:1], sres[0:1, 11:12])
                v.tensor_scalar(sres[0:1, 12:13], sres[0:1, 2:3],
                                1.0 / 65536.0, None, ALU.mult)
                v.drain()
                v.tensor_mul(sres[0:1, 10:11], sres[0:1, 12:13],
                             scalg[0:1, 1:2])
                v.drain()
                v.tensor_scalar(sres[0:1, 6:7], sres[0:1, 10:11],
                                1.0 / Q15, 1e-12, ALU.mult, ALU.max)
                v.drain()
                v.reciprocal(sres[0:1, 11:12], sres[0:1, 6:7])
                v.drain()
                v.tensor_mul(sres[0:1, 7:8], sres[0:1, 12:13],
                             sres[0:1, 11:12])

            collective(2, c3_math, [(3, 5), (4, 7)])

            wdq_tot = [0]

            def wdpre(sync):
                for ic in range(ICH):
                    wdq_tot[0] = ctx.b16(sync.dma_start(
                        out=xnT[:, ic * 1024:(ic + 1) * 1024],
                        in_=wdq_d[ic * 128:(ic + 1) * 128, :]), "dW50")
            B.sync(wdpre)

            ev_ld4, ev_v3, ev_v4, ev_st4 = {}, {}, {}, {}
            p9_lams = []
            S9 = ctx.cnt["S"]
            for k in range(ntile):
                ic, tc = divmod(k, 4)
                ut_ = ldt0 if k % 2 == 0 else ldt1
                ptt = ldt2 if k % 2 == 0 else ldt3
                uu = ub2 if k % 2 == 0 else ub3
                ru_ = ub4 if k % 2 == 0 else ub5
                rt_ = ub6 if k % 2 == 0 else ub7

                def ld4(sync, ic=ic, tc=tc, ut_=ut_, ptt=ptt, k=k):
                    if k >= 2:
                        ctx.w(sync, "V", ev_v4.get(k - 2, 0))
                    dsm = "dG0" if k % 2 == 0 else "dG1"
                    ctx.b16(sync.dma_start(
                        out=ut_[:, :],
                        in_=AP(u_sp.tensor, ic * 128 * TOK + tc * 1024,
                               [[TOK, 128], [1, 1024]])), dsm)
                    ctx.b16(sync.dma_start(
                        out=ptt[:, :],
                        in_=AP(pt_sp.tensor, ic * 128 * TOK + tc * 1024,
                               [[TOK, 128], [1, 1024]])), dsm)
                    ev_ld4[k] = ctx.cnt[dsm]
                B.sync(ld4)

                def mk_s3(k=k, ut_=ut_, uu=uu,
                          tok=("dG0" if k % 2 == 0 else "dG1", 0),
                          vprev=ev_v4.get(k - 2, 0)):
                    def f(s, ldv=ev_ld4[k]):
                        if k == 0:
                            ctx.w(s, "G", ev_bc[0])
                        ctx.w(s, "dG0" if k % 2 == 0 else "dG1", ldv)
                        if vprev > 0:
                            s.wait_ge(ctx.sem["V"], vprev)
                        got = ctx.b(s.activation(
                            uu[:, :], ut_[:, :], AF.Copy,
                            bias=MAGIC, scale=bc[:, 3:4]), "S")
                        assert got == S9 + 2 * k + 1
                    return f
                p9_lams.append(mk_s3())

                def v3f(v, k=k, uu=uu, ru_=ru_):
                    ctx.w(v, "S", S9 + 2 * k + 1)
                    ev_v3[k] = ctx.b(v.tensor_single_scalar(
                        ru_[:, :], uu[:, :], -MAGIC, ALU.add), "V")
                B.vector(v3f)

                def mk_s4(k=k, ptt=ptt, uu=uu, vv=None):
                    def f(s, vv=ev_v3[k]):
                        s.wait_ge(ctx.sem["V"], vv)
                        got = ctx.b(s.activation(
                            uu[:, :], ptt[:, :], AF.Copy,
                            bias=MAGIC, scale=bc[:, 4:5]), "S")
                        assert got == S9 + 2 * k + 2
                    return f
                p9_lams.append(mk_s4())

                def v4a(v, k=k, uu=uu, rt_=rt_):
                    ctx.w(v, "S", S9 + 2 * k + 2)
                    ctx.b(v.tensor_single_scalar(rt_[:, :], uu[:, :],
                                                 -MAGIC, ALU.add), "V")
                B.vector(v4a)

                def g4f(g, k=k, ru_=ru_, rt_=rt_):
                    ctx.w(g, "V")
                    if k >= 2:
                        ctx.w(g, "dTG0" if k % 2 == 0 else "dTG1",
                              ev_st4.get(k - 2, 0))
                    else:
                        ctx.w(g, "dTG0" if k % 2 == 0 else "dTG1")
                    sg = stgc0 if k % 2 == 0 else stgc1
                    ctx.b(g.tensor_mul(sg[:, :], ru_[:, :], rt_[:, :]), "G")
                B.gpsimd(g4f)

                def v4b(v, k=k):
                    ctx.w(v, "G")
                    sg = stgc0 if k % 2 == 0 else stgc1
                    ev_v4[k] = ctx.b(v.reduce_max(
                        macc[:, 32 + k:33 + k], sg[:, :], AX.X,
                        apply_absolute_value=True), "V")
                B.vector(v4b)

                def st4(g, k=k, ic=ic, tc=tc):
                    ctx.w(g, "V", ev_v4[k])
                    sg = stgc0 if k % 2 == 0 else stgc1
                    ev_st4[k] = ctx.b16(g.dma_start(
                        out=AP(pu_sp.tensor, ic * 128 * TOK + tc * 1024,
                               [[TOK, 128], [1, 1024]]),
                        in_=sg[:, :]), "dTG0" if k % 2 == 0 else "dTG1")
                B.gpsimd(st4)

            def p9_sbody(s):
                for f in p9_lams:
                    f(s)
            B.scalar(p9_sbody)
            mfin(32, ntile, 0, True)
            ctx.barrier()

            def c4_math(v):
                v.tensor_mul(sres[0:1, 12:13], sres[0:1, 6:7], sres[0:1, 4:5])
                v.drain()
                v.tensor_mul(sres[0:1, 10:11], sres[0:1, 12:13],
                             scalg[0:1, 0:1])
                v.drain()
                v.tensor_scalar(sres[0:1, 8:9], sres[0:1, 10:11],
                                1.0 / Q15, 1e-12, ALU.mult, ALU.max)
                v.drain()
                v.reciprocal(sres[0:1, 11:12], sres[0:1, 8:9])
                v.drain()
                v.tensor_mul(sres[0:1, 9:10], sres[0:1, 12:13],
                             sres[0:1, 11:12])

            collective(3, c4_math, [(5, 9), (6, 8)])

            ev_ro, ev_x10, ev_ao, ev_dn, ev_cmb, ev_ost = {}, {}, {}, {}, {}, {}
            ev_ost_last = {0: 0, 1: 0}
            p10_lams = []
            S10 = ctx.cnt["S"]
            for tr8 in range(8):
                robc = rob if tr8 % 2 == 0 else rob2
                for ic in range(ICH):
                    kr = tr8 * ICH + ic
                    ldt = ldt0 if kr % 2 == 0 else ldt1
                    uu = ub2 if kr % 2 == 0 else ub3

                    def ld5(sync, tr8=tr8, ic=ic, ldt=ldt, kr=kr):
                        if kr >= 2:
                            ctx.w(sync, "S", ev_ro.get((kr - 2, "s"), 0))
                        if tr8 >= 2:
                            ctx.w(sync, "T", ev_dn.get((tr8 - 2, 1), 0))
                        dsm = "dR0" if kr % 2 == 0 else "dR1"
                        ctx.b16(sync.dma_start(
                            out=ldt[:, 0:512],
                            in_=AP(pu_sp.tensor, ic * 128 * TOK + tr8 * 512,
                                   [[TOK, 128], [1, 512]])), dsm)
                        ev_ro[(kr, "d")] = ctx.cnt[dsm]
                    B.sync(ld5)

                    def mk_s5(kr=kr, ldt=ldt, uu=uu,
                              vprev=ev_ro.get((kr - 2, "v"), 0)):
                        def f(s, ldv=ev_ro[(kr, "d")]):
                            if kr == 0:
                                ctx.w(s, "G", ev_bc[0])
                            ctx.w(s, "dR0" if kr % 2 == 0 else "dR1", ldv)
                            if vprev > 0:
                                s.wait_ge(ctx.sem["V"], vprev)
                            got = ctx.b(s.activation(
                                uu[:, 0:512], ldt[:, 0:512], AF.Copy,
                                bias=MAGIC, scale=bc[:, 5:6]), "S")
                            assert got == S10 + kr + 1
                        return f
                    p10_lams.append(mk_s5())
                    ev_ro[(kr, "s")] = S10 + kr + 1

                    def v5f(v, kr=kr, ic=ic, uu=uu, robc=robc):
                        ctx.w(v, "S", ev_ro[(kr, "s")])
                        ev_ro[(kr, "v")] = ctx.b(v.tensor_single_scalar(
                            robc[:, ic * 512:(ic + 1) * 512], uu[:, 0:512],
                            -MAGIC, ALU.add), "V")
                    B.vector(v5f)

                ro_done = ctx.cnt["V"]

                for half in range(2):
                    banks = [ps0, ps1, ps2, ps3]

                    def merged(tr8=tr8, half=half, ro_done=ro_done,
                               banks=banks, robc=robc):
                        def mm5(t, tr8=tr8, half=half, robc=robc):
                            ctx.w(t, "dW50", wdq_tot[0])
                            ctx.w(t, "V")
                            for ic in range(ICH):
                                for tj in range(2):
                                    tt = half * 2 + tj
                                    for ch in range(2):
                                        ins = t.matmul(
                                            banks[tj * 2 + ch][:, :],
                                            robc[:, ic * 512 + tt * 128:ic * 512 + (tt + 1) * 128],
                                            xnT[:, ic * 1024 + ch * 512:ic * 1024 + (ch + 1) * 512],
                                            start=(ic == 0),
                                            stop=(ic == ICH - 1))
                            ev_dn[(tr8, half)] = ctx.b(ins, "T")
                        B.tensor(mm5)
                    merged()

                    for tj in range(2):
                        tt = half * 2 + tj
                        gtt = tr8 * 4 + tt
                        xt = xt0 if gtt % 2 == 0 else xt1

                        def ldx(sync, gtt=gtt, xt=xt, tr8=tr8, half=half,
                                tj=tj):
                            if ev_cmb:
                                ctx.w(sync, "V")
                            dsm = "dX0" if gtt % 2 == 0 else "dX1"
                            ctx.b16(sync.dma_start(
                                out=xt[:, :],
                                in_=x_d[gtt * 128:(gtt + 1) * 128, :]), dsm)
                            ev_x10[gtt] = (dsm, ctx.cnt[dsm])
                        B.sync(ldx)

                        for ch in range(2):
                            sh = stgh0 if ch == 0 else stgh1
                            ou = outs0 if ch == 0 else outs1

                            def lda2(sync, gtt=gtt, ch=ch, sh=sh):
                                ctx.w(sync, "V")
                                dsm = "dAO0" if ch == 0 else "dAO1"
                                ctx.b16(sync.dma_start(
                                    out=sh[:, :],
                                    in_=AP(ao_sp.tensor,
                                           gtt * 128 * C + ch * 512,
                                           [[C, 128], [1, 512]])), dsm)
                                ev_ao[(gtt, ch)] = (dsm, ctx.cnt[dsm])
                            B.sync(lda2)

                            def cmb2(v, tr8=tr8, half=half, tj=tj, ch=ch,
                                     gtt=gtt, xt=xt, sh=sh, ou=ou):
                                ctx.w(v, "T", ev_dn[(tr8, half)])
                                ctx.wt(v, ev_ao[(gtt, ch)])
                                ctx.wt(v, ev_x10[gtt])
                                ctx.w(v, "dOUT0" if ch == 0 else "dOUT1",
                                      ev_ost_last[ch])
                                bank = [ps0, ps1, ps2, ps3][tj * 2 + ch]
                                ev_cmb[(gtt, ch)] = ctx.b(
                                    v.scalar_tensor_tensor(
                                        ou[:, :], bank[:, :], bc[:, 6:7],
                                        sh[:, :], ALU.mult, ALU.add), "V")
                            B.vector(cmb2)

                            def gadd(g, gtt=gtt, ch=ch, ou=ou, xt=xt):
                                ctx.w(g, "V", ev_cmb[(gtt, ch)])
                                ev_cmb[(gtt, ch, "g")] = ctx.b(g.tensor_add(
                                    ou[:, :], ou[:, :],
                                    xt[:, ch * 512:(ch + 1) * 512]), "G")
                            B.gpsimd(gadd)

                            def ost(g, gtt=gtt, ch=ch, ou=ou):
                                ctx.w(g, "G", ev_cmb[(gtt, ch, "g")])
                                ev_ost[(gtt, ch)] = ctx.b16(g.dma_start(
                                    out=out_d[gtt * 128:(gtt + 1) * 128,
                                              ch * 512:(ch + 1) * 512],
                                    in_=ou[:, :]),
                                    "dOUT0" if ch == 0 else "dOUT1")
                                ev_ost_last[ch] = ev_ost[(gtt, ch)]
                            B.gpsimd(ost)

            def p10_sbody(s):
                for f in p10_lams:
                    f(s)
            B.scalar(p10_sbody)
            ctx.barrier()

    return nc


_NC_CACHE = [None]


def kernel(L0_attn_x, L0_attn_w_qkv, L0_attn_w_proj, L0_attn_b_proj,
           L0_ffn_w_gate, L0_ffn_w_up, L0_ffn_w_down):
    x = np.ascontiguousarray(np.asarray(L0_attn_x, dtype=np.float32))
    wqkv = np.ascontiguousarray(np.asarray(L0_attn_w_qkv, dtype=np.float32))
    wproj = np.ascontiguousarray(np.asarray(L0_attn_w_proj, dtype=np.float32))
    bproj = np.ascontiguousarray(np.asarray(L0_attn_b_proj, dtype=np.float32))
    wg = np.ascontiguousarray(np.asarray(L0_ffn_w_gate, dtype=np.float32))
    wu = np.ascontiguousarray(np.asarray(L0_ffn_w_up, dtype=np.float32))
    wd = np.ascontiguousarray(np.asarray(L0_ffn_w_down, dtype=np.float32))

    wqkT = np.ascontiguousarray(wqkv.transpose(0, 2, 1))
    wpT = np.ascontiguousarray(wproj.T)
    wgT = np.ascontiguousarray(wg.T)
    wuT = np.ascontiguousarray(wu.T)
    wdT = np.ascontiguousarray(wd.T)

    identb = np.eye(128, dtype=ml_dtypes.bfloat16)
    identf = np.eye(128, dtype=np.float32)
    maskneg = np.triu(np.full((128, 128), -1.0e30, dtype=np.float32), 1)
    ones1 = np.ones((1, 128), dtype=ml_dtypes.bfloat16)

    if _NC_CACHE[0] is None:
        _NC_CACHE[0] = build_nc()
    nc = _NC_CACHE[0]

    shared = dict(wqkT=wqkT, wpT=wpT, bproj=bproj.reshape(1, C),
                  wg=wg, wgT=wgT, wu=wu, wuT=wuT, wd=wd, wdT=wdT,
                  identb_in=identb, identf_in=identf,
                  maskneg_in=maskneg, ones1_in=ones1)
    in_maps = []
    for c in range(NCORE):
        m = dict(shared)
        m["x"] = np.ascontiguousarray(
            x[2 * c:2 * c + 2].reshape(TOK, C))
        in_maps.append(m)

    res = run_bass_kernel_spmd(nc, in_maps, core_ids=list(range(NCORE)))
    out = np.concatenate(
        [r["out"].reshape(2, T, C) for r in res.results], axis=0)
    return out.astype(np.float32)


# version 10
# speedup vs baseline: 1.1405x; 1.0971x over previous
import contextlib
import numpy as np
import ml_dtypes
import concourse.bass as bass
import concourse.mybir as mybir
from concourse.bass_utils import run_bass_kernel_spmd

F32 = mybir.dt.float32
BF16 = mybir.dt.bfloat16
F16 = mybir.dt.float16
AF = mybir.ActivationFunctionType
ALU = mybir.AluOpType
AX = mybir.AxisListType
AP = bass.AP

MAGIC = 12582912.0
Q15 = 32767.0

SEQ, T, C, H, HS, II = 2, 2048, 1024, 16, 64, 3072
TOK = SEQ * T
CCH = C // 128
ICH = II // 128
NTT = TOK // 128
NCORE = 8

ENGINES = ("sync", "vector", "scalar", "tensor", "gpsimd")


class Ctx:
    def __init__(self, block, sems, nc=None, es=None):
        self.block = block
        self.sem = sems
        self.cnt = {k: 0 for k in sems}
        self._nc = nc
        self._es = es

    def mksem(self, name):
        h = self._es.enter_context(self._nc.semaphore(name))
        self.sem[name] = h
        self.cnt[name] = 0
        return name

    def tok(self, s):
        return (s, self.cnt[s])

    def wt(self, eng, token):
        s, v = token
        if v > 0:
            eng.wait_ge(self.sem[s], v)

    def b(self, instr, s, amt=1):
        self.cnt[s] += amt
        instr.then_inc(self.sem[s], amt)
        return self.cnt[s]

    def b16(self, instr, s):
        return self.b(instr, s, 16)

    def w(self, eng, s, val=None):
        if val is None:
            val = self.cnt[s]
        if val > 0:
            eng.wait_ge(self.sem[s], val)

    def barrier(self):
        snap = dict(self.cnt)
        for ename in ENGINES:
            def body(e, snap=snap):
                for s, v in snap.items():
                    if v > 0:
                        e.wait_ge(self.sem[s], v)
            getattr(self.block, ename)(body)


def build_nc():
    nc = bass.Bass("TRN2", target_bir_lowering=False)

    x_d = nc.dram_tensor("x", [TOK, C], F32, kind="ExternalInput").ap()
    wqkT_d = nc.dram_tensor("wqkT", [H, C, 192], F32, kind="ExternalInput").ap()
    wpT_d = nc.dram_tensor("wpT", [C, C], F32, kind="ExternalInput").ap()
    bproj_d = nc.dram_tensor("bproj", [1, C], F32, kind="ExternalInput").ap()
    wg_d = nc.dram_tensor("wg", [II, C], F32, kind="ExternalInput").ap()
    wgT_d = nc.dram_tensor("wgT", [C, II], F32, kind="ExternalInput").ap()
    wu_d = nc.dram_tensor("wu", [II, C], F32, kind="ExternalInput").ap()
    wuT_d = nc.dram_tensor("wuT", [C, II], F32, kind="ExternalInput").ap()
    wd_d = nc.dram_tensor("wd", [C, II], F32, kind="ExternalInput").ap()
    wdT_d = nc.dram_tensor("wdT", [II, C], F32, kind="ExternalInput").ap()
    identb_d = nc.dram_tensor("identb_in", [128, 128], BF16, kind="ExternalInput").ap()
    identf_d = nc.dram_tensor("identf_in", [128, 128], F32, kind="ExternalInput").ap()
    maskneg_d = nc.dram_tensor("maskneg_in", [128, 128], F32, kind="ExternalInput").ap()
    ones1_d = nc.dram_tensor("ones1_in", [1, 128], BF16, kind="ExternalInput").ap()
    out_d = nc.dram_tensor("out", [TOK, C], F32, kind="ExternalOutput").ap()

    wqkb_d = nc.dram_tensor("wqkb", [H, C, 192], BF16).ap()
    wpb_d = nc.dram_tensor("wpb", [C, C], BF16).ap()
    srg_d = nc.dram_tensor("srg", [64, II], F32).ap()
    sru_d = nc.dram_tensor("sru", [64, II], F32).ap()
    srd_d = nc.dram_tensor("srd", [192, C], F32).ap()
    wgq_d = nc.dram_tensor("wgq", [C, II], BF16).ap()
    wuq_d = nc.dram_tensor("wuq", [C, II], BF16).ap()
    wdq_d = nc.dram_tensor("wdq", [II, C], BF16).ap()
    qk_sp = nc.dram_tensor("qk_sp", [SEQ * 8 * 2 * 128, T], BF16).ap()
    v_sp = nc.dram_tensor("v_sp", [SEQ * 128, 16 * C], BF16).ap()
    ao_sp = nc.dram_tensor("ao_sp", [TOK, C], F16).ap()
    g_sp = nc.dram_tensor("g_sp", [II, TOK], BF16).ap()
    u_sp = nc.dram_tensor("u_sp", [II, TOK], BF16).ap()
    pt_sp = nc.dram_tensor("pt_sp", [II, TOK], BF16).ap()
    pu_sp = nc.dram_tensor("pu_sp", [II, TOK], BF16).ap()
    cc_in = [nc.dram_tensor(f"cc_in{i}", [1, 16], F32).ap() for i in range(4)]
    cc_out = [nc.dram_tensor(f"cc_out{i}", [1, 16], F32).ap() for i in range(4)]
    sc_d = nc.dram_tensor("sc_d", [4, 16], F32).ap()

    al = nc.sbuf_tensor

    with contextlib.ExitStack() as _es1:
        xnT = _es1.enter_context(al("xnT", [128, CCH * TOK], BF16))
        xt0 = _es1.enter_context(al("xt0", [128, C], F32))
        xt1 = _es1.enter_context(al("xt1", [128, C], F32))
        stats = _es1.enter_context(al("stats", [128, 16], F32))
        macc = _es1.enter_context(al("macc", [128, 416], F32))
        wql0 = _es1.enter_context(al("wql0", [128, 1024], BF16))
        wql1 = _es1.enter_context(al("wql1", [128, 1024], BF16))
        gath = _es1.enter_context(al("gath", [1, 128], F32))
        scal = _es1.enter_context(al("scal", [1, 16], F32))
        scalg = _es1.enter_context(al("scalg", [1, 16], F32))
        sres = _es1.enter_context(al("sres", [1, 16], F32))
        identb = _es1.enter_context(al("identb_sb", [128, 128], BF16))
        identf = _es1.enter_context(al("identf_sb", [128, 128], F32))
        maskneg = _es1.enter_context(al("maskneg_sb", [128, 128], F32))
        ones1 = _es1.enter_context(al("ones1_sb", [1, 128], BF16))
        bproj = _es1.enter_context(al("bproj_sb", [1, C], F32))
        bprojb = _es1.enter_context(al("bprojb", [1, C], BF16))
        bc = _es1.enter_context(al("bc", [128, 8], F32))
        stg0 = _es1.enter_context(al("stg0", [128, 1024], BF16))
        stg1 = _es1.enter_context(al("stg1", [128, 1024], BF16))
        stgh0 = _es1.enter_context(al("stgh0", [128, 512], F16))
        stgh1 = _es1.enter_context(al("stgh1", [128, 512], F16))
        outs0 = _es1.enter_context(al("outs0", [128, 512], F32))
        outs1 = _es1.enter_context(al("outs1", [128, 512], F32))
        ps0 = _es1.enter_context(nc.psum_tensor("ps0", [128, 512], F32))
        ps1 = _es1.enter_context(nc.psum_tensor("ps1", [128, 512], F32))
        scps = _es1.enter_context(nc.psum_tensor("scps", [128, 2048], F32))
        ps2 = scps[:, 0:512]
        ps3 = scps[:, 512:1024]
        ps4 = scps[:, 1024:1536]
        ps5 = scps[:, 1536:2048]
        ps6 = _es1.enter_context(nc.psum_tensor("ps6", [128, 1024], BF16))
        ps7 = _es1.enter_context(nc.psum_tensor("ps7", [128, 512], F32))
        smT = _es1.enter_context(nc.semaphore("smT"))
        smV = _es1.enter_context(nc.semaphore("smV"))
        smS = _es1.enter_context(nc.semaphore("smS"))
        smG = _es1.enter_context(nc.semaphore("smG"))
        smCC = _es1.enter_context(nc.semaphore("smCC"))
        block = _es1.enter_context(nc.Block())
        ctx = Ctx(block, dict(T=smT, V=smV, S=smS, G=smG, CC=smCC),
                  nc=nc, es=_es1)
        for _snm in ("dP0", "dWL", "dWS", "dA0", "dA1", "dAS", "dB0", "dB1",
                     "dBS0", "dBS1", "dX0", "dX1", "dMG", "dCC", "dVW",
                     "dQW", "dVS0", "dVS1", "dPW", "dVB", "dQK", "dPS0",
                     "dPS1", "dFW0", "dFW1", "dFS0", "dFS1", "dG0", "dG1",
                     "dTS0", "dTS1", "dR0", "dR1", "dW50", "dW51", "dAO0",
                     "dAO1", "dOUT0", "dOUT1", "dTG0", "dTG1",
                     "dRX0", "dRX1", "dRX2", "dA2", "dB2", "dBS2"):
            ctx.mksem(_snm)
        B = block
        sbanks = [ps2, ps3, ps4, ps5]

        def p0(sync):
            ctx.b16(sync.dma_start(out=identb[:, :], in_=identb_d), "dP0")
            ctx.b16(sync.dma_start(out=identf[:, :], in_=identf_d), "dP0")
            ctx.b16(sync.dma_start(out=maskneg[:, :], in_=maskneg_d), "dP0")
            ctx.b16(sync.dma_start(out=ones1[:, :], in_=ones1_d), "dP0")
            ctx.b16(sync.dma_start(out=bproj[:, :], in_=bproj_d), "dP0")
        B.sync(p0)

        def p0b(s):
            ctx.w(s, "dP0")
            ctx.b(s.activation(bprojb[:, :], bproj[:, :], AF.Copy,
                               bias=0.0, scale=1.0), "S")
        B.scalar(p0b)
        ctx.barrier()

        ev_bc = [0]

        def collective(idx, post_math, bslots):
            def go(g):
                ctx.w(g, "V")
                ctx.b16(g.dma_start(out=cc_in[idx], in_=scal[:, :]), "dCC")
                ctx.w(g, "dCC")
                ctx.b(g.collective_compute(
                    "AllReduce", ALU.max,
                    replica_groups=[list(range(NCORE))],
                    ins=[cc_in[idx]], outs=[cc_out[idx]]), "CC")
                ctx.w(g, "CC")
                ctx.b16(g.dma_start(out=scalg[:, :], in_=cc_out[idx]), "dCC")
            B.gpsimd(go)

            def pm(v):
                ctx.w(v, "dCC")
                post_math(v)
                ctx.b(v.drain(), "V")
            B.vector(pm)

            def bounce(g, idx=idx):
                ctx.w(g, "V")
                ctx.b16(g.dma_start(out=sc_d[idx:idx + 1, :],
                                    in_=sres[:, :]), "dCC")
                ctx.w(g, "dCC")
                for (bcol, cix) in bslots:
                    src = AP(sc_d.tensor, idx * 16 + cix, [[0, 128], [1, 1]])
                    ctx.b16(g.dma_start(
                        out=bc[:, bcol:bcol + 1], in_=src), "dCC")
                ctx.w(g, "dCC")
                g.sem_inc(ctx.sem["G"], 1)
                ctx.cnt["G"] += 1
                ev_bc[0] = ctx.cnt["G"]
            B.gpsimd(bounce)

        def c1_math(v):
            v.tensor_scalar(sres[0:1, 0:1], scalg[0:1, 0:1],
                            1.0 / Q15, 1e-12, ALU.mult, ALU.max)
            v.drain()
            v.reciprocal(sres[0:1, 1:2], sres[0:1, 0:1])

        with contextlib.ExitStack() as _es2:
            wsA0 = _es2.enter_context(al("wsA0", [128, II], F32))
            wsA1 = _es2.enter_context(al("wsA1", [128, II], F32))
            wsA2 = _es2.enter_context(al("wsA2", [128, II], F32))
            wsB0 = _es2.enter_context(al("wsB0", [128, 512], F32))
            wsB1 = _es2.enter_context(al("wsB1", [128, 512], F32))
            wsB2 = _es2.enter_context(al("wsB2", [128, 512], F32))
            sxp0 = _es2.enter_context(al("sxp0", [128, 512], F32))
            sxp1 = _es2.enter_context(al("sxp1", [128, 512], F32))
            sxp2 = _es2.enter_context(al("sxp2", [128, 512], F32))
            rxp0 = _es2.enter_context(al("rxp0", [128, 512], F32))
            rxp1 = _es2.enter_context(al("rxp1", [128, 512], F32))
            rxp2 = _es2.enter_context(al("rxp2", [128, 512], F32))
            srn = _es2.enter_context(al("srn", [128, 192], F32))
            srev = _es2.enter_context(al("srev", [128, 256], F32))
            wqb0 = _es2.enter_context(al("wqb0", [128, 512], BF16))
            wqb1 = _es2.enter_context(al("wqb1", [128, 512], BF16))
            wqb2 = _es2.enter_context(al("wqb2", [128, 512], BF16))
            pqb = _es2.enter_context(al("pqb", [128, 512], F32))
            wkst = _es2.enter_context(al("wkst", [128, 1536], F32))
            wkbf = _es2.enter_context(al("wkbf", [128, 1536], BF16))
            xsq = _es2.enter_context(al("xsq", [128, C], F32))
            xnf = _es2.enter_context(al("xnf", [128, C], F32))

            sched = []
            s_base = ctx.cnt["S"]
            s_off = [0]
            asbar = {}

            def cast_gen():
                specs = [("qkv", h) for h in range(H)] + \
                        [("wp", ccb) for ccb in range(CCH)]
                evW = [0]
                evSt = [0]
                prev_prom = [None]
                for (kind, idx) in specs:
                    def ld(g, kind=kind, idx=idx, pp=prev_prom[0]):
                        if pp is not None:
                            ctx.w(g, "S", pp)
                        if kind == "qkv":
                            src = AP(wqkT_d.tensor, idx * C * 192,
                                     [[192, 128], [192 * 128, 8], [1, 192]])
                            evW[0] = ctx.b16(
                                g.dma_start(out=wkst[:, :], in_=src), "dWL")
                        else:
                            evW[0] = ctx.b16(g.dma_start(
                                out=wkst[:, 0:1024],
                                in_=wpT_d[idx * 128:(idx + 1) * 128, :]),
                                "dWL")
                    B.gpsimd(ld)
                    myW = evW[0]
                    mySt = evSt[0]
                    s_off[0] += 1
                    prom = s_base + s_off[0]

                    def cast(s, kind=kind, myW=myW, mySt=mySt, prom=prom):
                        ctx.w(s, "dWL", myW)
                        ctx.w(s, "dWS", mySt)
                        if kind == "qkv":
                            qv = AP(wkst, 0, [[1536, 128], [192, 8], [1, 64]])
                            qo = AP(wkbf, 0, [[1536, 128], [192, 8], [1, 64]])
                            s.activation(qo, qv, AF.Copy, bias=0.0,
                                         scale=0.125)
                            kv = AP(wkst, 64,
                                    [[1536, 128], [192, 8], [1, 128]])
                            ko = AP(wkbf, 64,
                                    [[1536, 128], [192, 8], [1, 128]])
                            got = ctx.b(s.activation(ko, kv, AF.Copy,
                                                     bias=0.0, scale=1.0),
                                        "S")
                        else:
                            got = ctx.b(s.activation(
                                wkbf[:, 0:1024], wkst[:, 0:1024], AF.Copy,
                                bias=0.0, scale=1.0), "S")
                        assert got == prom
                    sched.append(cast)
                    prev_prom[0] = prom

                    def st(g, kind=kind, idx=idx, prom=prom):
                        ctx.w(g, "S", prom)
                        if kind == "qkv":
                            dst = AP(wqkb_d.tensor, idx * C * 192,
                                     [[192, 128], [192 * 128, 8], [1, 192]])
                            evSt[0] = ctx.b16(
                                g.dma_start(out=dst, in_=wkbf[:, :]), "dWS")
                        else:
                            evSt[0] = ctx.b16(g.dma_start(
                                out=wpb_d[idx * 128:(idx + 1) * 128, :],
                                in_=wkbf[:, 0:1024]), "dWS")
                    B.gpsimd(st)
                    yield

            def A_gen():
                specs = []
                for (nat_d, sr_d, nrc, icols, nblk, srstride) in (
                        (wg_d, srg_d, ICH, C, 32, II),
                        (wu_d, sru_d, ICH, C, 32, II),
                        (wd_d, srd_d, CCH, II, 96, C)):
                    for rc in range(nrc):
                        specs.append((nat_d, sr_d, rc, icols, nblk,
                                      srstride))
                nA = len(specs)
                wsl = [wsA0, wsA1, wsA2]
                evV, evT, evG = {}, {}, {}

                def emit_ld(g):
                    (nat_d, sr_d, rc, icols, nblk, srstride) = specs[g]
                    ws = wsl[g % 3]

                    def ld(sync, g=g, rc=rc, ws=ws, icols=icols,
                           nat_d=nat_d):
                        if g >= 3:
                            ctx.w(sync, "V", evV[g - 3])
                        evV[(g, "ld")] = ctx.b16(sync.dma_start(
                            out=ws[:, 0:icols],
                            in_=nat_d[rc * 128:(rc + 1) * 128, :]),
                            ("dA0", "dA1", "dA2")[g % 3])
                    B.sync(ld)

                def emit_rest(g):
                    (nat_d, sr_d, rc, icols, nblk, srstride) = specs[g]
                    ws = wsl[g % 3]

                    def red(v, g=g, ws=ws, nblk=nblk):
                        ctx.w(v, ("dA0", "dA1", "dA2")[g % 3],
                              evV[(g, "ld")])
                        if g >= 1:
                            ctx.w(v, "T", evT[g - 1])
                        src = AP(ws, 0, [[II, 128], [32, nblk], [1, 32]])
                        v.reduce_max(srn[:, 0:nblk], src, AX.X,
                                     apply_absolute_value=True)
                        v.drain()
                        v.tensor_scalar(srn[:, 0:nblk], srn[:, 0:nblk],
                                        1.0 / 7.0, 1e-12, ALU.mult, ALU.max)
                        v.drain()
                        v.reciprocal(srn[:, nblk:2 * nblk], srn[:, 0:nblk])
                        evV[g] = ctx.b(v.drain(), "V")
                    B.vector(red)

                    def tr(t, g=g, nblk=nblk):
                        ctx.w(t, "V", evV[g])
                        if g >= 1:
                            ctx.w(t, "V", evV[(g - 1, "e")])
                        if nblk <= 64:
                            ins = t.transpose(ps5[0:2 * nblk, 0:128],
                                              srn[:, 0:2 * nblk],
                                              identf[:, :])
                        else:
                            t.transpose(ps5[0:nblk, 0:128], srn[:, 0:nblk],
                                        identf[:, :])
                            ins = t.transpose(ps5[0:nblk, 128:256],
                                              srn[:, nblk:2 * nblk],
                                              identf[:, :])
                        evT[g] = ctx.b(ins, "T")
                    B.tensor(tr)

                    def vev(v, g=g, nblk=nblk):
                        ctx.w(v, "T", evT[g])
                        if g >= 1:
                            ctx.w(v, "dAS", evG[g - 1])
                        if nblk <= 64:
                            ins = v.tensor_copy(srev[0:2 * nblk, 0:128],
                                                ps5[0:2 * nblk, 0:128])
                        else:
                            v.tensor_copy(srev[0:nblk, 0:128],
                                          ps5[0:nblk, 0:128])
                            ins = v.tensor_copy(srev[0:nblk, 128:256],
                                                ps5[0:nblk, 128:256])
                        evV[(g, "e")] = ctx.b(ins, "V")
                    B.vector(vev)

                    def st(gp, g=g, rc=rc, nblk=nblk, srstride=srstride,
                           sr_d=sr_d):
                        ctx.w(gp, "V", evV[(g, "e")])
                        if nblk <= 64:
                            dst = AP(sr_d.tensor, rc * 128,
                                     [[srstride, 2 * nblk], [1, 128]])
                            evG[g] = ctx.b16(
                                gp.dma_start(out=dst,
                                             in_=srev[0:2 * nblk, 0:128]),
                                "dAS")
                        else:
                            d1 = AP(sr_d.tensor, rc * 128,
                                    [[srstride, nblk], [1, 128]])
                            ctx.b16(gp.dma_start(out=d1,
                                                 in_=srev[0:nblk, 0:128]),
                                    "dAS")
                            d2 = AP(sr_d.tensor, nblk * srstride + rc * 128,
                                    [[srstride, nblk], [1, 128]])
                            evG[g] = ctx.b16(
                                gp.dma_start(out=d2,
                                             in_=srev[0:nblk, 128:256]),
                                "dAS")
                    B.gpsimd(st)

                for g in range(nA + 2):
                    if g < nA:
                        emit_ld(g)
                    if g >= 2:
                        emit_rest(g - 2)
                        if g - 2 == 23:
                            asbar[0] = ctx.cnt["dAS"]
                        elif g - 2 == 47:
                            asbar[1] = ctx.cnt["dAS"]
                        elif g - 2 == 55:
                            asbar[2] = ctx.cnt["dAS"]
                    yield

            def B_gen():
                specs = []
                for w, (wT_src, sr_d, wq_dst, nrc, ocols, rstride) in \
                        enumerate(((wgT_d, srg_d, wgq_d, CCH, II, 32),
                                   (wuT_d, sru_d, wuq_d, CCH, II, 32),
                                   (wdT_d, srd_d, wdq_d, ICH, C, 96))):
                    for rc in range(nrc):
                        for it in range(ocols // 512):
                            specs.append((w, wT_src, sr_d, wq_dst, rc, it,
                                          ocols, rstride))
                nB = len(specs)
                wsl = [wsB0, wsB1, wsB2]
                sxl = [sxp0, sxp1, sxp2]
                rxl = [rxp0, rxp1, rxp2]
                wql = [wqb0, wqb1, wqb2]
                evV, evG = {}, {}

                def emit_ld(k):
                    (w, wT_src, sr_d, wq_dst, rc, it, ocols, rstride) = \
                        specs[k]
                    ws, sx, rx_ = wsl[k % 3], sxl[k % 3], rxl[k % 3]

                    def ld(sync, w=w, rc=rc, it=it, ws=ws, sx=sx,
                           k=k, ocols=ocols, sr_d=sr_d, wT_src=wT_src):
                        ctx.w(sync, "dAS", asbar[w])
                        if k >= 3:
                            ctx.w(sync, "V", evV[k - 3])
                        dsm = ("dB0", "dB1", "dB2")[k % 3]
                        ctx.b16(sync.dma_start(
                            out=ws[:, :],
                            in_=wT_src[rc * 128:(rc + 1) * 128,
                                       it * 512:(it + 1) * 512]), dsm)
                        soff = 4 * rc * ocols + it * 512
                        evV[(k, "ld")] = ctx.b16(
                            sync.dma_start(out=sx[:, :], in_=AP(
                                sr_d.tensor, soff,
                                [[ocols, 4], [0, 32], [1, 512]])), dsm)
                    B.sync(ld)

                    def ldr(g, w=w, rc=rc, it=it, rx_=rx_, k=k,
                            ocols=ocols, rstride=rstride, sr_d=sr_d):
                        ctx.w(g, "dAS", asbar[w])
                        if k >= 3:
                            ctx.w(g, "V", evV[k - 3])
                        soff = 4 * rc * ocols + it * 512
                        evV[(k, "ldr")] = ctx.b16(
                            g.dma_start(out=rx_[:, :], in_=AP(
                                sr_d.tensor, rstride * ocols + soff,
                                [[ocols, 4], [0, 32], [1, 512]])),
                            ("dRX0", "dRX1", "dRX2")[k % 3])
                    B.gpsimd(ldr)

                def emit_rest(k):
                    (w, wT_src, sr_d, wq_dst, rc, it, ocols, rstride) = \
                        specs[k]
                    ws, sx, rx_ = wsl[k % 3], sxl[k % 3], rxl[k % 3]
                    wq = wql[k % 3]

                    def qz(v, ws=ws, sx=sx, rx_=rx_, wq=wq, k=k):
                        ctx.w(v, ("dB0", "dB1", "dB2")[k % 3],
                              evV[(k, "ld")])
                        ctx.w(v, ("dRX0", "dRX1", "dRX2")[k % 3],
                              evV[(k, "ldr")])
                        if k >= 3:
                            ctx.w(v, ("dBS0", "dBS1", "dBS2")[(k - 3) % 3],
                                  evG[k - 3])
                        v.drain()
                        v.tensor_mul(pqb[:, :], ws[:, :], rx_[:, :])
                        v.drain()
                        v.tensor_single_scalar(pqb[:, :], pqb[:, :],
                                               MAGIC, ALU.add)
                        v.drain()
                        evV[k] = ctx.b(v.scalar_tensor_tensor(
                            wq[:, :], pqb[:, :], -MAGIC, sx[:, :],
                            ALU.add, ALU.mult), "V")
                    B.vector(qz)

                    def st(g, rc=rc, it=it, wq=wq, k=k, wq_dst=wq_dst):
                        ctx.w(g, "V", evV[k])
                        evG[k] = ctx.b16(g.dma_start(
                            out=wq_dst[rc * 128:(rc + 1) * 128,
                                       it * 512:(it + 1) * 512],
                            in_=wq[:, :]),
                            ("dBS0", "dBS1", "dBS2")[k % 3])
                    B.gpsimd(st)

                for k in range(nB + 2):
                    if k < nB:
                        emit_ld(k)
                    if k >= 2:
                        emit_rest(k - 2)
                    yield

            cg = cast_gen()
            ag = A_gen()
            bg = B_gen()

            ev_xn, ev_tp, ev_ev = {}, {}, {}
            for tt in range(NTT):
                xt = xt0 if tt % 2 == 0 else xt1

                def ld(sync, tt=tt, xt=xt):
                    if tt >= 2:
                        ctx.w(sync, "V", ev_xn[tt - 2])
                    ev_xn[(tt, "ld")] = ctx.b16(sync.dma_start(
                        out=xt[:, :], in_=x_d[tt * 128:(tt + 1) * 128, :]),
                        "dX0" if tt % 2 == 0 else "dX1")
                B.sync(ld)

                def st1(v, tt=tt, xt=xt):
                    ctx.w(v, "dX0" if tt % 2 == 0 else "dX1",
                          ev_xn[(tt, "ld")])
                    v.tensor_mul(xsq[:, :], xt[:, :], xt[:, :])
                    v.drain()
                    v.reduce_sum(stats[:, 0:1], xt[:, :], AX.X)
                    v.reduce_sum(stats[:, 1:2], xsq[:, :], AX.X)
                    v.drain()
                    v.tensor_scalar(stats[:, 2:3], stats[:, 0:1],
                                    1.0 / C, None, ALU.mult)
                    v.drain()
                    v.tensor_mul(stats[:, 3:4], stats[:, 2:3], stats[:, 2:3])
                    v.drain()
                    v.scalar_tensor_tensor(stats[:, 4:5], stats[:, 1:2],
                                           1.0 / C, stats[:, 3:4],
                                           ALU.mult, ALU.subtract)
                    v.drain()
                    v.tensor_single_scalar(stats[:, 4:5], stats[:, 4:5],
                                           1e-5, ALU.add)
                    v.drain()
                    v.reciprocal(stats[:, 5:6], stats[:, 4:5])
                    ctx.b(v.drain(), "V")
                B.vector(st1)

                vcap = ctx.cnt["V"]
                s_off[0] += 1
                prom = s_base + s_off[0]

                def sq(s, vcap=vcap, prom=prom):
                    s.wait_ge(ctx.sem["V"], vcap)
                    got = ctx.b(s.activation(stats[:, 6:7], stats[:, 5:6],
                                             AF.Sqrt, bias=0.0, scale=1.0),
                                "S")
                    assert got == prom
                sched.append(sq)

                def st3(v, prom=prom):
                    ctx.w(v, "S", prom)
                    v.tensor_scalar(stats[:, 7:8], stats[:, 2:3], -1.0, None,
                                    ALU.mult)
                    ctx.b(v.drain(), "V")
                B.vector(st3)

                def xn_act(v, tt=tt, xt=xt):
                    if tt >= 1:
                        ctx.w(v, "T", ev_tp[tt - 1])
                    ev_xn[tt] = ctx.b(v.tensor_scalar(
                        xnf[:, :], xt[:, :], stats[:, 7:8], stats[:, 6:7],
                        ALU.add, ALU.mult), "V")
                B.vector(xn_act)

                def m1p(v, tt=tt):
                    v.drain()
                    ctx.b(v.reduce_max(macc[:, tt:tt + 1], xnf[:, :], AX.X,
                                       apply_absolute_value=True), "V")
                B.vector(m1p)

                next(ag, None)

                def tp(t, tt=tt):
                    ctx.w(t, "V", ev_xn[tt])
                    if tt >= 1:
                        ctx.w(t, "V", ev_ev[tt - 1])
                    for bk in range(2):
                        pb = ps3 if bk == 0 else ps4
                        for kx in range(4):
                            ci = 4 * bk + kx
                            ins = t.transpose(pb[:, kx * 128:(kx + 1) * 128],
                                              xnf[:, ci * 128:(ci + 1) * 128],
                                              identf[:, :])
                    ev_tp[tt] = ctx.b(ins, "T")
                B.tensor(tp)

                def ev(v, tt=tt):
                    ctx.w(v, "T", ev_tp[tt])
                    for bk in range(2):
                        dst = AP(xnT, (4 * bk) * TOK + tt * 128,
                                 [[CCH * TOK, 128], [TOK, 4], [1, 128]])
                        ins = v.tensor_copy(dst, AP(
                            scps, 512 + bk * 512,
                            [[2048, 128], [128, 4], [1, 128]]))
                    ev_ev[tt] = ctx.b(ins, "V")
                B.vector(ev)

                next(ag, None)
                next(cg, None)

            for _ in cg:
                pass
            for _ in ag:
                pass

            def ln_sbody(s):
                for fn in sched:
                    fn(s)
            B.scalar(ln_sbody)

            def m1fin(v):
                v.drain()
                v.reduce_max(stats[:, 8:9], macc[:, 0:NTT], AX.X)
                ctx.b(v.drain(), "V")
            B.vector(m1fin)

            def m1g(g):
                ctx.w(g, "V")
                ctx.b16(g.dma_start(out=gath[:, :], in_=stats[:, 8:9]), "dMG")
            B.gpsimd(m1g)

            def m1f2(v):
                ctx.w(v, "dMG")
                v.memset(scal[:, :], 0.0)
                v.memset(sres[:, :], 0.0)
                v.drain()
                v.reduce_max(scal[0:1, 0:1], gath[0:1, :], AX.X)
                ctx.b(v.drain(), "V")
            B.vector(m1f2)
            ctx.barrier()

            collective(0, c1_math, [(0, 1)])

            with contextlib.ExitStack() as _es3:
                wvw = _es3.enter_context(al("wvw", [128, 2 * CCH * 512], BF16))
                wqkp = _es3.enter_context(al("wqkp", [128, CCH * 256], BF16))
                qkev = _es3.enter_context(al("qkev", [128, 512], BF16))
                qkev2 = _es3.enter_context(al("qkev2", [128, 512], BF16))

                def ldv(sync):
                    for cc in range(CCH):
                        for vg in range(2):
                            src = AP(wqkb_d.tensor,
                                     (8 * vg) * C * 192 + (cc * 128) * 192 + 128,
                                     [[192, 128], [C * 192, 8], [1, 64]])
                            ctx.b16(sync.dma_start(
                                out=wvw[:, (cc * 2 + vg) * 512:(cc * 2 + vg + 1) * 512],
                                in_=src), "dVW")
                B.sync(ldv)

                ev_vev, ev_vst = {}, {}
                k = 0
                for sq in range(SEQ):
                    for vg in range(2):
                        for tt in range(16):
                            pb = ps0 if k % 2 == 0 else ps1

                            def mm(t, sq=sq, vg=vg, tt=tt, pb=pb, k=k):
                                ctx.w(t, "dVW", 256)
                                if k >= 2:
                                    ctx.w(t, "V", ev_vev[k - 2])
                                gt = sq * 16 + tt
                                for cc in range(CCH):
                                    ins = t.matmul(
                                        pb[:, :],
                                        xnT[:, cc * TOK + gt * 128:cc * TOK + (gt + 1) * 128],
                                        wvw[:, (cc * 2 + vg) * 512:(cc * 2 + vg + 1) * 512],
                                        start=(cc == 0), stop=(cc == CCH - 1))
                                ctx.b(ins, "T")
                            B.tensor(mm)

                            def ev(v, pb=pb, k=k):
                                ctx.w(v, "T")
                                qv = qkev if k % 2 == 0 else qkev2
                                if k >= 2:
                                    ctx.w(v, "dVS0" if k % 2 == 0 else "dVS1",
                                          ev_vst[k - 2])
                                ev_vev[k] = ctx.b(v.tensor_copy(qv[:, :], pb[:, :]),
                                                  "V")
                            B.vector(ev)

                            def st(g, sq=sq, vg=vg, tt=tt, k=k):
                                ctx.w(g, "V", ev_vev[k])
                                qv = qkev if k % 2 == 0 else qkev2
                                dst = AP(v_sp.tensor,
                                         sq * 128 * (16 * C) + tt * 1024 + vg * 512,
                                         [[16 * C, 128], [1, 512]])
                                ev_vst[k] = ctx.b16(
                                    g.dma_start(out=dst, in_=qv[:, :]),
                                    "dVS0" if k % 2 == 0 else "dVS1")
                            B.gpsimd(st)
                            next(bg, None)
                            k += 1

                ev_qkev, ev_qkst = {}, {}
                kk = 0
                for sq in range(SEQ):
                    for pr in range(8):
                        def ldqk(sync, sq=sq, pr=pr):
                            ctx.w(sync, "T")
                            for cc in range(CCH):
                                src = AP(wqkb_d.tensor,
                                         (2 * pr) * C * 192 + (cc * 128) * 192,
                                         [[192, 128], [C * 192, 2], [1, 128]])
                                ctx.b16(sync.dma_start(
                                    out=wqkp[:, cc * 256:(cc + 1) * 256],
                                    in_=src), "dQW")
                            ev_qkev[(sq, pr, "ld")] = ctx.cnt["dQW"]
                        B.sync(ldqk)
                        next(bg, None)

                        for tr in range(4):
                            def mm(t, sq=sq, pr=pr, tr=tr, kk=kk):
                                ctx.w(t, "dQW", ev_qkev[(sq, pr, "ld")])
                                if kk >= 1:
                                    ctx.w(t, "V", ev_qkev[kk - 1])
                                else:
                                    ctx.w(t, "V")
                                gt = sq * 4 + tr
                                for cc in range(CCH):
                                    rhs = xnT[:, cc * TOK + gt * 512:cc * TOK + (gt + 1) * 512]
                                    s_ = (cc == 0)
                                    p_ = (cc == CCH - 1)
                                    t.matmul(ps0[0:64, :],
                                             wqkp[:, cc * 256:cc * 256 + 64],
                                             rhs, start=s_, stop=p_,
                                             tile_position=(0, 0))
                                    t.matmul(ps2[64:128, :],
                                             wqkp[:, cc * 256 + 128:cc * 256 + 192],
                                             rhs, start=s_, stop=p_,
                                             tile_position=(0, 64))
                                    t.matmul(ps1[0:64, :],
                                             wqkp[:, cc * 256 + 64:cc * 256 + 128],
                                             rhs, start=s_, stop=p_,
                                             tile_position=(0, 0))
                                    ins = t.matmul(ps3[64:128, :],
                                                   wqkp[:, cc * 256 + 192:cc * 256 + 256],
                                                   rhs, start=s_, stop=p_,
                                                   tile_position=(0, 64))
                                ctx.b(ins, "T")
                            B.tensor(mm)

                            def ev(v, kk=kk):
                                ctx.w(v, "T")
                                if kk >= 1:
                                    ctx.w(v, "dVS1" if kk % 2 == 0 else "dVS0",
                                          ev_qkst[kk - 1])
                                else:
                                    ctx.w(v, "dVS0")
                                    ctx.w(v, "dVS1")
                                v.tensor_copy(qkev[0:64, :], ps0[0:64, :])
                                v.tensor_copy(qkev[64:128, :], ps2[64:128, :])
                                v.tensor_copy(qkev2[0:64, :], ps1[0:64, :])
                                ev_qkev[kk] = ctx.b(
                                    v.tensor_copy(qkev2[64:128, :],
                                                  ps3[64:128, :]), "V")
                            B.vector(ev)

                            def st(g, sq=sq, pr=pr, tr=tr, kk=kk):
                                ctx.w(g, "V", ev_qkev[kk])
                                base = ((sq * 8 + pr) * 2) * 128 * T
                                dq = AP(qk_sp.tensor, base + tr * 512,
                                        [[T, 128], [1, 512]])
                                dk = AP(qk_sp.tensor, base + 128 * T + tr * 512,
                                        [[T, 128], [1, 512]])
                                dsm = "dVS0" if kk % 2 == 0 else "dVS1"
                                ctx.b16(g.dma_start(out=dq, in_=qkev[:, :]), dsm)
                                ev_qkst[kk] = ctx.b16(
                                    g.dma_start(out=dk, in_=qkev2[:, :]), dsm)
                            B.gpsimd(st)
                            next(bg, None)
                            kk += 1

                for _ in bg:
                    pass
                ctx.barrier()

        _esP4b = contextlib.ExitStack()
        ub0 = _esP4b.enter_context(al("ub0", [128, C], F32))
        ub1 = _esP4b.enter_context(al("ub1", [128, C], F32))
        ev_i2 = {}
        rx_lams = []
        RX0 = ctx.cnt["S"]
        for j in range(NTT):
            ub = ub0 if j % 2 == 0 else ub1

            def mk_i1(j=j, ub=ub, vprev=ev_i2.get(j - 2, 0)):
                def f(s):
                    if j == 0:
                        ctx.w(s, "G", ev_bc[0])
                    if vprev > 0:
                        s.wait_ge(ctx.sem["V"], vprev)
                    got = ctx.b(s.activation(
                        ub[:, :], xnT[:, j * 1024:(j + 1) * 1024],
                        AF.Copy, bias=MAGIC, scale=bc[:, 0:1]), "S")
                    assert got == RX0 + j + 1
                return f
            rx_lams.append(mk_i1())

            def i2(v, j=j, ub=ub):
                ctx.w(v, "S", RX0 + j + 1)
                ev_i2[j] = ctx.b(v.tensor_single_scalar(
                    xnT[:, j * 1024:(j + 1) * 1024], ub[:, :],
                    -MAGIC, ALU.add), "V")
            B.vector(i2)

        def rx_sbody(s):
            for f in rx_lams:
                f(s)
        B.scalar(rx_sbody)
        ctx.barrier()
        _esP4b.close()

        ffn_state = dict(ke=0, km=0)
        ffn_ev_mm, ffn_ev_ev2, ffn_ev_st2 = {}, {}, {}
        nU = 16 * ICH

        def ffn_pump_ev():
            k = ffn_state["ke"]
            if k >= ffn_state["km"] or k >= nU:
                return
            ffn_state["ke"] += 1
            gw, t8 = divmod(k, 8)
            sp_dst = g_sp if gw < ICH else u_sp
            ic = gw % ICH
            mcol = 32 + k if k < 8 * ICH else 224 + (k - 8 * ICH)

            def ev2(v, gw=gw, t8=t8, k=k, mcol=mcol):
                ctx.w(v, "T", ffn_ev_mm[(gw, t8)])
                sg = stg0 if k % 2 == 0 else stg1
                if k >= 2:
                    ctx.w(v, "dFS0" if k % 2 == 0 else "dFS1",
                          ffn_ev_st2[k - 2])
                v.tensor_copy(sg[:, 0:512], ps0[:, :])
                ffn_ev_ev2[k] = ctx.b(v.reduce_max(
                    macc[:, mcol:mcol + 1],
                    ps0[:, :], AX.X,
                    apply_absolute_value=True), "V")
            B.vector(ev2)

            def st2(g, ic=ic, t8=t8, k=k, sp_dst=sp_dst):
                ctx.w(g, "V", ffn_ev_ev2[k])
                sg = stg0 if k % 2 == 0 else stg1
                dst = AP(sp_dst.tensor,
                         ic * 128 * TOK + t8 * 512,
                         [[TOK, 128], [1, 512]])
                ffn_ev_st2[k] = ctx.b16(
                    g.dma_start(out=dst, in_=sg[:, 0:512]),
                    "dFS0" if k % 2 == 0 else "dFS1")
            B.gpsimd(st2)

        def ffn_pump_mm():
            k = ffn_state["km"]
            if k >= nU or ffn_state["ke"] < k:
                return
            ffn_state["km"] += 1
            gw, t8 = divmod(k, 8)
            wq_src = wgq_d if gw < ICH else wuq_d
            ic = gw % ICH
            wql = wql0 if gw % 2 == 0 else wql1

            if t8 == 0:
                def ldw(sync, gw=gw, ic=ic, wql=wql, wq_src=wq_src):
                    if gw >= 2:
                        ctx.w(sync, "T", ffn_ev_mm[(gw - 2, 7)])
                    dsm = "dFW0" if gw % 2 == 0 else "dFW1"
                    for cc in range(CCH):
                        ctx.b16(sync.dma_start(
                            out=wql[:, cc * 128:(cc + 1) * 128],
                            in_=wq_src[cc * 128:(cc + 1) * 128,
                                       ic * 128:(ic + 1) * 128]), dsm)
                    ffn_ev_mm[(gw, "ld")] = ctx.cnt[dsm]
                B.sync(ldw)

            def mm(t, gw=gw, t8=t8, k=k, wql=wql):
                ctx.w(t, "dFW0" if gw % 2 == 0 else "dFW1",
                      ffn_ev_mm[(gw, "ld")])
                if k >= 1:
                    ctx.w(t, "V", ffn_ev_ev2[k - 1])
                for cc in range(CCH):
                    ins = t.matmul(
                        ps0[:, :], wql[:, cc * 128:(cc + 1) * 128],
                        xnT[:, cc * TOK + t8 * 512:cc * TOK + (t8 + 1) * 512],
                        start=(cc == 0), stop=(cc == CCH - 1))
                ffn_ev_mm[(gw, t8)] = ctx.b(ins, "T")
            B.tensor(mm)


        with contextlib.ExitStack() as _es4:
            v_sb = _es4.enter_context(al("v_sb", [128, 16 * C], BF16))
            atT = _es4.enter_context(al("atT", [128, 8 * T], BF16))
            qp = _es4.enter_context(al("qp", [128, T], BF16))
            kp = _es4.enter_context(al("kp", [128, T], BF16))
            Pa0 = _es4.enter_context(al("Pa0", [128, T], BF16))
            Pb0 = _es4.enter_context(al("Pb0", [128, T], BF16))
            Pa1 = _es4.enter_context(al("Pa1", [128, T], BF16))
            Pb1 = _es4.enter_context(al("Pb1", [128, T], BF16))
            PT0 = _es4.enter_context(al("PT0", [128, 1024], BF16))
            PT1 = _es4.enter_context(al("PT1", [128, 1024], BF16))
            cmx = _es4.enter_context(al("cmx", [128, 32], F32))
            dnp = _es4.enter_context(al("dnp", [128, 32], F32))
            wp_sb = _es4.enter_context(al("wp_sb", [128, CCH * 1024], BF16))
            def ld_wp(sync):
                for ccb in range(CCH):
                    ctx.b16(sync.dma_start(
                        out=wp_sb[:, ccb * 1024:(ccb + 1) * 1024],
                        in_=wpb_d[ccb * 128:(ccb + 1) * 128, :]), "dPW")
            B.sync(ld_wp)

            for sq in range(SEQ):
                def ldv2(sync, sq=sq):
                    ctx.w(sync, "T")
                    src = AP(v_sp.tensor, sq * 128 * 16 * C,
                             [[16 * C, 128], [1, 16 * C]])
                    ctx.b16(sync.dma_start(out=v_sb[:, :], in_=src), "dVB")
                B.sync(ldv2)
                ev_vb = ctx.cnt["dVB"]

                for pr in range(8):
                    def ldqk2(sync, sq=sq, pr=pr):
                        ctx.w(sync, "T")
                        base = ((sq * 8 + pr) * 2) * 128 * T
                        ctx.b16(sync.dma_start(
                            out=qp[:, :],
                            in_=AP(qk_sp.tensor, base,
                                   [[T, 128], [1, T]])), "dQK")
                        ctx.b16(sync.dma_start(
                            out=kp[:, :],
                            in_=AP(qk_sp.tensor, base + 128 * T,
                                   [[T, 128], [1, T]])), "dQK")
                    B.sync(ldqk2)
                    ev_qk2 = ctx.cnt["dQK"]

                    S0 = ctx.cnt["S"]

                    def sprom(qt, hh):
                        return S0 + 2 * qt + hh + 1 if qt >= 0 else S0

                    s_lams = []
                    ev_nrm = {}
                    ev_tp = {}
                    nrm_pend = []
                    for qt in range(17):
                      ffn_pump_ev()
                      if qt < 16:
                        nsr = qt // 4 + 1
                        last_n = (qt % 4 + 1) * 128
                        for hh in range(2):
                            b0 = 64 * hh
                            Ph = (Pa0 if hh == 0 else Pb0) if qt % 2 == 0 \
                                else (Pa1 if hh == 0 else Pb1)
                            slot = (qt % 2) * 2 + hh
                            if nsr <= 2:
                                scbase = 0 if hh == 0 else 1024
                                bankfree = sprom(qt - 1, hh)
                            else:
                                scbase = 0
                                bankfree = sprom(qt, hh - 1) if hh == 1 \
                                    else sprom(qt - 1, 1)
                            qw = (qt + 1) * 128

                            def mm(t, b0=b0, qt=qt, nsr=nsr, last_n=last_n,
                                   ev_qk2=ev_qk2, scbase=scbase,
                                   bankfree=bankfree):
                                ctx.w(t, "dQK", ev_qk2)
                                ctx.w(t, "S", bankfree)
                                for sr in range(nsr):
                                    n = 512 if sr < nsr - 1 else last_n
                                    ins = t.matmul(
                                        scps[:, scbase + sr * 512:scbase + sr * 512 + n],
                                        qp[b0:b0 + 64, qt * 128:(qt + 1) * 128],
                                        kp[b0:b0 + 64, sr * 512:sr * 512 + n],
                                        start=True, stop=True)
                                ctx.b(ins, "T")
                            B.tensor(mm)

                            def msk(v, qt=qt, hh=hh, qw=qw, scbase=scbase,
                                    slot=slot):
                                ctx.w(v, "T")
                                ctx.w(v, "S", sprom(qt - 2, hh))
                                v.tensor_add(scps[:, scbase + qw - 128:scbase + qw],
                                             scps[:, scbase + qw - 128:scbase + qw],
                                             maskneg[:, :])
                                v.drain()
                                ctx.b(v.reduce_max(
                                    cmx[:, slot * 8 + 6:slot * 8 + 7],
                                    scps[:, scbase:scbase + qw],
                                    AX.X, negate=True), "V")
                            B.vector(msk)

                            vev_cap = ctx.cnt["V"]
                            tev_cap = ev_tp.get(qt - 2, 0)

                            def ex(s, qt=qt, hh=hh, qw=qw, Ph=Ph, slot=slot,
                                   scbase=scbase, vev=vev_cap, tev=tev_cap):
                                if vev > 0:
                                    s.wait_ge(ctx.sem["V"], vev)
                                if tev > 0:
                                    s.wait_ge(ctx.sem["T"], tev)
                                ins = s.activation(
                                    Ph[:, 0:qw],
                                    scps[:, scbase:scbase + qw], AF.Exp,
                                    bias=cmx[:, slot * 8 + 6:slot * 8 + 7],
                                    scale=1.0,
                                    accum_out=dnp[:, slot * 8:slot * 8 + 1])
                                got = ctx.b(ins, "S")
                                assert got == sprom(qt, hh), (got, qt, hh)
                            s_lams.append(ex)

                            nrm_pend.append((qt, Ph, hh, slot))

                      ffn_pump_mm()

                      if qt >= 1:
                        while nrm_pend and nrm_pend[0][0] == qt - 1:
                            (nqt, nPh, nhh, nslot) = nrm_pend.pop(0)

                            def nrm(v, qt=nqt, Ph=nPh, hh=nhh, slot=nslot):
                                ctx.w(v, "S", sprom(qt, hh))
                                v.reciprocal(dnp[:, slot * 8 + 7:slot * 8 + 8],
                                             dnp[:, slot * 8:slot * 8 + 1])
                                v.drain()
                                ev_nrm[(qt, hh)] = ctx.b(v.tensor_scalar(
                                    Ph[:, 0:(qt + 1) * 128],
                                    Ph[:, 0:(qt + 1) * 128],
                                    dnp[:, slot * 8 + 7:slot * 8 + 8],
                                    None, ALU.mult), "V")
                            B.vector(nrm)
                        pqt = qt - 1
                        nbat = (pqt + 4) // 4
                        PhA = Pa0 if pqt % 2 == 0 else Pa1
                        PhB = Pb0 if pqt % 2 == 0 else Pb1
                        for bt in range(nbat):
                            scs = list(range(bt * 4, min(pqt + 1, bt * 4 + 4)))
                            PTb = PT0 if bt % 2 == 0 else PT1

                            def tp(t, pqt=pqt, scs=scs, PhA=PhA, PhB=PhB):
                                ctx.w(t, "V")
                                for i, sc in enumerate(scs):
                                    for hh in range(2):
                                        Ph = PhA if hh == 0 else PhB
                                        ins = t.transpose(
                                            ps6[:, (i * 2 + hh) * 128:(i * 2 + hh + 1) * 128],
                                            Ph[:, sc * 128:(sc + 1) * 128],
                                            identb[:, :])
                                ev_tp[pqt] = ctx.b(ins, "T")
                            B.tensor(tp)

                            def pev(v, PTb=PTb, scs=scs):
                                ctx.w(v, "T")
                                n = len(scs) * 256
                                ctx.b(v.tensor_copy(PTb[:, 0:n], ps6[:, 0:n]),
                                      "V")
                            B.vector(pev)

                            def av(t, pr=pr, pqt=pqt, PTb=PTb, scs=scs,
                                   ev_vb=ev_vb, S0=S0):
                                ctx.w(t, "dVB", ev_vb)
                                if pqt == 0 and pr == 0:
                                    ctx.w(t, "S", S0)
                                ctx.w(t, "V")
                                reg = (pqt % 4) * 128
                                for i, sc in enumerate(scs):
                                    for hh in range(2):
                                        h = 2 * pr + hh
                                        avb = ps7 if hh == 0 else ps1
                                        vcol = sc * 1024 + (h // 8) * 512 + (h % 8) * 64
                                        ins = t.matmul(
                                            avb[64 * hh:64 * hh + 64,
                                                reg:reg + 128],
                                            v_sb[:, vcol:vcol + 64],
                                            PTb[:, (i * 2 + hh) * 128:(i * 2 + hh) * 128 + 128],
                                            start=(sc == 0), stop=(sc == pqt),
                                            tile_position=(0, 64 * hh))
                                ev_tp[pqt] = ctx.b(ins, "T")
                            B.tensor(av)

                        if pqt % 4 == 3:
                            qr = pqt // 4

                            def aev(v, pr=pr, qr=qr):
                                ctx.w(v, "T")
                                v.tensor_copy(
                                    atT[0:64, pr * T + qr * 512:pr * T + qr * 512 + 512],
                                    ps7[0:64, :])
                                ctx.b(v.tensor_copy(
                                    atT[64:128, pr * T + qr * 512:pr * T + qr * 512 + 512],
                                    ps1[64:128, :]), "V")
                            B.vector(aev)


                    def sbody(s, s_lams=s_lams):
                        for f in s_lams:
                            f(s)
                    B.scalar(sbody)
                    assert ctx.cnt["S"] == S0 + 32

                ffn_pump_ev()

                ev_pst = {}
                kk = 0
                prj_lams = []
                PS0 = ctx.cnt["S"]
                for tt in range(16):
                    for ch in range(2):
                        pb = ps0 if kk % 2 == 0 else ps1

                        def mm(t, sq=sq, tt=tt, ch=ch, pb=pb, kk=kk):
                            ctx.w(t, "dPW", 128)
                            ctx.w(t, "V")
                            if kk >= 2:
                                ctx.w(t, "S", PS0 + kk - 1)
                            t.matmul(pb[:, :], ones1[0:1, :],
                                     bprojb[0:1, ch * 512:(ch + 1) * 512],
                                     start=True, stop=False)
                            for pr in range(8):
                                ins = t.matmul(
                                    pb[:, :],
                                    atT[:, pr * T + tt * 128:pr * T + (tt + 1) * 128],
                                    wp_sb[:, pr * 1024 + ch * 512:pr * 1024 + ch * 512 + 512],
                                    start=False, stop=(pr == 7))
                            ctx.b(ins, "T")
                        B.tensor(mm)

                        def mk_pev(kk=kk, pb=pb,
                                   tcap=None, pstprev=None):
                            def f(s, tcap=ctx.cnt["T"],
                                  pstprev=ev_pst.get(
                                      kk - 2,
                                      ctx.cnt["dPS0" if kk % 2 == 0
                                              else "dPS1"])):
                                s.wait_ge(ctx.sem["T"], tcap)
                                sh = stgh0 if kk % 2 == 0 else stgh1
                                ctx.w(s, "dPS0" if kk % 2 == 0 else "dPS1",
                                      pstprev)
                                got = ctx.b(s.activation(
                                    sh[:, :], pb[:, :], AF.Copy,
                                    bias=0.0, scale=1.0), "S")
                                assert got == PS0 + kk + 1
                            return f
                        prj_lams.append(mk_pev())

                        def pst(g, sq=sq, tt=tt, ch=ch, kk=kk):
                            ctx.w(g, "S", PS0 + kk + 1)
                            sh = stgh0 if kk % 2 == 0 else stgh1
                            dst = AP(ao_sp.tensor,
                                     (sq * 16 + tt) * 128 * C + ch * 512,
                                     [[C, 128], [1, 512]])
                            ev_pst[kk] = ctx.b16(
                                g.dma_start(out=dst, in_=sh[:, :]),
                                "dPS0" if kk % 2 == 0 else "dPS1")
                        B.gpsimd(pst)
                        kk += 1

                def prj_sbody(s, prj_lams=prj_lams):
                    for f in prj_lams:
                        f(s)
                B.scalar(prj_sbody)
            ctx.barrier()

        with contextlib.ExitStack() as _es5:
            ub2 = _es5.enter_context(al("ub2", [128, C], F32))
            ub6 = _es5.enter_context(al("ub6", [128, C], F32))
            ub7 = _es5.enter_context(al("ub7", [128, C], F32))
            stgc0 = _es5.enter_context(al("stgc0", [128, 1024], BF16))
            stgc1 = _es5.enter_context(al("stgc1", [128, 1024], BF16))
            ub3 = _es5.enter_context(al("ub3", [128, C], F32))
            ub4 = _es5.enter_context(al("ub4", [128, C], F32))
            ub5 = _es5.enter_context(al("ub5", [128, C], F32))
            ldt0 = _es5.enter_context(al("ldt0", [128, 1024], BF16))
            ldt1 = _es5.enter_context(al("ldt1", [128, 1024], BF16))
            ldt2 = _es5.enter_context(al("ldt2", [128, 1024], BF16))
            ldt3 = _es5.enter_context(al("ldt3", [128, 1024], BF16))
            rob = _es5.enter_context(al("rob", [128, ICH * 512], BF16))
            rob2 = _es5.enter_context(al("rob2", [128, ICH * 512], BF16))
            def mfin(col0, ncols, slot, clear):
                def f1(v):
                    v.drain()
                    v.reduce_max(stats[:, 9:10], macc[:, col0:col0 + ncols],
                                 AX.X)
                    ctx.b(v.drain(), "V")
                B.vector(f1)

                def f2(g):
                    ctx.w(g, "V")
                    ctx.b16(g.dma_start(out=gath[:, :], in_=stats[:, 9:10]),
                            "dMG")
                B.gpsimd(f2)

                def f3(v, slot=slot, clear=clear):
                    ctx.w(v, "dMG")
                    if clear:
                        v.memset(scal[:, :], 0.0)
                        v.drain()
                    v.reduce_max(scal[0:1, slot:slot + 1], gath[0:1, :], AX.X)
                    ctx.b(v.drain(), "V")
                B.vector(f3)

            ffn_pump_ev()
            mfin(32, 192, 0, True)
            ctx.barrier()

            def c2_math(v):
                v.tensor_mul(sres[0:1, 10:11], sres[0:1, 0:1], scalg[0:1, 0:1])
                v.drain()
                v.tensor_scalar(sres[0:1, 2:3], sres[0:1, 10:11],
                                1.0 / Q15, 1e-12, ALU.mult, ALU.max)
                v.drain()
                v.reciprocal(sres[0:1, 11:12], sres[0:1, 2:3])
                v.drain()
                v.tensor_mul(sres[0:1, 3:4], sres[0:1, 0:1], sres[0:1, 11:12])

            collective(1, c2_math, [(1, 3), (2, 2)])

            ntile = ICH * 4
            ev_ld3, ev_v1, ev_v2, ev_st3 = {}, {}, {}, {}
            sig_lams = []
            S8 = ctx.cnt["S"]
            for k in range(ntile):
                ffn_pump_ev()
                ffn_pump_mm()
                ic, tc = divmod(k, 4)
                gt_ = ldt0 if k % 2 == 0 else ldt1
                uu = ub2 if k % 2 == 0 else ub3
                rg_ = ub4 if k % 2 == 0 else ub5
                ss = ub6 if k % 2 == 0 else ub7

                def ld3(sync, ic=ic, tc=tc, gt_=gt_, k=k):
                    if k >= 2:
                        ctx.w(sync, "V", ev_v2.get(k - 2, 0))
                    dsm = "dG0" if k % 2 == 0 else "dG1"
                    ctx.b16(sync.dma_start(
                        out=gt_[:, :],
                        in_=AP(g_sp.tensor, ic * 128 * TOK + tc * 1024,
                               [[TOK, 128], [1, 1024]])), dsm)
                    ev_ld3[k] = ctx.cnt[dsm]
                B.sync(ld3)

                def v1f(v, k=k, gt_=gt_, uu=uu, rg_=rg_):
                    if k == 0:
                        ctx.w(v, "G", ev_bc[0])
                    ctx.w(v, "dG0" if k % 2 == 0 else "dG1", ev_ld3[k])
                    if k >= 2:
                        ctx.w(v, "S", S8 + k - 1)
                    v.tensor_scalar(uu[:, :], gt_[:, :], bc[:, 1:2], MAGIC,
                                    ALU.mult, ALU.add)
                    v.drain()
                    ev_v1[k] = ctx.b(v.tensor_single_scalar(
                        rg_[:, :], uu[:, :], -MAGIC, ALU.add), "V")
                B.vector(v1f)

                vcap1 = ctx.cnt["V"]
                vcap2 = ev_v2.get(k - 2, 0)
                sig_lams.append((vcap1, vcap2, k, rg_, ss))

                def v2a(v, k=k, uu=uu, ss=ss):
                    ctx.w(v, "S", S8 + k + 1)
                    v.tensor_scalar(uu[:, :], ss[:, :], 65536.0, MAGIC,
                                    ALU.mult, ALU.add)
                    v.drain()
                    ctx.b(v.tensor_scalar(ss[:, :], uu[:, :], -MAGIC, 65535.0,
                                          ALU.add, ALU.min), "V")
                B.vector(v2a)

                def g2f(g, k=k, ss=ss, rg_=rg_):
                    ctx.w(g, "V")
                    if k >= 2:
                        ctx.w(g, "dTG0" if k % 2 == 0 else "dTG1",
                              ev_st3.get(k - 2, 0))
                    sg = stgc0 if k % 2 == 0 else stgc1
                    ctx.b(g.tensor_mul(sg[:, :], rg_[:, :], ss[:, :]), "G")
                B.gpsimd(g2f)

                def v2b(v, k=k):
                    ctx.w(v, "G")
                    sg = stgc0 if k % 2 == 0 else stgc1
                    ev_v2[k] = ctx.b(v.reduce_max(
                        macc[:, 32 + k:33 + k], sg[:, :], AX.X,
                        apply_absolute_value=True), "V")
                B.vector(v2b)

                def st3(g, k=k, ic=ic, tc=tc):
                    ctx.w(g, "V", ev_v2[k])
                    sg = stgc0 if k % 2 == 0 else stgc1
                    ev_st3[k] = ctx.b16(g.dma_start(
                        out=AP(pt_sp.tensor, ic * 128 * TOK + tc * 1024,
                               [[TOK, 128], [1, 1024]]),
                        in_=sg[:, :]), "dTG0" if k % 2 == 0 else "dTG1")
                B.gpsimd(st3)

            def sig_body(s):
                for (vcap1, vcap2, k, rg_, ss) in sig_lams:
                    s.wait_ge(ctx.sem["V"], vcap1)
                    if vcap2 > 0:
                        s.wait_ge(ctx.sem["V"], vcap2)
                    got = ctx.b(s.activation(ss[:, :], rg_[:, :], AF.Sigmoid,
                                             bias=0.0, scale=bc[:, 2:3]), "S")
                    assert got == S8 + k + 1
            B.scalar(sig_body)
            mfin(32, ntile, 1, True)

            while ffn_state["km"] < nU or ffn_state["ke"] < nU:
                ffn_pump_ev()
                ffn_pump_mm()
            mfin(224, 192, 0, False)
            ctx.barrier()

            def c3_math(v):
                v.tensor_mul(sres[0:1, 10:11], sres[0:1, 0:1], scalg[0:1, 0:1])
                v.drain()
                v.tensor_scalar(sres[0:1, 4:5], sres[0:1, 10:11],
                                1.0 / Q15, 1e-12, ALU.mult, ALU.max)
                v.drain()
                v.reciprocal(sres[0:1, 11:12], sres[0:1, 4:5])
                v.drain()
                v.tensor_mul(sres[0:1, 5:6], sres[0:1, 0:1], sres[0:1, 11:12])
                v.tensor_scalar(sres[0:1, 12:13], sres[0:1, 2:3],
                                1.0 / 65536.0, None, ALU.mult)
                v.drain()
                v.tensor_mul(sres[0:1, 10:11], sres[0:1, 12:13],
                             scalg[0:1, 1:2])
                v.drain()
                v.tensor_scalar(sres[0:1, 6:7], sres[0:1, 10:11],
                                1.0 / Q15, 1e-12, ALU.mult, ALU.max)
                v.drain()
                v.reciprocal(sres[0:1, 11:12], sres[0:1, 6:7])
                v.drain()
                v.tensor_mul(sres[0:1, 7:8], sres[0:1, 12:13],
                             sres[0:1, 11:12])

            collective(2, c3_math, [(3, 5), (4, 7)])

            wdq_tot = [0]

            def wdpre(sync):
                for ic in range(ICH):
                    wdq_tot[0] = ctx.b16(sync.dma_start(
                        out=xnT[:, ic * 1024:(ic + 1) * 1024],
                        in_=wdq_d[ic * 128:(ic + 1) * 128, :]), "dW50")
            B.sync(wdpre)

            ev_ld4, ev_v3, ev_v4, ev_st4 = {}, {}, {}, {}
            p9_lams = []
            S9 = ctx.cnt["S"]
            for k in range(ntile):
                ic, tc = divmod(k, 4)
                ut_ = ldt0 if k % 2 == 0 else ldt1
                ptt = ldt2 if k % 2 == 0 else ldt3
                uu = ub2 if k % 2 == 0 else ub3
                ru_ = ub4 if k % 2 == 0 else ub5
                rt_ = ub6 if k % 2 == 0 else ub7

                def ld4(sync, ic=ic, tc=tc, ut_=ut_, ptt=ptt, k=k):
                    if k >= 2:
                        ctx.w(sync, "V", ev_v4.get(k - 2, 0))
                    dsm = "dG0" if k % 2 == 0 else "dG1"
                    ctx.b16(sync.dma_start(
                        out=ut_[:, :],
                        in_=AP(u_sp.tensor, ic * 128 * TOK + tc * 1024,
                               [[TOK, 128], [1, 1024]])), dsm)
                    ctx.b16(sync.dma_start(
                        out=ptt[:, :],
                        in_=AP(pt_sp.tensor, ic * 128 * TOK + tc * 1024,
                               [[TOK, 128], [1, 1024]])), dsm)
                    ev_ld4[k] = ctx.cnt[dsm]
                B.sync(ld4)

                def mk_s3(k=k, ut_=ut_, uu=uu,
                          tok=("dG0" if k % 2 == 0 else "dG1", 0),
                          vprev=ev_v4.get(k - 2, 0)):
                    def f(s, ldv=ev_ld4[k]):
                        if k == 0:
                            ctx.w(s, "G", ev_bc[0])
                        ctx.w(s, "dG0" if k % 2 == 0 else "dG1", ldv)
                        if vprev > 0:
                            s.wait_ge(ctx.sem["V"], vprev)
                        got = ctx.b(s.activation(
                            uu[:, :], ut_[:, :], AF.Copy,
                            bias=MAGIC, scale=bc[:, 3:4]), "S")
                        assert got == S9 + 2 * k + 1
                    return f
                p9_lams.append(mk_s3())

                def v3f(v, k=k, uu=uu, ru_=ru_):
                    ctx.w(v, "S", S9 + 2 * k + 1)
                    ev_v3[k] = ctx.b(v.tensor_single_scalar(
                        ru_[:, :], uu[:, :], -MAGIC, ALU.add), "V")
                B.vector(v3f)

                def mk_s4(k=k, ptt=ptt, uu=uu, vv=None):
                    def f(s, vv=ev_v3[k]):
                        s.wait_ge(ctx.sem["V"], vv)
                        got = ctx.b(s.activation(
                            uu[:, :], ptt[:, :], AF.Copy,
                            bias=MAGIC, scale=bc[:, 4:5]), "S")
                        assert got == S9 + 2 * k + 2
                    return f
                p9_lams.append(mk_s4())

                def v4a(v, k=k, uu=uu, rt_=rt_):
                    ctx.w(v, "S", S9 + 2 * k + 2)
                    ctx.b(v.tensor_single_scalar(rt_[:, :], uu[:, :],
                                                 -MAGIC, ALU.add), "V")
                B.vector(v4a)

                def g4f(g, k=k, ru_=ru_, rt_=rt_):
                    ctx.w(g, "V")
                    if k >= 2:
                        ctx.w(g, "dTG0" if k % 2 == 0 else "dTG1",
                              ev_st4.get(k - 2, 0))
                    else:
                        ctx.w(g, "dTG0" if k % 2 == 0 else "dTG1")
                    sg = stgc0 if k % 2 == 0 else stgc1
                    ctx.b(g.tensor_mul(sg[:, :], ru_[:, :], rt_[:, :]), "G")
                B.gpsimd(g4f)

                def v4b(v, k=k):
                    ctx.w(v, "G")
                    sg = stgc0 if k % 2 == 0 else stgc1
                    ev_v4[k] = ctx.b(v.reduce_max(
                        macc[:, 32 + k:33 + k], sg[:, :], AX.X,
                        apply_absolute_value=True), "V")
                B.vector(v4b)

                def st4(g, k=k, ic=ic, tc=tc):
                    ctx.w(g, "V", ev_v4[k])
                    sg = stgc0 if k % 2 == 0 else stgc1
                    ev_st4[k] = ctx.b16(g.dma_start(
                        out=AP(pu_sp.tensor, ic * 128 * TOK + tc * 1024,
                               [[TOK, 128], [1, 1024]]),
                        in_=sg[:, :]), "dTG0" if k % 2 == 0 else "dTG1")
                B.gpsimd(st4)

            def p9_sbody(s):
                for f in p9_lams:
                    f(s)
            B.scalar(p9_sbody)
            mfin(32, ntile, 0, True)
            ctx.barrier()

            def c4_math(v):
                v.tensor_mul(sres[0:1, 12:13], sres[0:1, 6:7], sres[0:1, 4:5])
                v.drain()
                v.tensor_mul(sres[0:1, 10:11], sres[0:1, 12:13],
                             scalg[0:1, 0:1])
                v.drain()
                v.tensor_scalar(sres[0:1, 8:9], sres[0:1, 10:11],
                                1.0 / Q15, 1e-12, ALU.mult, ALU.max)
                v.drain()
                v.reciprocal(sres[0:1, 11:12], sres[0:1, 8:9])
                v.drain()
                v.tensor_mul(sres[0:1, 9:10], sres[0:1, 12:13],
                             sres[0:1, 11:12])

            collective(3, c4_math, [(5, 9), (6, 8)])

            ev_ro, ev_x10, ev_ao, ev_dn, ev_cmb, ev_ost = {}, {}, {}, {}, {}, {}
            ev_ost_last = {0: 0, 1: 0}
            p10_lams = []
            S10 = ctx.cnt["S"]
            for tr8 in range(8):
                robc = rob if tr8 % 2 == 0 else rob2
                for ic in range(ICH):
                    kr = tr8 * ICH + ic
                    ldt = ldt0 if kr % 2 == 0 else ldt1
                    uu = ub2 if kr % 2 == 0 else ub3

                    def ld5(sync, tr8=tr8, ic=ic, ldt=ldt, kr=kr):
                        if kr >= 2:
                            ctx.w(sync, "S", ev_ro.get((kr - 2, "s"), 0))
                        if tr8 >= 2:
                            ctx.w(sync, "T", ev_dn.get((tr8 - 2, 1), 0))
                        dsm = "dR0" if kr % 2 == 0 else "dR1"
                        ctx.b16(sync.dma_start(
                            out=ldt[:, 0:512],
                            in_=AP(pu_sp.tensor, ic * 128 * TOK + tr8 * 512,
                                   [[TOK, 128], [1, 512]])), dsm)
                        ev_ro[(kr, "d")] = ctx.cnt[dsm]
                    B.sync(ld5)

                    def mk_s5(kr=kr, ldt=ldt, uu=uu,
                              vprev=ev_ro.get((kr - 2, "v"), 0)):
                        def f(s, ldv=ev_ro[(kr, "d")]):
                            if kr == 0:
                                ctx.w(s, "G", ev_bc[0])
                            ctx.w(s, "dR0" if kr % 2 == 0 else "dR1", ldv)
                            if vprev > 0:
                                s.wait_ge(ctx.sem["V"], vprev)
                            got = ctx.b(s.activation(
                                uu[:, 0:512], ldt[:, 0:512], AF.Copy,
                                bias=MAGIC, scale=bc[:, 5:6]), "S")
                            assert got == S10 + kr + 1
                        return f
                    p10_lams.append(mk_s5())
                    ev_ro[(kr, "s")] = S10 + kr + 1

                    def v5f(v, kr=kr, ic=ic, uu=uu, robc=robc):
                        ctx.w(v, "S", ev_ro[(kr, "s")])
                        ev_ro[(kr, "v")] = ctx.b(v.tensor_single_scalar(
                            robc[:, ic * 512:(ic + 1) * 512], uu[:, 0:512],
                            -MAGIC, ALU.add), "V")
                    B.vector(v5f)

                ro_done = ctx.cnt["V"]

                for half in range(2):
                    banks = [ps0, ps1, ps2, ps3]

                    def merged(tr8=tr8, half=half, ro_done=ro_done,
                               banks=banks, robc=robc):
                        def mm5(t, tr8=tr8, half=half, robc=robc):
                            ctx.w(t, "dW50", wdq_tot[0])
                            ctx.w(t, "V")
                            for ic in range(ICH):
                                for tj in range(2):
                                    tt = half * 2 + tj
                                    for ch in range(2):
                                        ins = t.matmul(
                                            banks[tj * 2 + ch][:, :],
                                            robc[:, ic * 512 + tt * 128:ic * 512 + (tt + 1) * 128],
                                            xnT[:, ic * 1024 + ch * 512:ic * 1024 + (ch + 1) * 512],
                                            start=(ic == 0),
                                            stop=(ic == ICH - 1))
                            ev_dn[(tr8, half)] = ctx.b(ins, "T")
                        B.tensor(mm5)
                    merged()

                    for tj in range(2):
                        tt = half * 2 + tj
                        gtt = tr8 * 4 + tt
                        xt = xt0 if gtt % 2 == 0 else xt1

                        def ldx(sync, gtt=gtt, xt=xt, tr8=tr8, half=half,
                                tj=tj):
                            if ev_cmb:
                                ctx.w(sync, "V")
                            dsm = "dX0" if gtt % 2 == 0 else "dX1"
                            ctx.b16(sync.dma_start(
                                out=xt[:, :],
                                in_=x_d[gtt * 128:(gtt + 1) * 128, :]), dsm)
                            ev_x10[gtt] = (dsm, ctx.cnt[dsm])
                        B.sync(ldx)

                        for ch in range(2):
                            sh = stgh0 if ch == 0 else stgh1
                            ou = outs0 if ch == 0 else outs1

                            def lda2(sync, gtt=gtt, ch=ch, sh=sh):
                                ctx.w(sync, "V")
                                dsm = "dAO0" if ch == 0 else "dAO1"
                                ctx.b16(sync.dma_start(
                                    out=sh[:, :],
                                    in_=AP(ao_sp.tensor,
                                           gtt * 128 * C + ch * 512,
                                           [[C, 128], [1, 512]])), dsm)
                                ev_ao[(gtt, ch)] = (dsm, ctx.cnt[dsm])
                            B.sync(lda2)

                            def cmb2(v, tr8=tr8, half=half, tj=tj, ch=ch,
                                     gtt=gtt, xt=xt, sh=sh, ou=ou):
                                ctx.w(v, "T", ev_dn[(tr8, half)])
                                ctx.wt(v, ev_ao[(gtt, ch)])
                                ctx.wt(v, ev_x10[gtt])
                                ctx.w(v, "dOUT0" if ch == 0 else "dOUT1",
                                      ev_ost_last[ch])
                                bank = [ps0, ps1, ps2, ps3][tj * 2 + ch]
                                ev_cmb[(gtt, ch)] = ctx.b(
                                    v.scalar_tensor_tensor(
                                        ou[:, :], bank[:, :], bc[:, 6:7],
                                        sh[:, :], ALU.mult, ALU.add), "V")
                            B.vector(cmb2)

                            def gadd(g, gtt=gtt, ch=ch, ou=ou, xt=xt):
                                ctx.w(g, "V", ev_cmb[(gtt, ch)])
                                ev_cmb[(gtt, ch, "g")] = ctx.b(g.tensor_add(
                                    ou[:, :], ou[:, :],
                                    xt[:, ch * 512:(ch + 1) * 512]), "G")
                            B.gpsimd(gadd)

                            def ost(g, gtt=gtt, ch=ch, ou=ou):
                                ctx.w(g, "G", ev_cmb[(gtt, ch, "g")])
                                ev_ost[(gtt, ch)] = ctx.b16(g.dma_start(
                                    out=out_d[gtt * 128:(gtt + 1) * 128,
                                              ch * 512:(ch + 1) * 512],
                                    in_=ou[:, :]),
                                    "dOUT0" if ch == 0 else "dOUT1")
                                ev_ost_last[ch] = ev_ost[(gtt, ch)]
                            B.gpsimd(ost)

            def p10_sbody(s):
                for f in p10_lams:
                    f(s)
            B.scalar(p10_sbody)
            ctx.barrier()

    return nc


_NC_CACHE = [None]


def kernel(L0_attn_x, L0_attn_w_qkv, L0_attn_w_proj, L0_attn_b_proj,
           L0_ffn_w_gate, L0_ffn_w_up, L0_ffn_w_down):
    x = np.ascontiguousarray(np.asarray(L0_attn_x, dtype=np.float32))
    wqkv = np.ascontiguousarray(np.asarray(L0_attn_w_qkv, dtype=np.float32))
    wproj = np.ascontiguousarray(np.asarray(L0_attn_w_proj, dtype=np.float32))
    bproj = np.ascontiguousarray(np.asarray(L0_attn_b_proj, dtype=np.float32))
    wg = np.ascontiguousarray(np.asarray(L0_ffn_w_gate, dtype=np.float32))
    wu = np.ascontiguousarray(np.asarray(L0_ffn_w_up, dtype=np.float32))
    wd = np.ascontiguousarray(np.asarray(L0_ffn_w_down, dtype=np.float32))

    wqkT = np.ascontiguousarray(wqkv.transpose(0, 2, 1))
    wpT = np.ascontiguousarray(wproj.T)
    wgT = np.ascontiguousarray(wg.T)
    wuT = np.ascontiguousarray(wu.T)
    wdT = np.ascontiguousarray(wd.T)

    identb = np.eye(128, dtype=ml_dtypes.bfloat16)
    identf = np.eye(128, dtype=np.float32)
    maskneg = np.triu(np.full((128, 128), -1.0e30, dtype=np.float32), 1)
    ones1 = np.ones((1, 128), dtype=ml_dtypes.bfloat16)

    if _NC_CACHE[0] is None:
        _NC_CACHE[0] = build_nc()
    nc = _NC_CACHE[0]

    shared = dict(wqkT=wqkT, wpT=wpT, bproj=bproj.reshape(1, C),
                  wg=wg, wgT=wgT, wu=wu, wuT=wuT, wd=wd, wdT=wdT,
                  identb_in=identb, identf_in=identf,
                  maskneg_in=maskneg, ones1_in=ones1)
    in_maps = []
    for c in range(NCORE):
        m = dict(shared)
        m["x"] = np.ascontiguousarray(
            x[2 * c:2 * c + 2].reshape(TOK, C))
        in_maps.append(m)

    res = run_bass_kernel_spmd(nc, in_maps, core_ids=list(range(NCORE)))
    out = np.concatenate(
        [r["out"].reshape(2, T, C) for r in res.results], axis=0)
    return out.astype(np.float32)


# version 11
# speedup vs baseline: 1.1912x; 1.0444x over previous
import contextlib
import numpy as np
import ml_dtypes
import concourse.bass as bass
import concourse.mybir as mybir
from concourse.bass_utils import run_bass_kernel_spmd

F32 = mybir.dt.float32
BF16 = mybir.dt.bfloat16
F16 = mybir.dt.float16
AF = mybir.ActivationFunctionType
ALU = mybir.AluOpType
AX = mybir.AxisListType
AP = bass.AP

MAGIC = 12582912.0
Q15 = 32767.0

SEQ, T, C, H, HS, II = 2, 2048, 1024, 16, 64, 3072
TOK = SEQ * T
CCH = C // 128
ICH = II // 128
NTT = TOK // 128
NCORE = 8

ENGINES = ("sync", "vector", "scalar", "tensor", "gpsimd")


class Ctx:
    def __init__(self, block, sems, nc=None, es=None):
        self.block = block
        self.sem = sems
        self.cnt = {k: 0 for k in sems}
        self._nc = nc
        self._es = es

    def mksem(self, name):
        h = self._es.enter_context(self._nc.semaphore(name))
        self.sem[name] = h
        self.cnt[name] = 0
        return name

    def tok(self, s):
        return (s, self.cnt[s])

    def wt(self, eng, token):
        s, v = token
        if v > 0:
            eng.wait_ge(self.sem[s], v)

    def b(self, instr, s, amt=1):
        self.cnt[s] += amt
        instr.then_inc(self.sem[s], amt)
        return self.cnt[s]

    def b16(self, instr, s):
        return self.b(instr, s, 16)

    def w(self, eng, s, val=None):
        if val is None:
            val = self.cnt[s]
        if val > 0:
            eng.wait_ge(self.sem[s], val)

    def barrier(self):
        snap = dict(self.cnt)
        for ename in ENGINES:
            def body(e, snap=snap):
                for s, v in snap.items():
                    if v > 0:
                        e.wait_ge(self.sem[s], v)
            getattr(self.block, ename)(body)


def build_nc():
    nc = bass.Bass("TRN2", target_bir_lowering=False)

    x_d = nc.dram_tensor("x", [TOK, C], F32, kind="ExternalInput").ap()
    wqkT_d = nc.dram_tensor("wqkT", [H, C, 192], F32, kind="ExternalInput").ap()
    wpT_d = nc.dram_tensor("wpT", [C, C], F32, kind="ExternalInput").ap()
    bproj_d = nc.dram_tensor("bproj", [1, C], F32, kind="ExternalInput").ap()
    wg_d = nc.dram_tensor("wg", [II, C], F32, kind="ExternalInput").ap()
    wgT_d = nc.dram_tensor("wgT", [C, II], F32, kind="ExternalInput").ap()
    wu_d = nc.dram_tensor("wu", [II, C], F32, kind="ExternalInput").ap()
    wuT_d = nc.dram_tensor("wuT", [C, II], F32, kind="ExternalInput").ap()
    wd_d = nc.dram_tensor("wd", [C, II], F32, kind="ExternalInput").ap()
    wdT_d = nc.dram_tensor("wdT", [II, C], F32, kind="ExternalInput").ap()
    identb_d = nc.dram_tensor("identb_in", [128, 128], BF16, kind="ExternalInput").ap()
    identf_d = nc.dram_tensor("identf_in", [128, 128], F32, kind="ExternalInput").ap()
    maskneg_d = nc.dram_tensor("maskneg_in", [128, 128], F32, kind="ExternalInput").ap()
    ones1_d = nc.dram_tensor("ones1_in", [1, 128], BF16, kind="ExternalInput").ap()
    out_d = nc.dram_tensor("out", [TOK, C], F32, kind="ExternalOutput").ap()

    wqkb_d = nc.dram_tensor("wqkb", [H, C, 192], BF16).ap()
    wpb_d = nc.dram_tensor("wpb", [C, C], BF16).ap()
    srg_d = nc.dram_tensor("srg", [64, II], F32).ap()
    sru_d = nc.dram_tensor("sru", [64, II], F32).ap()
    srd_d = nc.dram_tensor("srd", [192, C], F32).ap()
    wgq_d = nc.dram_tensor("wgq", [C, II], BF16).ap()
    wuq_d = nc.dram_tensor("wuq", [C, II], BF16).ap()
    wdq_d = nc.dram_tensor("wdq", [II, C], BF16).ap()
    qk_sp = nc.dram_tensor("qk_sp", [SEQ * 8 * 2 * 128, T], BF16).ap()
    v_sp = nc.dram_tensor("v_sp", [SEQ * 128, 16 * C], BF16).ap()
    ao_sp = nc.dram_tensor("ao_sp", [TOK, C], F16).ap()
    g_sp = nc.dram_tensor("g_sp", [II, TOK], BF16).ap()
    u_sp = nc.dram_tensor("u_sp", [II, TOK], BF16).ap()
    pt_sp = nc.dram_tensor("pt_sp", [II, TOK], BF16).ap()
    pu_sp = nc.dram_tensor("pu_sp", [II, TOK], BF16).ap()
    cc_in = [nc.dram_tensor(f"cc_in{i}", [1, 16], F32).ap() for i in range(4)]
    cc_out = [nc.dram_tensor(f"cc_out{i}", [1, 16], F32).ap() for i in range(4)]
    sc_d = nc.dram_tensor("sc_d", [4, 16], F32).ap()

    al = nc.sbuf_tensor

    with contextlib.ExitStack() as _es1:
        xnT = _es1.enter_context(al("xnT", [128, CCH * TOK], BF16))
        xt0 = _es1.enter_context(al("xt0", [128, C], F32))
        xt1 = _es1.enter_context(al("xt1", [128, C], F32))
        stats = _es1.enter_context(al("stats", [128, 16], F32))
        macc = _es1.enter_context(al("macc", [128, 416], F32))
        wql0 = _es1.enter_context(al("wql0", [128, 1024], BF16))
        wql1 = _es1.enter_context(al("wql1", [128, 1024], BF16))
        gath = _es1.enter_context(al("gath", [1, 128], F32))
        scal = _es1.enter_context(al("scal", [1, 16], F32))
        scalg = _es1.enter_context(al("scalg", [1, 16], F32))
        sres = _es1.enter_context(al("sres", [1, 16], F32))
        identb = _es1.enter_context(al("identb_sb", [128, 128], BF16))
        identf = _es1.enter_context(al("identf_sb", [128, 128], F32))
        maskneg = _es1.enter_context(al("maskneg_sb", [128, 128], F32))
        ones1 = _es1.enter_context(al("ones1_sb", [1, 128], BF16))
        bproj = _es1.enter_context(al("bproj_sb", [1, C], F32))
        bprojb = _es1.enter_context(al("bprojb", [1, C], BF16))
        bc = _es1.enter_context(al("bc", [128, 8], F32))
        stg0 = _es1.enter_context(al("stg0", [128, 1024], BF16))
        stg1 = _es1.enter_context(al("stg1", [128, 1024], BF16))
        stgh0 = _es1.enter_context(al("stgh0", [128, 512], F16))
        stgh1 = _es1.enter_context(al("stgh1", [128, 512], F16))
        outs0 = _es1.enter_context(al("outs0", [128, 512], F32))
        outs1 = _es1.enter_context(al("outs1", [128, 512], F32))
        ps0 = _es1.enter_context(nc.psum_tensor("ps0", [128, 512], F32))
        ps1 = _es1.enter_context(nc.psum_tensor("ps1", [128, 512], F32))
        scps = _es1.enter_context(nc.psum_tensor("scps", [128, 2048], F32))
        ps2 = scps[:, 0:512]
        ps3 = scps[:, 512:1024]
        ps4 = scps[:, 1024:1536]
        ps5 = scps[:, 1536:2048]
        ps6 = _es1.enter_context(nc.psum_tensor("ps6", [128, 1024], BF16))
        ps7 = _es1.enter_context(nc.psum_tensor("ps7", [128, 512], F32))
        smT = _es1.enter_context(nc.semaphore("smT"))
        smV = _es1.enter_context(nc.semaphore("smV"))
        smS = _es1.enter_context(nc.semaphore("smS"))
        smG = _es1.enter_context(nc.semaphore("smG"))
        smCC = _es1.enter_context(nc.semaphore("smCC"))
        block = _es1.enter_context(nc.Block())
        ctx = Ctx(block, dict(T=smT, V=smV, S=smS, G=smG, CC=smCC),
                  nc=nc, es=_es1)
        for _snm in ("dP0", "dWL", "dWS", "dA0", "dA1", "dAS", "dB0", "dB1",
                     "dBS0", "dBS1", "dX0", "dX1", "dMG", "dCC", "dVW",
                     "dQW", "dVS0", "dVS1", "dPW", "dVB", "dQK", "dPS0",
                     "dPS1", "dFW0", "dFW1", "dFS0", "dFS1", "dG0", "dG1",
                     "dTS0", "dTS1", "dR0", "dR1", "dW50", "dW51", "dAO0",
                     "dAO1", "dOUT0", "dOUT1", "dTG0", "dTG1",
                     "dRX0", "dRX1", "dRX2", "dA2", "dB2", "dBS2",
                     "dQW1"):
            ctx.mksem(_snm)
        B = block
        sbanks = [ps2, ps3, ps4, ps5]

        def p0(sync):
            ctx.b16(sync.dma_start(out=identb[:, :], in_=identb_d), "dP0")
            ctx.b16(sync.dma_start(out=identf[:, :], in_=identf_d), "dP0")
            ctx.b16(sync.dma_start(out=maskneg[:, :], in_=maskneg_d), "dP0")
            ctx.b16(sync.dma_start(out=ones1[:, :], in_=ones1_d), "dP0")
            ctx.b16(sync.dma_start(out=bproj[:, :], in_=bproj_d), "dP0")
        B.sync(p0)

        def p0b(s):
            ctx.w(s, "dP0")
            ctx.b(s.activation(bprojb[:, :], bproj[:, :], AF.Copy,
                               bias=0.0, scale=1.0), "S")
        B.scalar(p0b)
        ctx.barrier()

        ev_bc = [0]

        def collective(idx, post_math, bslots):
            def go(g):
                ctx.w(g, "V")
                ctx.b16(g.dma_start(out=cc_in[idx], in_=scal[:, :]), "dCC")
                ctx.w(g, "dCC")
                ctx.b(g.collective_compute(
                    "AllReduce", ALU.max,
                    replica_groups=[list(range(NCORE))],
                    ins=[cc_in[idx]], outs=[cc_out[idx]]), "CC")
                ctx.w(g, "CC")
                ctx.b16(g.dma_start(out=scalg[:, :], in_=cc_out[idx]), "dCC")
            B.gpsimd(go)

            def pm(v):
                ctx.w(v, "dCC")
                post_math(v)
                ctx.b(v.drain(), "V")
            B.vector(pm)

            def bounce(g, idx=idx):
                ctx.w(g, "V")
                ctx.b16(g.dma_start(out=sc_d[idx:idx + 1, :],
                                    in_=sres[:, :]), "dCC")
                ctx.w(g, "dCC")
                for (bcol, cix) in bslots:
                    src = AP(sc_d.tensor, idx * 16 + cix, [[0, 128], [1, 1]])
                    ctx.b16(g.dma_start(
                        out=bc[:, bcol:bcol + 1], in_=src), "dCC")
                ctx.w(g, "dCC")
                g.sem_inc(ctx.sem["G"], 1)
                ctx.cnt["G"] += 1
                ev_bc[0] = ctx.cnt["G"]
            B.gpsimd(bounce)

        def c1_math(v):
            v.tensor_scalar(sres[0:1, 0:1], scalg[0:1, 0:1],
                            1.0 / Q15, 1e-12, ALU.mult, ALU.max)
            v.drain()
            v.reciprocal(sres[0:1, 1:2], sres[0:1, 0:1])

        with contextlib.ExitStack() as _es2:
            wsA0 = _es2.enter_context(al("wsA0", [128, II], F32))
            wsA1 = _es2.enter_context(al("wsA1", [128, II], F32))
            wsA2 = _es2.enter_context(al("wsA2", [128, II], F32))
            wsB0 = _es2.enter_context(al("wsB0", [128, 512], F32))
            wsB1 = _es2.enter_context(al("wsB1", [128, 512], F32))
            wsB2 = _es2.enter_context(al("wsB2", [128, 512], F32))
            sxp0 = _es2.enter_context(al("sxp0", [128, 512], F32))
            sxp1 = _es2.enter_context(al("sxp1", [128, 512], F32))
            sxp2 = _es2.enter_context(al("sxp2", [128, 512], F32))
            rxp0 = _es2.enter_context(al("rxp0", [128, 512], F32))
            rxp1 = _es2.enter_context(al("rxp1", [128, 512], F32))
            rxp2 = _es2.enter_context(al("rxp2", [128, 512], F32))
            srn = _es2.enter_context(al("srn", [128, 192], F32))
            srev = _es2.enter_context(al("srev", [128, 256], F32))
            wqb0 = _es2.enter_context(al("wqb0", [128, 512], BF16))
            wqb1 = _es2.enter_context(al("wqb1", [128, 512], BF16))
            wqb2 = _es2.enter_context(al("wqb2", [128, 512], BF16))
            pqb = _es2.enter_context(al("pqb", [128, 512], F32))
            wkst = _es2.enter_context(al("wkst", [128, 1536], F32))
            wkbf = _es2.enter_context(al("wkbf", [128, 1536], BF16))
            xsq = _es2.enter_context(al("xsq", [128, C], F32))
            xnf = _es2.enter_context(al("xnf", [128, C], F32))

            sched = []
            s_base = ctx.cnt["S"]
            s_off = [0]
            asbar = {}

            def cast_gen():
                specs = [("qkv", h) for h in range(H)] + \
                        [("wp", ccb) for ccb in range(CCH)]
                evW = [0]
                evSt = [0]
                prev_prom = [None]
                for (kind, idx) in specs:
                    def ld(g, kind=kind, idx=idx, pp=prev_prom[0]):
                        if pp is not None:
                            ctx.w(g, "S", pp)
                        if kind == "qkv":
                            src = AP(wqkT_d.tensor, idx * C * 192,
                                     [[192, 128], [192 * 128, 8], [1, 192]])
                            evW[0] = ctx.b16(
                                g.dma_start(out=wkst[:, :], in_=src), "dWL")
                        else:
                            evW[0] = ctx.b16(g.dma_start(
                                out=wkst[:, 0:1024],
                                in_=wpT_d[idx * 128:(idx + 1) * 128, :]),
                                "dWL")
                    B.gpsimd(ld)
                    myW = evW[0]
                    mySt = evSt[0]
                    s_off[0] += 1
                    prom = s_base + s_off[0]

                    def cast(s, kind=kind, myW=myW, mySt=mySt, prom=prom):
                        ctx.w(s, "dWL", myW)
                        ctx.w(s, "dWS", mySt)
                        if kind == "qkv":
                            qv = AP(wkst, 0, [[1536, 128], [192, 8], [1, 64]])
                            qo = AP(wkbf, 0, [[1536, 128], [192, 8], [1, 64]])
                            s.activation(qo, qv, AF.Copy, bias=0.0,
                                         scale=0.125)
                            kv = AP(wkst, 64,
                                    [[1536, 128], [192, 8], [1, 128]])
                            ko = AP(wkbf, 64,
                                    [[1536, 128], [192, 8], [1, 128]])
                            got = ctx.b(s.activation(ko, kv, AF.Copy,
                                                     bias=0.0, scale=1.0),
                                        "S")
                        else:
                            got = ctx.b(s.activation(
                                wkbf[:, 0:1024], wkst[:, 0:1024], AF.Copy,
                                bias=0.0, scale=1.0), "S")
                        assert got == prom
                    sched.append(cast)
                    prev_prom[0] = prom

                    def st(g, kind=kind, idx=idx, prom=prom):
                        ctx.w(g, "S", prom)
                        if kind == "qkv":
                            dst = AP(wqkb_d.tensor, idx * C * 192,
                                     [[192, 128], [192 * 128, 8], [1, 192]])
                            evSt[0] = ctx.b16(
                                g.dma_start(out=dst, in_=wkbf[:, :]), "dWS")
                        else:
                            evSt[0] = ctx.b16(g.dma_start(
                                out=wpb_d[idx * 128:(idx + 1) * 128, :],
                                in_=wkbf[:, 0:1024]), "dWS")
                    B.gpsimd(st)
                    yield

            def A_gen():
                specs = []
                for (nat_d, sr_d, nrc, icols, nblk, srstride) in (
                        (wg_d, srg_d, ICH, C, 32, II),
                        (wu_d, sru_d, ICH, C, 32, II),
                        (wd_d, srd_d, CCH, II, 96, C)):
                    for rc in range(nrc):
                        specs.append((nat_d, sr_d, rc, icols, nblk,
                                      srstride))
                nA = len(specs)
                wsl = [wsA0, wsA1, wsA2]
                evV, evT, evG = {}, {}, {}

                def emit_ld(g):
                    (nat_d, sr_d, rc, icols, nblk, srstride) = specs[g]
                    ws = wsl[g % 3]

                    def ld(sync, g=g, rc=rc, ws=ws, icols=icols,
                           nat_d=nat_d):
                        if g >= 3:
                            ctx.w(sync, "V", evV[g - 3])
                        evV[(g, "ld")] = ctx.b16(sync.dma_start(
                            out=ws[:, 0:icols],
                            in_=nat_d[rc * 128:(rc + 1) * 128, :]),
                            ("dA0", "dA1", "dA2")[g % 3])
                    B.sync(ld)

                def emit_rest(g):
                    (nat_d, sr_d, rc, icols, nblk, srstride) = specs[g]
                    ws = wsl[g % 3]

                    def red(v, g=g, ws=ws, nblk=nblk):
                        ctx.w(v, ("dA0", "dA1", "dA2")[g % 3],
                              evV[(g, "ld")])
                        if g >= 1:
                            ctx.w(v, "T", evT[g - 1])
                        src = AP(ws, 0, [[II, 128], [32, nblk], [1, 32]])
                        v.reduce_max(srn[:, 0:nblk], src, AX.X,
                                     apply_absolute_value=True)
                        v.drain()
                        v.tensor_scalar(srn[:, 0:nblk], srn[:, 0:nblk],
                                        1.0 / 7.0, 1e-12, ALU.mult, ALU.max)
                        v.drain()
                        v.reciprocal(srn[:, nblk:2 * nblk], srn[:, 0:nblk])
                        evV[g] = ctx.b(v.drain(), "V")
                    B.vector(red)

                    def tr(t, g=g, nblk=nblk):
                        ctx.w(t, "V", evV[g])
                        if g >= 1:
                            ctx.w(t, "V", evV[(g - 1, "e")])
                        if nblk <= 64:
                            ins = t.transpose(ps5[0:2 * nblk, 0:128],
                                              srn[:, 0:2 * nblk],
                                              identf[:, :])
                        else:
                            t.transpose(ps5[0:nblk, 0:128], srn[:, 0:nblk],
                                        identf[:, :])
                            ins = t.transpose(ps5[0:nblk, 128:256],
                                              srn[:, nblk:2 * nblk],
                                              identf[:, :])
                        evT[g] = ctx.b(ins, "T")
                    B.tensor(tr)

                    def vev(v, g=g, nblk=nblk):
                        ctx.w(v, "T", evT[g])
                        if g >= 1:
                            ctx.w(v, "dAS", evG[g - 1])
                        if nblk <= 64:
                            ins = v.tensor_copy(srev[0:2 * nblk, 0:128],
                                                ps5[0:2 * nblk, 0:128])
                        else:
                            v.tensor_copy(srev[0:nblk, 0:128],
                                          ps5[0:nblk, 0:128])
                            ins = v.tensor_copy(srev[0:nblk, 128:256],
                                                ps5[0:nblk, 128:256])
                        evV[(g, "e")] = ctx.b(ins, "V")
                    B.vector(vev)

                    def st(gp, g=g, rc=rc, nblk=nblk, srstride=srstride,
                           sr_d=sr_d):
                        ctx.w(gp, "V", evV[(g, "e")])
                        if nblk <= 64:
                            dst = AP(sr_d.tensor, rc * 128,
                                     [[srstride, 2 * nblk], [1, 128]])
                            evG[g] = ctx.b16(
                                gp.dma_start(out=dst,
                                             in_=srev[0:2 * nblk, 0:128]),
                                "dAS")
                        else:
                            d1 = AP(sr_d.tensor, rc * 128,
                                    [[srstride, nblk], [1, 128]])
                            ctx.b16(gp.dma_start(out=d1,
                                                 in_=srev[0:nblk, 0:128]),
                                    "dAS")
                            d2 = AP(sr_d.tensor, nblk * srstride + rc * 128,
                                    [[srstride, nblk], [1, 128]])
                            evG[g] = ctx.b16(
                                gp.dma_start(out=d2,
                                             in_=srev[0:nblk, 128:256]),
                                "dAS")
                    B.gpsimd(st)

                for g in range(nA + 2):
                    if g < nA:
                        emit_ld(g)
                    if g >= 2:
                        emit_rest(g - 2)
                        if g - 2 == 23:
                            asbar[0] = ctx.cnt["dAS"]
                        elif g - 2 == 47:
                            asbar[1] = ctx.cnt["dAS"]
                        elif g - 2 == 55:
                            asbar[2] = ctx.cnt["dAS"]
                    yield

            def B_gen():
                specs = []
                for w, (wT_src, sr_d, wq_dst, nrc, ocols, rstride) in \
                        enumerate(((wgT_d, srg_d, wgq_d, CCH, II, 32),
                                   (wuT_d, sru_d, wuq_d, CCH, II, 32),
                                   (wdT_d, srd_d, wdq_d, ICH, C, 96))):
                    for rc in range(nrc):
                        for it in range(ocols // 512):
                            specs.append((w, wT_src, sr_d, wq_dst, rc, it,
                                          ocols, rstride))
                nB = len(specs)
                wsl = [wsB0, wsB1, wsB2]
                sxl = [sxp0, sxp1, sxp2]
                rxl = [rxp0, rxp1, rxp2]
                wql = [wqb0, wqb1, wqb2]
                evV, evG = {}, {}

                def emit_ld(k):
                    (w, wT_src, sr_d, wq_dst, rc, it, ocols, rstride) = \
                        specs[k]
                    ws, sx, rx_ = wsl[k % 3], sxl[k % 3], rxl[k % 3]

                    def ld(sync, w=w, rc=rc, it=it, ws=ws, sx=sx,
                           k=k, ocols=ocols, sr_d=sr_d, wT_src=wT_src):
                        ctx.w(sync, "dAS", asbar[w])
                        if k >= 3:
                            ctx.w(sync, "V", evV[k - 3])
                        dsm = ("dB0", "dB1", "dB2")[k % 3]
                        ctx.b16(sync.dma_start(
                            out=ws[:, :],
                            in_=wT_src[rc * 128:(rc + 1) * 128,
                                       it * 512:(it + 1) * 512]), dsm)
                        soff = 4 * rc * ocols + it * 512
                        evV[(k, "ld")] = ctx.b16(
                            sync.dma_start(out=sx[:, :], in_=AP(
                                sr_d.tensor, soff,
                                [[ocols, 4], [0, 32], [1, 512]])), dsm)
                    B.sync(ld)

                    def ldr(g, w=w, rc=rc, it=it, rx_=rx_, k=k,
                            ocols=ocols, rstride=rstride, sr_d=sr_d):
                        ctx.w(g, "dAS", asbar[w])
                        if k >= 3:
                            ctx.w(g, "V", evV[k - 3])
                        soff = 4 * rc * ocols + it * 512
                        evV[(k, "ldr")] = ctx.b16(
                            g.dma_start(out=rx_[:, :], in_=AP(
                                sr_d.tensor, rstride * ocols + soff,
                                [[ocols, 4], [0, 32], [1, 512]])),
                            ("dRX0", "dRX1", "dRX2")[k % 3])
                    B.gpsimd(ldr)

                def emit_rest(k):
                    (w, wT_src, sr_d, wq_dst, rc, it, ocols, rstride) = \
                        specs[k]
                    ws, sx, rx_ = wsl[k % 3], sxl[k % 3], rxl[k % 3]
                    wq = wql[k % 3]

                    def qz(v, ws=ws, sx=sx, rx_=rx_, wq=wq, k=k):
                        ctx.w(v, ("dB0", "dB1", "dB2")[k % 3],
                              evV[(k, "ld")])
                        ctx.w(v, ("dRX0", "dRX1", "dRX2")[k % 3],
                              evV[(k, "ldr")])
                        if k >= 3:
                            ctx.w(v, ("dBS0", "dBS1", "dBS2")[(k - 3) % 3],
                                  evG[k - 3])
                        v.drain()
                        v.tensor_mul(pqb[:, :], ws[:, :], rx_[:, :])
                        v.drain()
                        v.tensor_single_scalar(pqb[:, :], pqb[:, :],
                                               MAGIC, ALU.add)
                        v.drain()
                        evV[k] = ctx.b(v.scalar_tensor_tensor(
                            wq[:, :], pqb[:, :], -MAGIC, sx[:, :],
                            ALU.add, ALU.mult), "V")
                    B.vector(qz)

                    def st(g, rc=rc, it=it, wq=wq, k=k, wq_dst=wq_dst):
                        ctx.w(g, "V", evV[k])
                        evG[k] = ctx.b16(g.dma_start(
                            out=wq_dst[rc * 128:(rc + 1) * 128,
                                       it * 512:(it + 1) * 512],
                            in_=wq[:, :]),
                            ("dBS0", "dBS1", "dBS2")[k % 3])
                    B.gpsimd(st)

                for k in range(nB + 2):
                    if k < nB:
                        emit_ld(k)
                    if k >= 2:
                        emit_rest(k - 2)
                    yield

            cg = cast_gen()
            ag = A_gen()
            bg = B_gen()

            ev_xn, ev_tp, ev_ev = {}, {}, {}
            for tt in range(NTT):
                xt = xt0 if tt % 2 == 0 else xt1

                def ld(sync, tt=tt, xt=xt):
                    if tt >= 2:
                        ctx.w(sync, "V", ev_xn[tt - 2])
                    ev_xn[(tt, "ld")] = ctx.b16(sync.dma_start(
                        out=xt[:, :], in_=x_d[tt * 128:(tt + 1) * 128, :]),
                        "dX0" if tt % 2 == 0 else "dX1")
                B.sync(ld)

                def st1(v, tt=tt, xt=xt):
                    ctx.w(v, "dX0" if tt % 2 == 0 else "dX1",
                          ev_xn[(tt, "ld")])
                    v.tensor_mul(xsq[:, :], xt[:, :], xt[:, :])
                    v.drain()
                    v.reduce_sum(stats[:, 0:1], xt[:, :], AX.X)
                    v.reduce_sum(stats[:, 1:2], xsq[:, :], AX.X)
                    v.drain()
                    v.tensor_scalar(stats[:, 2:3], stats[:, 0:1],
                                    1.0 / C, None, ALU.mult)
                    v.drain()
                    v.tensor_mul(stats[:, 3:4], stats[:, 2:3], stats[:, 2:3])
                    v.drain()
                    v.scalar_tensor_tensor(stats[:, 4:5], stats[:, 1:2],
                                           1.0 / C, stats[:, 3:4],
                                           ALU.mult, ALU.subtract)
                    v.drain()
                    v.tensor_single_scalar(stats[:, 4:5], stats[:, 4:5],
                                           1e-5, ALU.add)
                    v.drain()
                    v.reciprocal(stats[:, 5:6], stats[:, 4:5])
                    ctx.b(v.drain(), "V")
                B.vector(st1)

                vcap = ctx.cnt["V"]
                s_off[0] += 1
                prom = s_base + s_off[0]

                def sq(s, vcap=vcap, prom=prom):
                    s.wait_ge(ctx.sem["V"], vcap)
                    got = ctx.b(s.activation(stats[:, 6:7], stats[:, 5:6],
                                             AF.Sqrt, bias=0.0, scale=1.0),
                                "S")
                    assert got == prom
                sched.append(sq)

                def st3(v, prom=prom):
                    ctx.w(v, "S", prom)
                    v.tensor_scalar(stats[:, 7:8], stats[:, 2:3], -1.0, None,
                                    ALU.mult)
                    ctx.b(v.drain(), "V")
                B.vector(st3)

                def xn_act(v, tt=tt, xt=xt):
                    if tt >= 1:
                        ctx.w(v, "T", ev_tp[tt - 1])
                    ev_xn[tt] = ctx.b(v.tensor_scalar(
                        xnf[:, :], xt[:, :], stats[:, 7:8], stats[:, 6:7],
                        ALU.add, ALU.mult), "V")
                B.vector(xn_act)

                def m1p(v, tt=tt):
                    v.drain()
                    ctx.b(v.reduce_max(macc[:, tt:tt + 1], xnf[:, :], AX.X,
                                       apply_absolute_value=True), "V")
                B.vector(m1p)

                next(ag, None)

                def tp(t, tt=tt):
                    ctx.w(t, "V", ev_xn[tt])
                    if tt >= 1:
                        ctx.w(t, "V", ev_ev[tt - 1])
                    for bk in range(2):
                        pb = ps3 if bk == 0 else ps4
                        for kx in range(4):
                            ci = 4 * bk + kx
                            ins = t.transpose(pb[:, kx * 128:(kx + 1) * 128],
                                              xnf[:, ci * 128:(ci + 1) * 128],
                                              identf[:, :])
                    ev_tp[tt] = ctx.b(ins, "T")
                B.tensor(tp)

                def ev(v, tt=tt):
                    ctx.w(v, "T", ev_tp[tt])
                    for bk in range(2):
                        dst = AP(xnT, (4 * bk) * TOK + tt * 128,
                                 [[CCH * TOK, 128], [TOK, 4], [1, 128]])
                        ins = v.tensor_copy(dst, AP(
                            scps, 512 + bk * 512,
                            [[2048, 128], [128, 4], [1, 128]]))
                    ev_ev[tt] = ctx.b(ins, "V")
                B.vector(ev)

                next(ag, None)
                next(cg, None)

            for _ in cg:
                pass
            for _ in ag:
                pass

            def ln_sbody(s):
                for fn in sched:
                    fn(s)
            B.scalar(ln_sbody)

            def m1fin(v):
                v.drain()
                v.reduce_max(stats[:, 8:9], macc[:, 0:NTT], AX.X)
                ctx.b(v.drain(), "V")
            B.vector(m1fin)

            def m1g(g):
                ctx.w(g, "V")
                ctx.b16(g.dma_start(out=gath[:, :], in_=stats[:, 8:9]), "dMG")
            B.gpsimd(m1g)

            def m1f2(v):
                ctx.w(v, "dMG")
                v.memset(scal[:, :], 0.0)
                v.memset(sres[:, :], 0.0)
                v.drain()
                v.reduce_max(scal[0:1, 0:1], gath[0:1, :], AX.X)
                ctx.b(v.drain(), "V")
            B.vector(m1f2)
            ctx.barrier()

            collective(0, c1_math, [(0, 1)])

            with contextlib.ExitStack() as _es3:
                wvw = _es3.enter_context(al("wvw", [128, 2 * CCH * 512], BF16))
                wqkp0 = _es3.enter_context(al("wqkp0", [128, CCH * 256], BF16))
                wqkp1 = _es3.enter_context(al("wqkp1", [128, CCH * 256], BF16))
                qkev = _es3.enter_context(al("qkev", [128, 512], BF16))
                qkev2 = _es3.enter_context(al("qkev2", [128, 512], BF16))
                qkev3 = _es3.enter_context(al("qkev3", [128, 512], BF16))
                qkev4 = _es3.enter_context(al("qkev4", [128, 512], BF16))

                def ldv(sync):
                    for cc in range(CCH):
                        for vg in range(2):
                            src = AP(wqkb_d.tensor,
                                     (8 * vg) * C * 192 + (cc * 128) * 192 + 128,
                                     [[192, 128], [C * 192, 8], [1, 64]])
                            ctx.b16(sync.dma_start(
                                out=wvw[:, (cc * 2 + vg) * 512:(cc * 2 + vg + 1) * 512],
                                in_=src), "dVW")
                B.sync(ldv)

                ev_vev, ev_vst = {}, {}
                k = 0
                for sq in range(SEQ):
                    for vg in range(2):
                        for tt in range(16):
                            pb = ps0 if k % 2 == 0 else ps1

                            def mm(t, sq=sq, vg=vg, tt=tt, pb=pb, k=k):
                                ctx.w(t, "dVW", 256)
                                if k >= 2:
                                    ctx.w(t, "V", ev_vev[k - 2])
                                gt = sq * 16 + tt
                                for cc in range(CCH):
                                    ins = t.matmul(
                                        pb[:, :],
                                        xnT[:, cc * TOK + gt * 128:cc * TOK + (gt + 1) * 128],
                                        wvw[:, (cc * 2 + vg) * 512:(cc * 2 + vg + 1) * 512],
                                        start=(cc == 0), stop=(cc == CCH - 1))
                                ctx.b(ins, "T")
                            B.tensor(mm)

                            def ev(v, pb=pb, k=k):
                                ctx.w(v, "T")
                                qv = qkev if k % 2 == 0 else qkev2
                                if k >= 2:
                                    ctx.w(v, "dVS0" if k % 2 == 0 else "dVS1",
                                          ev_vst[k - 2])
                                ev_vev[k] = ctx.b(v.tensor_copy(qv[:, :], pb[:, :]),
                                                  "V")
                            B.vector(ev)

                            def st(g, sq=sq, vg=vg, tt=tt, k=k):
                                ctx.w(g, "V", ev_vev[k])
                                qv = qkev if k % 2 == 0 else qkev2
                                dst = AP(v_sp.tensor,
                                         sq * 128 * (16 * C) + tt * 1024 + vg * 512,
                                         [[16 * C, 128], [1, 512]])
                                ev_vst[k] = ctx.b16(
                                    g.dma_start(out=dst, in_=qv[:, :]),
                                    "dVS0" if k % 2 == 0 else "dVS1")
                            B.gpsimd(st)
                            next(bg, None)
                            k += 1

                ev_qkev, ev_qkst, ev_qk3, ev_qkT = {}, {}, {}, {}

                def ldqk(sync, spr):
                    sq, pr = divmod(spr, 8)
                    wqkp = wqkp0 if spr % 2 == 0 else wqkp1
                    if spr >= 2:
                        ctx.w(sync, "T", ev_qkT[spr * 4 - 5])
                    elif spr == 0:
                        ctx.w(sync, "T")
                    dsm = "dQW" if spr % 2 == 0 else "dQW1"
                    for cc in range(CCH):
                        src_ = AP(wqkb_d.tensor,
                                  (2 * pr) * C * 192 + (cc * 128) * 192,
                                  [[192, 128], [C * 192, 2], [1, 128]])
                        ctx.b16(sync.dma_start(
                            out=wqkp[:, cc * 256:(cc + 1) * 256],
                            in_=src_), dsm)
                    ev_qkev[(spr, "ld")] = ctx.cnt[dsm]

                kk = 0
                for sq in range(SEQ):
                    for pr in range(8):
                        spr = sq * 8 + pr
                        if spr == 0:
                            B.sync(lambda sync: ldqk(sync, 0))
                        if spr + 1 < 16:
                            B.sync(lambda sync, spr=spr: ldqk(sync, spr + 1))
                        next(bg, None)
                        wqkp = wqkp0 if spr % 2 == 0 else wqkp1

                        for tr in range(4):
                            if kk % 2 == 0:
                                bq0, bq1, bk0 = ps0, ps2, ps1
                                qa, qb = qkev, qkev2
                            else:
                                bq0, bq1, bk0 = ps4, ps5, ps7
                                qa, qb = qkev3, qkev4

                            def mm(t, sq=sq, spr=spr, tr=tr, kk=kk,
                                   wqkp=wqkp, bq0=bq0, bq1=bq1, bk0=bk0):
                                ctx.w(t, "dQW" if spr % 2 == 0 else "dQW1",
                                      ev_qkev[(spr, "ld")])
                                if kk >= 1:
                                    ctx.w(t, "V", ev_qk3[kk - 1])
                                else:
                                    ctx.w(t, "V")
                                if kk >= 2:
                                    ctx.w(t, "V", ev_qkev[kk - 2])
                                gt = sq * 4 + tr
                                for cc in range(CCH):
                                    rhs = xnT[:, cc * TOK + gt * 512:cc * TOK + (gt + 1) * 512]
                                    s_ = (cc == 0)
                                    p_ = (cc == CCH - 1)
                                    t.matmul(bq0[0:64, :],
                                             wqkp[:, cc * 256:cc * 256 + 64],
                                             rhs, start=s_, stop=p_,
                                             tile_position=(0, 0))
                                    t.matmul(bq1[64:128, :],
                                             wqkp[:, cc * 256 + 128:cc * 256 + 192],
                                             rhs, start=s_, stop=p_,
                                             tile_position=(0, 64))
                                    t.matmul(bk0[0:64, :],
                                             wqkp[:, cc * 256 + 64:cc * 256 + 128],
                                             rhs, start=s_, stop=p_,
                                             tile_position=(0, 0))
                                    ins = t.matmul(ps3[64:128, :],
                                                   wqkp[:, cc * 256 + 192:cc * 256 + 256],
                                                   rhs, start=s_, stop=p_,
                                                   tile_position=(0, 64))
                                ev_qkT[kk] = ctx.b(ins, "T")
                            B.tensor(mm)

                            def ev(v, kk=kk, bq0=bq0, bq1=bq1, bk0=bk0,
                                   qa=qa, qb=qb):
                                ctx.w(v, "T")
                                if kk >= 2:
                                    ctx.w(v, "dVS0" if kk % 2 == 0 else "dVS1",
                                          ev_qkst[kk - 2])
                                else:
                                    ctx.w(v, "dVS0")
                                    ctx.w(v, "dVS1")
                                ev_qk3[kk] = ctx.b(
                                    v.tensor_copy(qb[64:128, :],
                                                  ps3[64:128, :]), "V")
                                v.tensor_copy(qa[0:64, :], bq0[0:64, :])
                                v.tensor_copy(qa[64:128, :], bq1[64:128, :])
                                ev_qkev[kk] = ctx.b(
                                    v.tensor_copy(qb[0:64, :],
                                                  bk0[0:64, :]), "V")
                            B.vector(ev)

                            def st(g, sq=sq, pr=pr, tr=tr, kk=kk,
                                   qa=qa, qb=qb):
                                ctx.w(g, "V", ev_qkev[kk])
                                base = ((sq * 8 + pr) * 2) * 128 * T
                                dq = AP(qk_sp.tensor, base + tr * 512,
                                        [[T, 128], [1, 512]])
                                dk = AP(qk_sp.tensor, base + 128 * T + tr * 512,
                                        [[T, 128], [1, 512]])
                                dsm = "dVS0" if kk % 2 == 0 else "dVS1"
                                ctx.b16(g.dma_start(out=dq, in_=qa[:, :]), dsm)
                                ev_qkst[kk] = ctx.b16(
                                    g.dma_start(out=dk, in_=qb[:, :]), dsm)
                            B.gpsimd(st)
                            next(bg, None)
                            kk += 1

                for _ in bg:
                    pass
                ctx.barrier()

        _esP4b = contextlib.ExitStack()
        ub0 = _esP4b.enter_context(al("ub0", [128, C], F32))
        ub1 = _esP4b.enter_context(al("ub1", [128, C], F32))
        ev_i2 = {}
        rx_lams = []
        RX0 = ctx.cnt["S"]
        for j in range(NTT):
            ub = ub0 if j % 2 == 0 else ub1

            def mk_i1(j=j, ub=ub, vprev=ev_i2.get(j - 2, 0)):
                def f(s):
                    if j == 0:
                        ctx.w(s, "G", ev_bc[0])
                    if vprev > 0:
                        s.wait_ge(ctx.sem["V"], vprev)
                    got = ctx.b(s.activation(
                        ub[:, :], xnT[:, j * 1024:(j + 1) * 1024],
                        AF.Copy, bias=MAGIC, scale=bc[:, 0:1]), "S")
                    assert got == RX0 + j + 1
                return f
            rx_lams.append(mk_i1())

            def i2(v, j=j, ub=ub):
                ctx.w(v, "S", RX0 + j + 1)
                ev_i2[j] = ctx.b(v.tensor_single_scalar(
                    xnT[:, j * 1024:(j + 1) * 1024], ub[:, :],
                    -MAGIC, ALU.add), "V")
            B.vector(i2)

        def rx_sbody(s):
            for f in rx_lams:
                f(s)
        B.scalar(rx_sbody)
        ctx.barrier()
        _esP4b.close()

        ffn_state = dict(ke=0, km=0)
        ffn_ev_mm, ffn_ev_ev2, ffn_ev_st2 = {}, {}, {}
        nU = 16 * ICH

        def ffn_pump_ev():
            k = ffn_state["ke"]
            if k >= ffn_state["km"] or k >= nU:
                return
            ffn_state["ke"] += 1
            gw, t8 = divmod(k, 8)
            sp_dst = g_sp if gw < ICH else u_sp
            ic = gw % ICH
            mcol = 32 + k if k < 8 * ICH else 224 + (k - 8 * ICH)

            def ev2(v, gw=gw, t8=t8, k=k, mcol=mcol):
                ctx.w(v, "T", ffn_ev_mm[(gw, t8)])
                sg = stg0 if k % 2 == 0 else stg1
                if k >= 2:
                    ctx.w(v, "dFS0" if k % 2 == 0 else "dFS1",
                          ffn_ev_st2[k - 2])
                v.tensor_copy(sg[:, 0:512], ps0[:, :])
                ffn_ev_ev2[k] = ctx.b(v.reduce_max(
                    macc[:, mcol:mcol + 1],
                    ps0[:, :], AX.X,
                    apply_absolute_value=True), "V")
            B.vector(ev2)

            def st2(g, ic=ic, t8=t8, k=k, sp_dst=sp_dst):
                ctx.w(g, "V", ffn_ev_ev2[k])
                sg = stg0 if k % 2 == 0 else stg1
                dst = AP(sp_dst.tensor,
                         ic * 128 * TOK + t8 * 512,
                         [[TOK, 128], [1, 512]])
                ffn_ev_st2[k] = ctx.b16(
                    g.dma_start(out=dst, in_=sg[:, 0:512]),
                    "dFS0" if k % 2 == 0 else "dFS1")
            B.gpsimd(st2)

        def ffn_pump_mm():
            k = ffn_state["km"]
            if k >= nU or ffn_state["ke"] < k:
                return
            ffn_state["km"] += 1
            gw, t8 = divmod(k, 8)
            wq_src = wgq_d if gw < ICH else wuq_d
            ic = gw % ICH
            wql = wql0 if gw % 2 == 0 else wql1

            if t8 == 0:
                def ldw(sync, gw=gw, ic=ic, wql=wql, wq_src=wq_src):
                    if gw >= 2:
                        ctx.w(sync, "T", ffn_ev_mm[(gw - 2, 7)])
                    dsm = "dFW0" if gw % 2 == 0 else "dFW1"
                    for cc in range(CCH):
                        ctx.b16(sync.dma_start(
                            out=wql[:, cc * 128:(cc + 1) * 128],
                            in_=wq_src[cc * 128:(cc + 1) * 128,
                                       ic * 128:(ic + 1) * 128]), dsm)
                    ffn_ev_mm[(gw, "ld")] = ctx.cnt[dsm]
                B.sync(ldw)

            def mm(t, gw=gw, t8=t8, k=k, wql=wql):
                ctx.w(t, "dFW0" if gw % 2 == 0 else "dFW1",
                      ffn_ev_mm[(gw, "ld")])
                if k >= 1:
                    ctx.w(t, "V", ffn_ev_ev2[k - 1])
                for cc in range(CCH):
                    ins = t.matmul(
                        ps0[:, :], wql[:, cc * 128:(cc + 1) * 128],
                        xnT[:, cc * TOK + t8 * 512:cc * TOK + (t8 + 1) * 512],
                        start=(cc == 0), stop=(cc == CCH - 1))
                ffn_ev_mm[(gw, t8)] = ctx.b(ins, "T")
            B.tensor(mm)


        with contextlib.ExitStack() as _es4:
            v_sb = _es4.enter_context(al("v_sb", [128, 16 * C], BF16))
            atT = _es4.enter_context(al("atT", [128, 8 * T], BF16))
            qp = _es4.enter_context(al("qp", [128, T], BF16))
            kp = _es4.enter_context(al("kp", [128, T], BF16))
            Pa0 = _es4.enter_context(al("Pa0", [128, T], BF16))
            Pb0 = _es4.enter_context(al("Pb0", [128, T], BF16))
            Pa1 = _es4.enter_context(al("Pa1", [128, T], BF16))
            Pb1 = _es4.enter_context(al("Pb1", [128, T], BF16))
            PT0 = _es4.enter_context(al("PT0", [128, 1024], BF16))
            PT1 = _es4.enter_context(al("PT1", [128, 1024], BF16))
            cmx = _es4.enter_context(al("cmx", [128, 32], F32))
            dnp = _es4.enter_context(al("dnp", [128, 32], F32))
            wp_sb = _es4.enter_context(al("wp_sb", [128, CCH * 1024], BF16))
            def ld_wp(sync):
                for ccb in range(CCH):
                    ctx.b16(sync.dma_start(
                        out=wp_sb[:, ccb * 1024:(ccb + 1) * 1024],
                        in_=wpb_d[ccb * 128:(ccb + 1) * 128, :]), "dPW")
            B.sync(ld_wp)

            for sq in range(SEQ):
                def ldv2(sync, sq=sq):
                    ctx.w(sync, "T")
                    src = AP(v_sp.tensor, sq * 128 * 16 * C,
                             [[16 * C, 128], [1, 16 * C]])
                    ctx.b16(sync.dma_start(out=v_sb[:, :], in_=src), "dVB")
                B.sync(ldv2)
                ev_vb = ctx.cnt["dVB"]

                for pr in range(8):
                    def ldqk2(sync, sq=sq, pr=pr):
                        ctx.w(sync, "T")
                        base = ((sq * 8 + pr) * 2) * 128 * T
                        ctx.b16(sync.dma_start(
                            out=qp[:, :],
                            in_=AP(qk_sp.tensor, base,
                                   [[T, 128], [1, T]])), "dQK")
                        ctx.b16(sync.dma_start(
                            out=kp[:, :],
                            in_=AP(qk_sp.tensor, base + 128 * T,
                                   [[T, 128], [1, T]])), "dQK")
                    B.sync(ldqk2)
                    ev_qk2 = ctx.cnt["dQK"]

                    S0 = ctx.cnt["S"]

                    def sprom(qt, hh):
                        return S0 + 2 * qt + hh + 1 if qt >= 0 else S0

                    s_lams = []
                    ev_nrm = {}
                    ev_tp = {}
                    nrm_pend = []
                    for qt in range(17):
                      ffn_pump_ev()
                      if qt < 16:
                        nsr = qt // 4 + 1
                        last_n = (qt % 4 + 1) * 128
                        for hh in range(2):
                            b0 = 64 * hh
                            Ph = (Pa0 if hh == 0 else Pb0) if qt % 2 == 0 \
                                else (Pa1 if hh == 0 else Pb1)
                            slot = (qt % 2) * 2 + hh
                            if nsr <= 2:
                                scbase = 0 if hh == 0 else 1024
                                bankfree = sprom(qt - 1, hh)
                            else:
                                scbase = 0
                                bankfree = sprom(qt, hh - 1) if hh == 1 \
                                    else sprom(qt - 1, 1)
                            qw = (qt + 1) * 128

                            def mm(t, b0=b0, qt=qt, nsr=nsr, last_n=last_n,
                                   ev_qk2=ev_qk2, scbase=scbase,
                                   bankfree=bankfree):
                                ctx.w(t, "dQK", ev_qk2)
                                ctx.w(t, "S", bankfree)
                                for sr in range(nsr):
                                    n = 512 if sr < nsr - 1 else last_n
                                    ins = t.matmul(
                                        scps[:, scbase + sr * 512:scbase + sr * 512 + n],
                                        qp[b0:b0 + 64, qt * 128:(qt + 1) * 128],
                                        kp[b0:b0 + 64, sr * 512:sr * 512 + n],
                                        start=True, stop=True)
                                ctx.b(ins, "T")
                            B.tensor(mm)

                            def msk(v, qt=qt, hh=hh, qw=qw, scbase=scbase,
                                    slot=slot):
                                ctx.w(v, "T")
                                ctx.w(v, "S", sprom(qt - 2, hh))
                                v.tensor_add(scps[:, scbase + qw - 128:scbase + qw],
                                             scps[:, scbase + qw - 128:scbase + qw],
                                             maskneg[:, :])
                                v.drain()
                                ctx.b(v.reduce_max(
                                    cmx[:, slot * 8 + 6:slot * 8 + 7],
                                    scps[:, scbase:scbase + qw],
                                    AX.X, negate=True), "V")
                            B.vector(msk)

                            vev_cap = ctx.cnt["V"]
                            tev_cap = ev_tp.get(qt - 2, 0)

                            def ex(s, qt=qt, hh=hh, qw=qw, Ph=Ph, slot=slot,
                                   scbase=scbase, vev=vev_cap, tev=tev_cap):
                                if vev > 0:
                                    s.wait_ge(ctx.sem["V"], vev)
                                if tev > 0:
                                    s.wait_ge(ctx.sem["T"], tev)
                                ins = s.activation(
                                    Ph[:, 0:qw],
                                    scps[:, scbase:scbase + qw], AF.Exp,
                                    bias=cmx[:, slot * 8 + 6:slot * 8 + 7],
                                    scale=1.0,
                                    accum_out=dnp[:, slot * 8:slot * 8 + 1])
                                got = ctx.b(ins, "S")
                                assert got == sprom(qt, hh), (got, qt, hh)
                            s_lams.append(ex)

                            nrm_pend.append((qt, Ph, hh, slot))

                      ffn_pump_mm()

                      if qt >= 1:
                        while nrm_pend and nrm_pend[0][0] == qt - 1:
                            (nqt, nPh, nhh, nslot) = nrm_pend.pop(0)

                            def nrm(v, qt=nqt, Ph=nPh, hh=nhh, slot=nslot):
                                ctx.w(v, "S", sprom(qt, hh))
                                v.reciprocal(dnp[:, slot * 8 + 7:slot * 8 + 8],
                                             dnp[:, slot * 8:slot * 8 + 1])
                                v.drain()
                                ev_nrm[(qt, hh)] = ctx.b(v.tensor_scalar(
                                    Ph[:, 0:(qt + 1) * 128],
                                    Ph[:, 0:(qt + 1) * 128],
                                    dnp[:, slot * 8 + 7:slot * 8 + 8],
                                    None, ALU.mult), "V")
                            B.vector(nrm)
                        pqt = qt - 1
                        nbat = (pqt + 4) // 4
                        PhA = Pa0 if pqt % 2 == 0 else Pa1
                        PhB = Pb0 if pqt % 2 == 0 else Pb1
                        for bt in range(nbat):
                            scs = list(range(bt * 4, min(pqt + 1, bt * 4 + 4)))
                            PTb = PT0 if bt % 2 == 0 else PT1

                            def tp(t, pqt=pqt, scs=scs, PhA=PhA, PhB=PhB):
                                ctx.w(t, "V")
                                for i, sc in enumerate(scs):
                                    for hh in range(2):
                                        Ph = PhA if hh == 0 else PhB
                                        ins = t.transpose(
                                            ps6[:, (i * 2 + hh) * 128:(i * 2 + hh + 1) * 128],
                                            Ph[:, sc * 128:(sc + 1) * 128],
                                            identb[:, :])
                                ev_tp[pqt] = ctx.b(ins, "T")
                            B.tensor(tp)

                            def pev(v, PTb=PTb, scs=scs):
                                ctx.w(v, "T")
                                n = len(scs) * 256
                                ctx.b(v.tensor_copy(PTb[:, 0:n], ps6[:, 0:n]),
                                      "V")
                            B.vector(pev)

                            def av(t, pr=pr, pqt=pqt, PTb=PTb, scs=scs,
                                   ev_vb=ev_vb, S0=S0):
                                ctx.w(t, "dVB", ev_vb)
                                if pqt == 0 and pr == 0:
                                    ctx.w(t, "S", S0)
                                ctx.w(t, "V")
                                reg = (pqt % 4) * 128
                                for i, sc in enumerate(scs):
                                    for hh in range(2):
                                        h = 2 * pr + hh
                                        avb = ps7 if hh == 0 else ps1
                                        vcol = sc * 1024 + (h // 8) * 512 + (h % 8) * 64
                                        ins = t.matmul(
                                            avb[64 * hh:64 * hh + 64,
                                                reg:reg + 128],
                                            v_sb[:, vcol:vcol + 64],
                                            PTb[:, (i * 2 + hh) * 128:(i * 2 + hh) * 128 + 128],
                                            start=(sc == 0), stop=(sc == pqt),
                                            tile_position=(0, 64 * hh))
                                ev_tp[pqt] = ctx.b(ins, "T")
                            B.tensor(av)

                        if pqt % 4 == 3:
                            qr = pqt // 4

                            def aev(v, pr=pr, qr=qr):
                                ctx.w(v, "T")
                                v.tensor_copy(
                                    atT[0:64, pr * T + qr * 512:pr * T + qr * 512 + 512],
                                    ps7[0:64, :])
                                ctx.b(v.tensor_copy(
                                    atT[64:128, pr * T + qr * 512:pr * T + qr * 512 + 512],
                                    ps1[64:128, :]), "V")
                            B.vector(aev)


                    def sbody(s, s_lams=s_lams):
                        for f in s_lams:
                            f(s)
                    B.scalar(sbody)
                    assert ctx.cnt["S"] == S0 + 32

                ffn_pump_ev()

                ev_pst = {}
                kk = 0
                prj_lams = []
                PS0 = ctx.cnt["S"]
                for tt in range(16):
                    for ch in range(2):
                        pb = ps0 if kk % 2 == 0 else ps1

                        def mm(t, sq=sq, tt=tt, ch=ch, pb=pb, kk=kk):
                            ctx.w(t, "dPW", 128)
                            ctx.w(t, "V")
                            if kk >= 2:
                                ctx.w(t, "S", PS0 + kk - 1)
                            t.matmul(pb[:, :], ones1[0:1, :],
                                     bprojb[0:1, ch * 512:(ch + 1) * 512],
                                     start=True, stop=False)
                            for pr in range(8):
                                ins = t.matmul(
                                    pb[:, :],
                                    atT[:, pr * T + tt * 128:pr * T + (tt + 1) * 128],
                                    wp_sb[:, pr * 1024 + ch * 512:pr * 1024 + ch * 512 + 512],
                                    start=False, stop=(pr == 7))
                            ctx.b(ins, "T")
                        B.tensor(mm)

                        def mk_pev(kk=kk, pb=pb,
                                   tcap=None, pstprev=None):
                            def f(s, tcap=ctx.cnt["T"],
                                  pstprev=ev_pst.get(
                                      kk - 2,
                                      ctx.cnt["dPS0" if kk % 2 == 0
                                              else "dPS1"])):
                                s.wait_ge(ctx.sem["T"], tcap)
                                sh = stgh0 if kk % 2 == 0 else stgh1
                                ctx.w(s, "dPS0" if kk % 2 == 0 else "dPS1",
                                      pstprev)
                                got = ctx.b(s.activation(
                                    sh[:, :], pb[:, :], AF.Copy,
                                    bias=0.0, scale=1.0), "S")
                                assert got == PS0 + kk + 1
                            return f
                        prj_lams.append(mk_pev())

                        def pst(g, sq=sq, tt=tt, ch=ch, kk=kk):
                            ctx.w(g, "S", PS0 + kk + 1)
                            sh = stgh0 if kk % 2 == 0 else stgh1
                            dst = AP(ao_sp.tensor,
                                     (sq * 16 + tt) * 128 * C + ch * 512,
                                     [[C, 128], [1, 512]])
                            ev_pst[kk] = ctx.b16(
                                g.dma_start(out=dst, in_=sh[:, :]),
                                "dPS0" if kk % 2 == 0 else "dPS1")
                        B.gpsimd(pst)
                        kk += 1

                def prj_sbody(s, prj_lams=prj_lams):
                    for f in prj_lams:
                        f(s)
                B.scalar(prj_sbody)
            ctx.barrier()

        with contextlib.ExitStack() as _es5:
            ub2 = _es5.enter_context(al("ub2", [128, C], F32))
            ub6 = _es5.enter_context(al("ub6", [128, C], F32))
            ub7 = _es5.enter_context(al("ub7", [128, C], F32))
            stgc0 = _es5.enter_context(al("stgc0", [128, 1024], BF16))
            stgc1 = _es5.enter_context(al("stgc1", [128, 1024], BF16))
            ub3 = _es5.enter_context(al("ub3", [128, C], F32))
            ub4 = _es5.enter_context(al("ub4", [128, C], F32))
            ub5 = _es5.enter_context(al("ub5", [128, C], F32))
            ldt0 = _es5.enter_context(al("ldt0", [128, 1024], BF16))
            ldt1 = _es5.enter_context(al("ldt1", [128, 1024], BF16))
            ldt2 = _es5.enter_context(al("ldt2", [128, 1024], BF16))
            ldt3 = _es5.enter_context(al("ldt3", [128, 1024], BF16))
            rob = _es5.enter_context(al("rob", [128, ICH * 512], BF16))
            rob2 = _es5.enter_context(al("rob2", [128, ICH * 512], BF16))
            def mfin(col0, ncols, slot, clear):
                def f1(v):
                    v.drain()
                    v.reduce_max(stats[:, 9:10], macc[:, col0:col0 + ncols],
                                 AX.X)
                    ctx.b(v.drain(), "V")
                B.vector(f1)

                def f2(g):
                    ctx.w(g, "V")
                    ctx.b16(g.dma_start(out=gath[:, :], in_=stats[:, 9:10]),
                            "dMG")
                B.gpsimd(f2)

                def f3(v, slot=slot, clear=clear):
                    ctx.w(v, "dMG")
                    if clear:
                        v.memset(scal[:, :], 0.0)
                        v.drain()
                    v.reduce_max(scal[0:1, slot:slot + 1], gath[0:1, :], AX.X)
                    ctx.b(v.drain(), "V")
                B.vector(f3)

            ffn_pump_ev()
            mfin(32, 192, 0, True)
            ctx.barrier()

            def c2_math(v):
                v.tensor_mul(sres[0:1, 10:11], sres[0:1, 0:1], scalg[0:1, 0:1])
                v.drain()
                v.tensor_scalar(sres[0:1, 2:3], sres[0:1, 10:11],
                                1.0 / Q15, 1e-12, ALU.mult, ALU.max)
                v.drain()
                v.reciprocal(sres[0:1, 11:12], sres[0:1, 2:3])
                v.drain()
                v.tensor_mul(sres[0:1, 3:4], sres[0:1, 0:1], sres[0:1, 11:12])

            collective(1, c2_math, [(1, 3), (2, 2)])

            ntile = ICH * 4
            ev_ld3, ev_v1, ev_v2, ev_st3 = {}, {}, {}, {}
            sig_lams = []
            S8 = ctx.cnt["S"]
            for k in range(ntile):
                ffn_pump_ev()
                ffn_pump_mm()
                ic, tc = divmod(k, 4)
                gt_ = ldt0 if k % 2 == 0 else ldt1
                uu = ub2 if k % 2 == 0 else ub3
                rg_ = ub4 if k % 2 == 0 else ub5
                ss = ub6 if k % 2 == 0 else ub7

                def ld3(sync, ic=ic, tc=tc, gt_=gt_, k=k):
                    if k >= 2:
                        ctx.w(sync, "V", ev_v2.get(k - 2, 0))
                    dsm = "dG0" if k % 2 == 0 else "dG1"
                    ctx.b16(sync.dma_start(
                        out=gt_[:, :],
                        in_=AP(g_sp.tensor, ic * 128 * TOK + tc * 1024,
                               [[TOK, 128], [1, 1024]])), dsm)
                    ev_ld3[k] = ctx.cnt[dsm]
                B.sync(ld3)

                def v1f(v, k=k, gt_=gt_, uu=uu, rg_=rg_):
                    if k == 0:
                        ctx.w(v, "G", ev_bc[0])
                    ctx.w(v, "dG0" if k % 2 == 0 else "dG1", ev_ld3[k])
                    if k >= 2:
                        ctx.w(v, "S", S8 + k - 1)
                    v.tensor_scalar(uu[:, :], gt_[:, :], bc[:, 1:2], MAGIC,
                                    ALU.mult, ALU.add)
                    v.drain()
                    ev_v1[k] = ctx.b(v.tensor_single_scalar(
                        rg_[:, :], uu[:, :], -MAGIC, ALU.add), "V")
                B.vector(v1f)

                vcap1 = ctx.cnt["V"]
                vcap2 = ev_v2.get(k - 2, 0)
                sig_lams.append((vcap1, vcap2, k, rg_, ss))

                def v2a(v, k=k, uu=uu, ss=ss):
                    ctx.w(v, "S", S8 + k + 1)
                    v.tensor_scalar(uu[:, :], ss[:, :], 65536.0, MAGIC,
                                    ALU.mult, ALU.add)
                    v.drain()
                    ctx.b(v.tensor_scalar(ss[:, :], uu[:, :], -MAGIC, 65535.0,
                                          ALU.add, ALU.min), "V")
                B.vector(v2a)

                def g2f(g, k=k, ss=ss, rg_=rg_):
                    ctx.w(g, "V")
                    if k >= 2:
                        ctx.w(g, "dTG0" if k % 2 == 0 else "dTG1",
                              ev_st3.get(k - 2, 0))
                    sg = stgc0 if k % 2 == 0 else stgc1
                    ctx.b(g.tensor_mul(sg[:, :], rg_[:, :], ss[:, :]), "G")
                B.gpsimd(g2f)

                def v2b(v, k=k):
                    ctx.w(v, "G")
                    sg = stgc0 if k % 2 == 0 else stgc1
                    ev_v2[k] = ctx.b(v.reduce_max(
                        macc[:, 32 + k:33 + k], sg[:, :], AX.X,
                        apply_absolute_value=True), "V")
                B.vector(v2b)

                def st3(g, k=k, ic=ic, tc=tc):
                    ctx.w(g, "V", ev_v2[k])
                    sg = stgc0 if k % 2 == 0 else stgc1
                    ev_st3[k] = ctx.b16(g.dma_start(
                        out=AP(pt_sp.tensor, ic * 128 * TOK + tc * 1024,
                               [[TOK, 128], [1, 1024]]),
                        in_=sg[:, :]), "dTG0" if k % 2 == 0 else "dTG1")
                B.gpsimd(st3)

            def sig_body(s):
                for (vcap1, vcap2, k, rg_, ss) in sig_lams:
                    s.wait_ge(ctx.sem["V"], vcap1)
                    if vcap2 > 0:
                        s.wait_ge(ctx.sem["V"], vcap2)
                    got = ctx.b(s.activation(ss[:, :], rg_[:, :], AF.Sigmoid,
                                             bias=0.0, scale=bc[:, 2:3]), "S")
                    assert got == S8 + k + 1
            B.scalar(sig_body)
            mfin(32, ntile, 1, True)

            while ffn_state["km"] < nU or ffn_state["ke"] < nU:
                ffn_pump_ev()
                ffn_pump_mm()
            mfin(224, 192, 0, False)
            ctx.barrier()

            def c3_math(v):
                v.tensor_mul(sres[0:1, 10:11], sres[0:1, 0:1], scalg[0:1, 0:1])
                v.drain()
                v.tensor_scalar(sres[0:1, 4:5], sres[0:1, 10:11],
                                1.0 / Q15, 1e-12, ALU.mult, ALU.max)
                v.drain()
                v.reciprocal(sres[0:1, 11:12], sres[0:1, 4:5])
                v.drain()
                v.tensor_mul(sres[0:1, 5:6], sres[0:1, 0:1], sres[0:1, 11:12])
                v.tensor_scalar(sres[0:1, 12:13], sres[0:1, 2:3],
                                1.0 / 65536.0, None, ALU.mult)
                v.drain()
                v.tensor_mul(sres[0:1, 10:11], sres[0:1, 12:13],
                             scalg[0:1, 1:2])
                v.drain()
                v.tensor_scalar(sres[0:1, 6:7], sres[0:1, 10:11],
                                1.0 / Q15, 1e-12, ALU.mult, ALU.max)
                v.drain()
                v.reciprocal(sres[0:1, 11:12], sres[0:1, 6:7])
                v.drain()
                v.tensor_mul(sres[0:1, 7:8], sres[0:1, 12:13],
                             sres[0:1, 11:12])

            collective(2, c3_math, [(3, 5), (4, 7)])

            wdq_tot = [0]

            def wdpre(sync):
                for ic in range(ICH):
                    wdq_tot[0] = ctx.b16(sync.dma_start(
                        out=xnT[:, ic * 1024:(ic + 1) * 1024],
                        in_=wdq_d[ic * 128:(ic + 1) * 128, :]), "dW50")
            B.sync(wdpre)

            ev_ld4, ev_v3, ev_v4, ev_st4 = {}, {}, {}, {}
            p9_lams = []
            S9 = ctx.cnt["S"]
            for k in range(ntile):
                ic, tc = divmod(k, 4)
                ut_ = ldt0 if k % 2 == 0 else ldt1
                ptt = ldt2 if k % 2 == 0 else ldt3
                uu = ub2 if k % 2 == 0 else ub3
                ru_ = ub4 if k % 2 == 0 else ub5
                rt_ = ub6 if k % 2 == 0 else ub7

                def ld4(sync, ic=ic, tc=tc, ut_=ut_, ptt=ptt, k=k):
                    if k >= 2:
                        ctx.w(sync, "V", ev_v4.get(k - 2, 0))
                    dsm = "dG0" if k % 2 == 0 else "dG1"
                    ctx.b16(sync.dma_start(
                        out=ut_[:, :],
                        in_=AP(u_sp.tensor, ic * 128 * TOK + tc * 1024,
                               [[TOK, 128], [1, 1024]])), dsm)
                    ctx.b16(sync.dma_start(
                        out=ptt[:, :],
                        in_=AP(pt_sp.tensor, ic * 128 * TOK + tc * 1024,
                               [[TOK, 128], [1, 1024]])), dsm)
                    ev_ld4[k] = ctx.cnt[dsm]
                B.sync(ld4)

                def mk_s3(k=k, ut_=ut_, uu=uu,
                          tok=("dG0" if k % 2 == 0 else "dG1", 0),
                          vprev=ev_v4.get(k - 2, 0)):
                    def f(s, ldv=ev_ld4[k]):
                        if k == 0:
                            ctx.w(s, "G", ev_bc[0])
                        ctx.w(s, "dG0" if k % 2 == 0 else "dG1", ldv)
                        if vprev > 0:
                            s.wait_ge(ctx.sem["V"], vprev)
                        got = ctx.b(s.activation(
                            uu[:, :], ut_[:, :], AF.Copy,
                            bias=MAGIC, scale=bc[:, 3:4]), "S")
                        assert got == S9 + 2 * k + 1
                    return f
                p9_lams.append(mk_s3())

                def v3f(v, k=k, uu=uu, ru_=ru_):
                    ctx.w(v, "S", S9 + 2 * k + 1)
                    ev_v3[k] = ctx.b(v.tensor_single_scalar(
                        ru_[:, :], uu[:, :], -MAGIC, ALU.add), "V")
                B.vector(v3f)

                def mk_s4(k=k, ptt=ptt, uu=uu, vv=None):
                    def f(s, vv=ev_v3[k]):
                        s.wait_ge(ctx.sem["V"], vv)
                        got = ctx.b(s.activation(
                            uu[:, :], ptt[:, :], AF.Copy,
                            bias=MAGIC, scale=bc[:, 4:5]), "S")
                        assert got == S9 + 2 * k + 2
                    return f
                p9_lams.append(mk_s4())

                def v4a(v, k=k, uu=uu, rt_=rt_):
                    ctx.w(v, "S", S9 + 2 * k + 2)
                    ctx.b(v.tensor_single_scalar(rt_[:, :], uu[:, :],
                                                 -MAGIC, ALU.add), "V")
                B.vector(v4a)

                def g4f(g, k=k, ru_=ru_, rt_=rt_):
                    ctx.w(g, "V")
                    if k >= 2:
                        ctx.w(g, "dTG0" if k % 2 == 0 else "dTG1",
                              ev_st4.get(k - 2, 0))
                    else:
                        ctx.w(g, "dTG0" if k % 2 == 0 else "dTG1")
                    sg = stgc0 if k % 2 == 0 else stgc1
                    ctx.b(g.tensor_mul(sg[:, :], ru_[:, :], rt_[:, :]), "G")
                B.gpsimd(g4f)

                def v4b(v, k=k):
                    ctx.w(v, "G")
                    sg = stgc0 if k % 2 == 0 else stgc1
                    ev_v4[k] = ctx.b(v.reduce_max(
                        macc[:, 32 + k:33 + k], sg[:, :], AX.X,
                        apply_absolute_value=True), "V")
                B.vector(v4b)

                def st4(g, k=k, ic=ic, tc=tc):
                    ctx.w(g, "V", ev_v4[k])
                    sg = stgc0 if k % 2 == 0 else stgc1
                    ev_st4[k] = ctx.b16(g.dma_start(
                        out=AP(pu_sp.tensor, ic * 128 * TOK + tc * 1024,
                               [[TOK, 128], [1, 1024]]),
                        in_=sg[:, :]), "dTG0" if k % 2 == 0 else "dTG1")
                B.gpsimd(st4)

            def p9_sbody(s):
                for f in p9_lams:
                    f(s)
            B.scalar(p9_sbody)
            mfin(32, ntile, 0, True)
            ctx.barrier()

            def c4_math(v):
                v.tensor_mul(sres[0:1, 12:13], sres[0:1, 6:7], sres[0:1, 4:5])
                v.drain()
                v.tensor_mul(sres[0:1, 10:11], sres[0:1, 12:13],
                             scalg[0:1, 0:1])
                v.drain()
                v.tensor_scalar(sres[0:1, 8:9], sres[0:1, 10:11],
                                1.0 / Q15, 1e-12, ALU.mult, ALU.max)
                v.drain()
                v.reciprocal(sres[0:1, 11:12], sres[0:1, 8:9])
                v.drain()
                v.tensor_mul(sres[0:1, 9:10], sres[0:1, 12:13],
                             sres[0:1, 11:12])

            collective(3, c4_math, [(5, 9), (6, 8)])

            ev_ro, ev_x10, ev_ao, ev_dn, ev_cmb, ev_ost = {}, {}, {}, {}, {}, {}
            ev_ost_last = {0: 0, 1: 0}
            p10_lams = []
            S10 = ctx.cnt["S"]
            for tr8 in range(8):
                robc = rob if tr8 % 2 == 0 else rob2
                for ic in range(ICH):
                    kr = tr8 * ICH + ic
                    ldt = ldt0 if kr % 2 == 0 else ldt1
                    uu = ub2 if kr % 2 == 0 else ub3

                    def ld5(sync, tr8=tr8, ic=ic, ldt=ldt, kr=kr):
                        if kr >= 2:
                            ctx.w(sync, "S", ev_ro.get((kr - 2, "s"), 0))
                        if tr8 >= 2:
                            ctx.w(sync, "T", ev_dn.get((tr8 - 2, 1), 0))
                        dsm = "dR0" if kr % 2 == 0 else "dR1"
                        ctx.b16(sync.dma_start(
                            out=ldt[:, 0:512],
                            in_=AP(pu_sp.tensor, ic * 128 * TOK + tr8 * 512,
                                   [[TOK, 128], [1, 512]])), dsm)
                        ev_ro[(kr, "d")] = ctx.cnt[dsm]
                    B.sync(ld5)

                    def mk_s5(kr=kr, ldt=ldt, uu=uu,
                              vprev=ev_ro.get((kr - 2, "v"), 0)):
                        def f(s, ldv=ev_ro[(kr, "d")]):
                            if kr == 0:
                                ctx.w(s, "G", ev_bc[0])
                            ctx.w(s, "dR0" if kr % 2 == 0 else "dR1", ldv)
                            if vprev > 0:
                                s.wait_ge(ctx.sem["V"], vprev)
                            got = ctx.b(s.activation(
                                uu[:, 0:512], ldt[:, 0:512], AF.Copy,
                                bias=MAGIC, scale=bc[:, 5:6]), "S")
                            assert got == S10 + kr + 1
                        return f
                    p10_lams.append(mk_s5())
                    ev_ro[(kr, "s")] = S10 + kr + 1

                    def v5f(v, kr=kr, ic=ic, uu=uu, robc=robc):
                        ctx.w(v, "S", ev_ro[(kr, "s")])
                        ev_ro[(kr, "v")] = ctx.b(v.tensor_single_scalar(
                            robc[:, ic * 512:(ic + 1) * 512], uu[:, 0:512],
                            -MAGIC, ALU.add), "V")
                    B.vector(v5f)

                ro_done = ctx.cnt["V"]

                for half in range(2):
                    banks = [ps0, ps1, ps2, ps3]

                    def merged(tr8=tr8, half=half, ro_done=ro_done,
                               banks=banks, robc=robc):
                        def mm5(t, tr8=tr8, half=half, robc=robc):
                            ctx.w(t, "dW50", wdq_tot[0])
                            ctx.w(t, "V")
                            for ic in range(ICH):
                                for tj in range(2):
                                    tt = half * 2 + tj
                                    for ch in range(2):
                                        ins = t.matmul(
                                            banks[tj * 2 + ch][:, :],
                                            robc[:, ic * 512 + tt * 128:ic * 512 + (tt + 1) * 128],
                                            xnT[:, ic * 1024 + ch * 512:ic * 1024 + (ch + 1) * 512],
                                            start=(ic == 0),
                                            stop=(ic == ICH - 1))
                            ev_dn[(tr8, half)] = ctx.b(ins, "T")
                        B.tensor(mm5)
                    merged()

                    for tj in range(2):
                        tt = half * 2 + tj
                        gtt = tr8 * 4 + tt
                        xt = xt0 if gtt % 2 == 0 else xt1

                        def ldx(sync, gtt=gtt, xt=xt, tr8=tr8, half=half,
                                tj=tj):
                            if ev_cmb:
                                ctx.w(sync, "V")
                            dsm = "dX0" if gtt % 2 == 0 else "dX1"
                            ctx.b16(sync.dma_start(
                                out=xt[:, :],
                                in_=x_d[gtt * 128:(gtt + 1) * 128, :]), dsm)
                            ev_x10[gtt] = (dsm, ctx.cnt[dsm])
                        B.sync(ldx)

                        for ch in range(2):
                            sh = stgh0 if ch == 0 else stgh1
                            ou = outs0 if ch == 0 else outs1

                            def lda2(sync, gtt=gtt, ch=ch, sh=sh):
                                ctx.w(sync, "V")
                                dsm = "dAO0" if ch == 0 else "dAO1"
                                ctx.b16(sync.dma_start(
                                    out=sh[:, :],
                                    in_=AP(ao_sp.tensor,
                                           gtt * 128 * C + ch * 512,
                                           [[C, 128], [1, 512]])), dsm)
                                ev_ao[(gtt, ch)] = (dsm, ctx.cnt[dsm])
                            B.sync(lda2)

                            def cmb2(v, tr8=tr8, half=half, tj=tj, ch=ch,
                                     gtt=gtt, xt=xt, sh=sh, ou=ou):
                                ctx.w(v, "T", ev_dn[(tr8, half)])
                                ctx.wt(v, ev_ao[(gtt, ch)])
                                ctx.wt(v, ev_x10[gtt])
                                ctx.w(v, "dOUT0" if ch == 0 else "dOUT1",
                                      ev_ost_last[ch])
                                bank = [ps0, ps1, ps2, ps3][tj * 2 + ch]
                                ev_cmb[(gtt, ch)] = ctx.b(
                                    v.scalar_tensor_tensor(
                                        ou[:, :], bank[:, :], bc[:, 6:7],
                                        sh[:, :], ALU.mult, ALU.add), "V")
                            B.vector(cmb2)

                            def gadd(g, gtt=gtt, ch=ch, ou=ou, xt=xt):
                                ctx.w(g, "V", ev_cmb[(gtt, ch)])
                                ev_cmb[(gtt, ch, "g")] = ctx.b(g.tensor_add(
                                    ou[:, :], ou[:, :],
                                    xt[:, ch * 512:(ch + 1) * 512]), "G")
                            B.gpsimd(gadd)

                            def ost(g, gtt=gtt, ch=ch, ou=ou):
                                ctx.w(g, "G", ev_cmb[(gtt, ch, "g")])
                                ev_ost[(gtt, ch)] = ctx.b16(g.dma_start(
                                    out=out_d[gtt * 128:(gtt + 1) * 128,
                                              ch * 512:(ch + 1) * 512],
                                    in_=ou[:, :]),
                                    "dOUT0" if ch == 0 else "dOUT1")
                                ev_ost_last[ch] = ev_ost[(gtt, ch)]
                            B.gpsimd(ost)

            def p10_sbody(s):
                for f in p10_lams:
                    f(s)
            B.scalar(p10_sbody)
            ctx.barrier()

    return nc


_NC_CACHE = [None]


def kernel(L0_attn_x, L0_attn_w_qkv, L0_attn_w_proj, L0_attn_b_proj,
           L0_ffn_w_gate, L0_ffn_w_up, L0_ffn_w_down):
    x = np.ascontiguousarray(np.asarray(L0_attn_x, dtype=np.float32))
    wqkv = np.ascontiguousarray(np.asarray(L0_attn_w_qkv, dtype=np.float32))
    wproj = np.ascontiguousarray(np.asarray(L0_attn_w_proj, dtype=np.float32))
    bproj = np.ascontiguousarray(np.asarray(L0_attn_b_proj, dtype=np.float32))
    wg = np.ascontiguousarray(np.asarray(L0_ffn_w_gate, dtype=np.float32))
    wu = np.ascontiguousarray(np.asarray(L0_ffn_w_up, dtype=np.float32))
    wd = np.ascontiguousarray(np.asarray(L0_ffn_w_down, dtype=np.float32))

    wqkT = np.ascontiguousarray(wqkv.transpose(0, 2, 1))
    wpT = np.ascontiguousarray(wproj.T)
    wgT = np.ascontiguousarray(wg.T)
    wuT = np.ascontiguousarray(wu.T)
    wdT = np.ascontiguousarray(wd.T)

    identb = np.eye(128, dtype=ml_dtypes.bfloat16)
    identf = np.eye(128, dtype=np.float32)
    maskneg = np.triu(np.full((128, 128), -1.0e30, dtype=np.float32), 1)
    ones1 = np.ones((1, 128), dtype=ml_dtypes.bfloat16)

    if _NC_CACHE[0] is None:
        _NC_CACHE[0] = build_nc()
    nc = _NC_CACHE[0]

    shared = dict(wqkT=wqkT, wpT=wpT, bproj=bproj.reshape(1, C),
                  wg=wg, wgT=wgT, wu=wu, wuT=wuT, wd=wd, wdT=wdT,
                  identb_in=identb, identf_in=identf,
                  maskneg_in=maskneg, ones1_in=ones1)
    in_maps = []
    for c in range(NCORE):
        m = dict(shared)
        m["x"] = np.ascontiguousarray(
            x[2 * c:2 * c + 2].reshape(TOK, C))
        in_maps.append(m)

    res = run_bass_kernel_spmd(nc, in_maps, core_ids=list(range(NCORE)))
    out = np.concatenate(
        [r["out"].reshape(2, T, C) for r in res.results], axis=0)
    return out.astype(np.float32)


# version 12 (final)
# speedup vs baseline: 1.2001x; 1.0075x over previous
import contextlib
import numpy as np
import ml_dtypes
import concourse.bass as bass
import concourse.mybir as mybir
from concourse.bass_utils import run_bass_kernel_spmd

F32 = mybir.dt.float32
BF16 = mybir.dt.bfloat16
F16 = mybir.dt.float16
AF = mybir.ActivationFunctionType
ALU = mybir.AluOpType
AX = mybir.AxisListType
AP = bass.AP

MAGIC = 12582912.0
Q15 = 32767.0

SEQ, T, C, H, HS, II = 2, 2048, 1024, 16, 64, 3072
TOK = SEQ * T
CCH = C // 128
ICH = II // 128
NTT = TOK // 128
NCORE = 8

ENGINES = ("sync", "vector", "scalar", "tensor", "gpsimd")


class Ctx:
    def __init__(self, block, sems, nc=None, es=None):
        self.block = block
        self.sem = sems
        self.cnt = {k: 0 for k in sems}
        self._nc = nc
        self._es = es

    def mksem(self, name):
        h = self._es.enter_context(self._nc.semaphore(name))
        self.sem[name] = h
        self.cnt[name] = 0
        return name

    def tok(self, s):
        return (s, self.cnt[s])

    def wt(self, eng, token):
        s, v = token
        if v > 0:
            eng.wait_ge(self.sem[s], v)

    def b(self, instr, s, amt=1):
        self.cnt[s] += amt
        instr.then_inc(self.sem[s], amt)
        return self.cnt[s]

    def b16(self, instr, s):
        return self.b(instr, s, 16)

    def w(self, eng, s, val=None):
        if val is None:
            val = self.cnt[s]
        if val > 0:
            eng.wait_ge(self.sem[s], val)

    def barrier(self):
        snap = dict(self.cnt)
        for ename in ENGINES:
            def body(e, snap=snap):
                for s, v in snap.items():
                    if v > 0:
                        e.wait_ge(self.sem[s], v)
            getattr(self.block, ename)(body)


def build_nc():
    nc = bass.Bass("TRN2", target_bir_lowering=False)

    x_d = nc.dram_tensor("x", [TOK, C], F32, kind="ExternalInput").ap()
    wqkT_d = nc.dram_tensor("wqkT", [H, C, 192], F32, kind="ExternalInput").ap()
    wpT_d = nc.dram_tensor("wpT", [C, C], F32, kind="ExternalInput").ap()
    bproj_d = nc.dram_tensor("bproj", [1, C], F32, kind="ExternalInput").ap()
    wg_d = nc.dram_tensor("wg", [II, C], F32, kind="ExternalInput").ap()
    wgT_d = nc.dram_tensor("wgT", [C, II], F32, kind="ExternalInput").ap()
    wu_d = nc.dram_tensor("wu", [II, C], F32, kind="ExternalInput").ap()
    wuT_d = nc.dram_tensor("wuT", [C, II], F32, kind="ExternalInput").ap()
    wd_d = nc.dram_tensor("wd", [C, II], F32, kind="ExternalInput").ap()
    wdT_d = nc.dram_tensor("wdT", [II, C], F32, kind="ExternalInput").ap()
    identb_d = nc.dram_tensor("identb_in", [128, 128], BF16, kind="ExternalInput").ap()
    identf_d = nc.dram_tensor("identf_in", [128, 128], F32, kind="ExternalInput").ap()
    maskneg_d = nc.dram_tensor("maskneg_in", [128, 128], F32, kind="ExternalInput").ap()
    ones1_d = nc.dram_tensor("ones1_in", [1, 128], BF16, kind="ExternalInput").ap()
    out_d = nc.dram_tensor("out", [TOK, C], F32, kind="ExternalOutput").ap()

    wqkb_d = nc.dram_tensor("wqkb", [H, C, 192], BF16).ap()
    wpb_d = nc.dram_tensor("wpb", [C, C], BF16).ap()
    srg_d = nc.dram_tensor("srg", [64, II], F32).ap()
    sru_d = nc.dram_tensor("sru", [64, II], F32).ap()
    srd_d = nc.dram_tensor("srd", [192, C], F32).ap()
    wgq_d = nc.dram_tensor("wgq", [C, II], BF16).ap()
    wuq_d = nc.dram_tensor("wuq", [C, II], BF16).ap()
    wdq_d = nc.dram_tensor("wdq", [II, C], BF16).ap()
    qk_sp = nc.dram_tensor("qk_sp", [SEQ * 8 * 2 * 128, T], BF16).ap()
    v_sp = nc.dram_tensor("v_sp", [SEQ * 128, 16 * C], BF16).ap()
    ao_sp = nc.dram_tensor("ao_sp", [TOK, C], F16).ap()
    g_sp = nc.dram_tensor("g_sp", [II, TOK], BF16).ap()
    u_sp = nc.dram_tensor("u_sp", [II, TOK], BF16).ap()
    pt_sp = nc.dram_tensor("pt_sp", [II, TOK], BF16).ap()
    pu_sp = nc.dram_tensor("pu_sp", [II, TOK], BF16).ap()
    cc_in = [nc.dram_tensor(f"cc_in{i}", [1, 16], F32).ap() for i in range(4)]
    cc_out = [nc.dram_tensor(f"cc_out{i}", [1, 16], F32).ap() for i in range(4)]
    sc_d = nc.dram_tensor("sc_d", [4, 16], F32).ap()

    al = nc.sbuf_tensor

    with contextlib.ExitStack() as _es1:
        xnT = _es1.enter_context(al("xnT", [128, CCH * TOK], BF16))
        xt0 = _es1.enter_context(al("xt0", [128, C], F32))
        xt1 = _es1.enter_context(al("xt1", [128, C], F32))
        stats = _es1.enter_context(al("stats", [128, 16], F32))
        macc = _es1.enter_context(al("macc", [128, 416], F32))
        wql0 = _es1.enter_context(al("wql0", [128, 1024], BF16))
        wql1 = _es1.enter_context(al("wql1", [128, 1024], BF16))
        gath = _es1.enter_context(al("gath", [1, 128], F32))
        scal = _es1.enter_context(al("scal", [1, 16], F32))
        scalg = _es1.enter_context(al("scalg", [1, 16], F32))
        sres = _es1.enter_context(al("sres", [1, 16], F32))
        identb = _es1.enter_context(al("identb_sb", [128, 128], BF16))
        identf = _es1.enter_context(al("identf_sb", [128, 128], F32))
        maskneg = _es1.enter_context(al("maskneg_sb", [128, 128], F32))
        ones1 = _es1.enter_context(al("ones1_sb", [1, 128], BF16))
        bproj = _es1.enter_context(al("bproj_sb", [1, C], F32))
        bprojb = _es1.enter_context(al("bprojb", [1, C], BF16))
        bc = _es1.enter_context(al("bc", [128, 8], F32))
        stg0 = _es1.enter_context(al("stg0", [128, 1024], BF16))
        stg1 = _es1.enter_context(al("stg1", [128, 1024], BF16))
        stgh0 = _es1.enter_context(al("stgh0", [128, 512], F16))
        stgh1 = _es1.enter_context(al("stgh1", [128, 512], F16))
        outs0 = _es1.enter_context(al("outs0", [128, 512], F32))
        outs1 = _es1.enter_context(al("outs1", [128, 512], F32))
        ps0 = _es1.enter_context(nc.psum_tensor("ps0", [128, 512], F32))
        ps1 = _es1.enter_context(nc.psum_tensor("ps1", [128, 512], F32))
        scps = _es1.enter_context(nc.psum_tensor("scps", [128, 2048], F32))
        ps2 = scps[:, 0:512]
        ps3 = scps[:, 512:1024]
        ps4 = scps[:, 1024:1536]
        ps5 = scps[:, 1536:2048]
        ps6 = _es1.enter_context(nc.psum_tensor("ps6", [128, 1024], BF16))
        ps7 = _es1.enter_context(nc.psum_tensor("ps7", [128, 512], F32))
        smT = _es1.enter_context(nc.semaphore("smT"))
        smV = _es1.enter_context(nc.semaphore("smV"))
        smS = _es1.enter_context(nc.semaphore("smS"))
        smG = _es1.enter_context(nc.semaphore("smG"))
        smCC = _es1.enter_context(nc.semaphore("smCC"))
        block = _es1.enter_context(nc.Block())
        ctx = Ctx(block, dict(T=smT, V=smV, S=smS, G=smG, CC=smCC),
                  nc=nc, es=_es1)
        for _snm in ("dP0", "dWL", "dWS", "dA0", "dA1", "dAS", "dB0", "dB1",
                     "dBS0", "dBS1", "dX0", "dX1", "dMG", "dCC", "dVW",
                     "dQW", "dVS0", "dVS1", "dPW", "dVB", "dQK", "dPS0",
                     "dPS1", "dFW0", "dFW1", "dFS0", "dFS1", "dG0", "dG1",
                     "dTS0", "dTS1", "dR0", "dR1", "dW50", "dW51", "dAO0",
                     "dAO1", "dOUT0", "dOUT1", "dTG0", "dTG1",
                     "dRX0", "dRX1", "dRX2", "dA2", "dB2", "dBS2",
                     "dQW1"):
            ctx.mksem(_snm)
        B = block
        sbanks = [ps2, ps3, ps4, ps5]

        def p0(sync):
            ctx.b16(sync.dma_start(out=identb[:, :], in_=identb_d), "dP0")
            ctx.b16(sync.dma_start(out=identf[:, :], in_=identf_d), "dP0")
            ctx.b16(sync.dma_start(out=maskneg[:, :], in_=maskneg_d), "dP0")
            ctx.b16(sync.dma_start(out=ones1[:, :], in_=ones1_d), "dP0")
            ctx.b16(sync.dma_start(out=bproj[:, :], in_=bproj_d), "dP0")
        B.sync(p0)

        def p0b(s):
            ctx.w(s, "dP0")
            ctx.b(s.activation(bprojb[:, :], bproj[:, :], AF.Copy,
                               bias=0.0, scale=1.0), "S")
        B.scalar(p0b)
        ctx.barrier()

        ev_bc = [0]

        def collective(idx, post_math, bslots):
            def go(g):
                ctx.w(g, "V")
                ctx.b16(g.dma_start(out=cc_in[idx], in_=scal[:, :]), "dCC")
                ctx.w(g, "dCC")
                ctx.b(g.collective_compute(
                    "AllReduce", ALU.max,
                    replica_groups=[list(range(NCORE))],
                    ins=[cc_in[idx]], outs=[cc_out[idx]]), "CC")
                ctx.w(g, "CC")
                ctx.b16(g.dma_start(out=scalg[:, :], in_=cc_out[idx]), "dCC")
            B.gpsimd(go)

            def pm(v):
                ctx.w(v, "dCC")
                post_math(v)
                ctx.b(v.drain(), "V")
            B.vector(pm)

            def bounce(g, idx=idx):
                ctx.w(g, "V")
                ctx.b16(g.dma_start(out=sc_d[idx:idx + 1, :],
                                    in_=sres[:, :]), "dCC")
                ctx.w(g, "dCC")
                for (bcol, cix) in bslots:
                    src = AP(sc_d.tensor, idx * 16 + cix, [[0, 128], [1, 1]])
                    ctx.b16(g.dma_start(
                        out=bc[:, bcol:bcol + 1], in_=src), "dCC")
                ctx.w(g, "dCC")
                g.sem_inc(ctx.sem["G"], 1)
                ctx.cnt["G"] += 1
                ev_bc[0] = ctx.cnt["G"]
            B.gpsimd(bounce)

        def c1_math(v):
            v.tensor_scalar(sres[0:1, 0:1], scalg[0:1, 0:1],
                            1.0 / Q15, 1e-12, ALU.mult, ALU.max)
            v.drain()
            v.reciprocal(sres[0:1, 1:2], sres[0:1, 0:1])

        with contextlib.ExitStack() as _es2:
            wsA0 = _es2.enter_context(al("wsA0", [128, II], F32))
            wsA1 = _es2.enter_context(al("wsA1", [128, II], F32))
            wsA2 = _es2.enter_context(al("wsA2", [128, II], F32))
            wsB0 = _es2.enter_context(al("wsB0", [128, 512], F32))
            wsB1 = _es2.enter_context(al("wsB1", [128, 512], F32))
            wsB2 = _es2.enter_context(al("wsB2", [128, 512], F32))
            sxp0 = _es2.enter_context(al("sxp0", [128, 512], F32))
            sxp1 = _es2.enter_context(al("sxp1", [128, 512], F32))
            sxp2 = _es2.enter_context(al("sxp2", [128, 512], F32))
            rxp0 = _es2.enter_context(al("rxp0", [128, 512], F32))
            rxp1 = _es2.enter_context(al("rxp1", [128, 512], F32))
            rxp2 = _es2.enter_context(al("rxp2", [128, 512], F32))
            srn = _es2.enter_context(al("srn", [128, 192], F32))
            srev = _es2.enter_context(al("srev", [128, 256], F32))
            wqb0 = _es2.enter_context(al("wqb0", [128, 512], BF16))
            wqb1 = _es2.enter_context(al("wqb1", [128, 512], BF16))
            wqb2 = _es2.enter_context(al("wqb2", [128, 512], BF16))
            pqb = _es2.enter_context(al("pqb", [128, 512], F32))
            wkst = _es2.enter_context(al("wkst", [128, 1536], F32))
            wkbf = _es2.enter_context(al("wkbf", [128, 1536], BF16))
            xsq = _es2.enter_context(al("xsq", [128, C], F32))
            xnf = _es2.enter_context(al("xnf", [128, C], F32))

            sched = []
            s_base = ctx.cnt["S"]
            s_off = [0]
            asbar = {}

            def cast_gen():
                specs = [("qkv", h) for h in range(H)] + \
                        [("wp", ccb) for ccb in range(CCH)]
                evW = [0]
                evSt = [0]
                prev_prom = [None]
                for (kind, idx) in specs:
                    def ld(g, kind=kind, idx=idx, pp=prev_prom[0]):
                        if pp is not None:
                            ctx.w(g, "S", pp)
                        if kind == "qkv":
                            src = AP(wqkT_d.tensor, idx * C * 192,
                                     [[192, 128], [192 * 128, 8], [1, 192]])
                            evW[0] = ctx.b16(
                                g.dma_start(out=wkst[:, :], in_=src), "dWL")
                        else:
                            evW[0] = ctx.b16(g.dma_start(
                                out=wkst[:, 0:1024],
                                in_=wpT_d[idx * 128:(idx + 1) * 128, :]),
                                "dWL")
                    B.gpsimd(ld)
                    myW = evW[0]
                    mySt = evSt[0]
                    s_off[0] += 1
                    prom = s_base + s_off[0]

                    def cast(s, kind=kind, myW=myW, mySt=mySt, prom=prom):
                        ctx.w(s, "dWL", myW)
                        ctx.w(s, "dWS", mySt)
                        if kind == "qkv":
                            qv = AP(wkst, 0, [[1536, 128], [192, 8], [1, 64]])
                            qo = AP(wkbf, 0, [[1536, 128], [192, 8], [1, 64]])
                            s.activation(qo, qv, AF.Copy, bias=0.0,
                                         scale=0.125)
                            kv = AP(wkst, 64,
                                    [[1536, 128], [192, 8], [1, 128]])
                            ko = AP(wkbf, 64,
                                    [[1536, 128], [192, 8], [1, 128]])
                            got = ctx.b(s.activation(ko, kv, AF.Copy,
                                                     bias=0.0, scale=1.0),
                                        "S")
                        else:
                            got = ctx.b(s.activation(
                                wkbf[:, 0:1024], wkst[:, 0:1024], AF.Copy,
                                bias=0.0, scale=1.0), "S")
                        assert got == prom
                    sched.append(cast)
                    prev_prom[0] = prom

                    def st(g, kind=kind, idx=idx, prom=prom):
                        ctx.w(g, "S", prom)
                        if kind == "qkv":
                            dst = AP(wqkb_d.tensor, idx * C * 192,
                                     [[192, 128], [192 * 128, 8], [1, 192]])
                            evSt[0] = ctx.b16(
                                g.dma_start(out=dst, in_=wkbf[:, :]), "dWS")
                        else:
                            evSt[0] = ctx.b16(g.dma_start(
                                out=wpb_d[idx * 128:(idx + 1) * 128, :],
                                in_=wkbf[:, 0:1024]), "dWS")
                    B.gpsimd(st)
                    yield

            def A_gen():
                specs = []
                for (nat_d, sr_d, nrc, icols, nblk, srstride) in (
                        (wg_d, srg_d, ICH, C, 32, II),
                        (wu_d, sru_d, ICH, C, 32, II),
                        (wd_d, srd_d, CCH, II, 96, C)):
                    for rc in range(nrc):
                        specs.append((nat_d, sr_d, rc, icols, nblk,
                                      srstride))
                nA = len(specs)
                wsl = [wsA0, wsA1, wsA2]
                evV, evT, evG = {}, {}, {}

                def emit_ld(g):
                    (nat_d, sr_d, rc, icols, nblk, srstride) = specs[g]
                    ws = wsl[g % 3]

                    def ld(sync, g=g, rc=rc, ws=ws, icols=icols,
                           nat_d=nat_d):
                        if g >= 3:
                            ctx.w(sync, "V", evV[g - 3])
                        evV[(g, "ld")] = ctx.b16(sync.dma_start(
                            out=ws[:, 0:icols],
                            in_=nat_d[rc * 128:(rc + 1) * 128, :]),
                            ("dA0", "dA1", "dA2")[g % 3])
                    B.sync(ld)

                def emit_rest(g):
                    (nat_d, sr_d, rc, icols, nblk, srstride) = specs[g]
                    ws = wsl[g % 3]

                    def red(v, g=g, ws=ws, nblk=nblk):
                        ctx.w(v, ("dA0", "dA1", "dA2")[g % 3],
                              evV[(g, "ld")])
                        if g >= 1:
                            ctx.w(v, "T", evT[g - 1])
                        src = AP(ws, 0, [[II, 128], [32, nblk], [1, 32]])
                        v.reduce_max(srn[:, 0:nblk], src, AX.X,
                                     apply_absolute_value=True)
                        v.drain()
                        v.tensor_scalar(srn[:, 0:nblk], srn[:, 0:nblk],
                                        1.0 / 7.0, 1e-12, ALU.mult, ALU.max)
                        v.drain()
                        v.reciprocal(srn[:, nblk:2 * nblk], srn[:, 0:nblk])
                        evV[g] = ctx.b(v.drain(), "V")
                    B.vector(red)

                    def tr(t, g=g, nblk=nblk):
                        ctx.w(t, "V", evV[g])
                        if g >= 1:
                            ctx.w(t, "V", evV[(g - 1, "e")])
                        if nblk <= 64:
                            ins = t.transpose(ps5[0:2 * nblk, 0:128],
                                              srn[:, 0:2 * nblk],
                                              identf[:, :])
                        else:
                            t.transpose(ps5[0:nblk, 0:128], srn[:, 0:nblk],
                                        identf[:, :])
                            ins = t.transpose(ps5[0:nblk, 128:256],
                                              srn[:, nblk:2 * nblk],
                                              identf[:, :])
                        evT[g] = ctx.b(ins, "T")
                    B.tensor(tr)

                    def vev(v, g=g, nblk=nblk):
                        ctx.w(v, "T", evT[g])
                        if g >= 1:
                            ctx.w(v, "dAS", evG[g - 1])
                        if nblk <= 64:
                            ins = v.tensor_copy(srev[0:2 * nblk, 0:128],
                                                ps5[0:2 * nblk, 0:128])
                        else:
                            v.tensor_copy(srev[0:nblk, 0:128],
                                          ps5[0:nblk, 0:128])
                            ins = v.tensor_copy(srev[0:nblk, 128:256],
                                                ps5[0:nblk, 128:256])
                        evV[(g, "e")] = ctx.b(ins, "V")
                    B.vector(vev)

                    def st(gp, g=g, rc=rc, nblk=nblk, srstride=srstride,
                           sr_d=sr_d):
                        ctx.w(gp, "V", evV[(g, "e")])
                        if nblk <= 64:
                            dst = AP(sr_d.tensor, rc * 128,
                                     [[srstride, 2 * nblk], [1, 128]])
                            evG[g] = ctx.b16(
                                gp.dma_start(out=dst,
                                             in_=srev[0:2 * nblk, 0:128]),
                                "dAS")
                        else:
                            d1 = AP(sr_d.tensor, rc * 128,
                                    [[srstride, nblk], [1, 128]])
                            ctx.b16(gp.dma_start(out=d1,
                                                 in_=srev[0:nblk, 0:128]),
                                    "dAS")
                            d2 = AP(sr_d.tensor, nblk * srstride + rc * 128,
                                    [[srstride, nblk], [1, 128]])
                            evG[g] = ctx.b16(
                                gp.dma_start(out=d2,
                                             in_=srev[0:nblk, 128:256]),
                                "dAS")
                    B.gpsimd(st)

                for g in range(nA + 2):
                    if g < nA:
                        emit_ld(g)
                    if g >= 2:
                        emit_rest(g - 2)
                        if g - 2 == 23:
                            asbar[0] = ctx.cnt["dAS"]
                        elif g - 2 == 47:
                            asbar[1] = ctx.cnt["dAS"]
                        elif g - 2 == 55:
                            asbar[2] = ctx.cnt["dAS"]
                    yield

            def B_gen():
                specs = []
                for w, (wT_src, sr_d, wq_dst, nrc, ocols, rstride) in \
                        enumerate(((wgT_d, srg_d, wgq_d, CCH, II, 32),
                                   (wuT_d, sru_d, wuq_d, CCH, II, 32),
                                   (wdT_d, srd_d, wdq_d, ICH, C, 96))):
                    for rc in range(nrc):
                        for it in range(ocols // 512):
                            specs.append((w, wT_src, sr_d, wq_dst, rc, it,
                                          ocols, rstride))
                nB = len(specs)
                wsl = [wsB0, wsB1, wsB2]
                sxl = [sxp0, sxp1, sxp2]
                rxl = [rxp0, rxp1, rxp2]
                wql = [wqb0, wqb1, wqb2]
                evV, evG = {}, {}

                def emit_ld(k):
                    (w, wT_src, sr_d, wq_dst, rc, it, ocols, rstride) = \
                        specs[k]
                    ws, sx, rx_ = wsl[k % 3], sxl[k % 3], rxl[k % 3]

                    def ld(sync, w=w, rc=rc, it=it, ws=ws, sx=sx,
                           k=k, ocols=ocols, sr_d=sr_d, wT_src=wT_src):
                        ctx.w(sync, "dAS", asbar[w])
                        if k >= 3:
                            ctx.w(sync, "V", evV[k - 3])
                        dsm = ("dB0", "dB1", "dB2")[k % 3]
                        ctx.b16(sync.dma_start(
                            out=ws[:, :],
                            in_=wT_src[rc * 128:(rc + 1) * 128,
                                       it * 512:(it + 1) * 512]), dsm)
                        soff = 4 * rc * ocols + it * 512
                        evV[(k, "ld")] = ctx.b16(
                            sync.dma_start(out=sx[:, :], in_=AP(
                                sr_d.tensor, soff,
                                [[ocols, 4], [0, 32], [1, 512]])), dsm)
                    B.sync(ld)

                    def ldr(g, w=w, rc=rc, it=it, rx_=rx_, k=k,
                            ocols=ocols, rstride=rstride, sr_d=sr_d):
                        ctx.w(g, "dAS", asbar[w])
                        if k >= 3:
                            ctx.w(g, "V", evV[k - 3])
                        soff = 4 * rc * ocols + it * 512
                        evV[(k, "ldr")] = ctx.b16(
                            g.dma_start(out=rx_[:, :], in_=AP(
                                sr_d.tensor, rstride * ocols + soff,
                                [[ocols, 4], [0, 32], [1, 512]])),
                            ("dRX0", "dRX1", "dRX2")[k % 3])
                    B.gpsimd(ldr)

                def emit_rest(k):
                    (w, wT_src, sr_d, wq_dst, rc, it, ocols, rstride) = \
                        specs[k]
                    ws, sx, rx_ = wsl[k % 3], sxl[k % 3], rxl[k % 3]
                    wq = wql[k % 3]

                    def qz(v, ws=ws, sx=sx, rx_=rx_, wq=wq, k=k):
                        ctx.w(v, ("dB0", "dB1", "dB2")[k % 3],
                              evV[(k, "ld")])
                        ctx.w(v, ("dRX0", "dRX1", "dRX2")[k % 3],
                              evV[(k, "ldr")])
                        if k >= 3:
                            ctx.w(v, ("dBS0", "dBS1", "dBS2")[(k - 3) % 3],
                                  evG[k - 3])
                        v.drain()
                        v.tensor_mul(pqb[:, :], ws[:, :], rx_[:, :])
                        v.drain()
                        v.tensor_single_scalar(pqb[:, :], pqb[:, :],
                                               MAGIC, ALU.add)
                        v.drain()
                        evV[k] = ctx.b(v.scalar_tensor_tensor(
                            wq[:, :], pqb[:, :], -MAGIC, sx[:, :],
                            ALU.add, ALU.mult), "V")
                    B.vector(qz)

                    def st(g, rc=rc, it=it, wq=wq, k=k, wq_dst=wq_dst):
                        ctx.w(g, "V", evV[k])
                        evG[k] = ctx.b16(g.dma_start(
                            out=wq_dst[rc * 128:(rc + 1) * 128,
                                       it * 512:(it + 1) * 512],
                            in_=wq[:, :]),
                            ("dBS0", "dBS1", "dBS2")[k % 3])
                    B.gpsimd(st)

                for k in range(nB + 2):
                    if k < nB:
                        emit_ld(k)
                    if k >= 2:
                        emit_rest(k - 2)
                    yield

            cg = cast_gen()
            ag = A_gen()
            bg = B_gen()

            ev_xn, ev_tp, ev_ev = {}, {}, {}
            for tt in range(NTT):
                xt = xt0 if tt % 2 == 0 else xt1

                def ld(sync, tt=tt, xt=xt):
                    if tt >= 2:
                        ctx.w(sync, "V", ev_xn[tt - 2])
                    ev_xn[(tt, "ld")] = ctx.b16(sync.dma_start(
                        out=xt[:, :], in_=x_d[tt * 128:(tt + 1) * 128, :]),
                        "dX0" if tt % 2 == 0 else "dX1")
                B.sync(ld)

                def st1(v, tt=tt, xt=xt):
                    ctx.w(v, "dX0" if tt % 2 == 0 else "dX1",
                          ev_xn[(tt, "ld")])
                    v.tensor_mul(xsq[:, :], xt[:, :], xt[:, :])
                    v.drain()
                    v.reduce_sum(stats[:, 0:1], xt[:, :], AX.X)
                    v.reduce_sum(stats[:, 1:2], xsq[:, :], AX.X)
                    v.drain()
                    v.tensor_scalar(stats[:, 2:3], stats[:, 0:1],
                                    1.0 / C, None, ALU.mult)
                    v.drain()
                    v.tensor_mul(stats[:, 3:4], stats[:, 2:3], stats[:, 2:3])
                    v.drain()
                    v.scalar_tensor_tensor(stats[:, 4:5], stats[:, 1:2],
                                           1.0 / C, stats[:, 3:4],
                                           ALU.mult, ALU.subtract)
                    v.drain()
                    v.tensor_single_scalar(stats[:, 4:5], stats[:, 4:5],
                                           1e-5, ALU.add)
                    v.drain()
                    v.reciprocal(stats[:, 5:6], stats[:, 4:5])
                    ctx.b(v.drain(), "V")
                B.vector(st1)

                vcap = ctx.cnt["V"]
                s_off[0] += 1
                prom = s_base + s_off[0]

                def sq(s, vcap=vcap, prom=prom):
                    s.wait_ge(ctx.sem["V"], vcap)
                    got = ctx.b(s.activation(stats[:, 6:7], stats[:, 5:6],
                                             AF.Sqrt, bias=0.0, scale=1.0),
                                "S")
                    assert got == prom
                sched.append(sq)

                def st3(v, prom=prom):
                    ctx.w(v, "S", prom)
                    v.tensor_scalar(stats[:, 7:8], stats[:, 2:3], -1.0, None,
                                    ALU.mult)
                    ctx.b(v.drain(), "V")
                B.vector(st3)

                def xn_act(v, tt=tt, xt=xt):
                    if tt >= 1:
                        ctx.w(v, "T", ev_tp[tt - 1])
                    ev_xn[tt] = ctx.b(v.tensor_scalar(
                        xnf[:, :], xt[:, :], stats[:, 7:8], stats[:, 6:7],
                        ALU.add, ALU.mult), "V")
                B.vector(xn_act)

                def m1p(v, tt=tt):
                    v.drain()
                    ctx.b(v.reduce_max(macc[:, tt:tt + 1], xnf[:, :], AX.X,
                                       apply_absolute_value=True), "V")
                B.vector(m1p)

                next(ag, None)

                def tp(t, tt=tt):
                    ctx.w(t, "V", ev_xn[tt])
                    if tt >= 1:
                        ctx.w(t, "V", ev_ev[tt - 1])
                    for bk in range(2):
                        pb = ps3 if bk == 0 else ps4
                        for kx in range(4):
                            ci = 4 * bk + kx
                            ins = t.transpose(pb[:, kx * 128:(kx + 1) * 128],
                                              xnf[:, ci * 128:(ci + 1) * 128],
                                              identf[:, :])
                    ev_tp[tt] = ctx.b(ins, "T")
                B.tensor(tp)

                def ev(v, tt=tt):
                    ctx.w(v, "T", ev_tp[tt])
                    for bk in range(2):
                        dst = AP(xnT, (4 * bk) * TOK + tt * 128,
                                 [[CCH * TOK, 128], [TOK, 4], [1, 128]])
                        ins = v.tensor_copy(dst, AP(
                            scps, 512 + bk * 512,
                            [[2048, 128], [128, 4], [1, 128]]))
                    ev_ev[tt] = ctx.b(ins, "V")
                B.vector(ev)

                next(ag, None)
                next(cg, None)

            for _ in cg:
                pass
            for _ in ag:
                pass

            def ln_sbody(s):
                for fn in sched:
                    fn(s)
            B.scalar(ln_sbody)

            def m1fin(v):
                v.drain()
                v.reduce_max(stats[:, 8:9], macc[:, 0:NTT], AX.X)
                ctx.b(v.drain(), "V")
            B.vector(m1fin)

            def m1g(g):
                ctx.w(g, "V")
                ctx.b16(g.dma_start(out=gath[:, :], in_=stats[:, 8:9]), "dMG")
            B.gpsimd(m1g)

            def m1f2(v):
                ctx.w(v, "dMG")
                v.memset(scal[:, :], 0.0)
                v.memset(sres[:, :], 0.0)
                v.drain()
                v.reduce_max(scal[0:1, 0:1], gath[0:1, :], AX.X)
                ctx.b(v.drain(), "V")
            B.vector(m1f2)
            ctx.barrier()

            collective(0, c1_math, [(0, 1)])

            with contextlib.ExitStack() as _es3:
                wvw = _es3.enter_context(al("wvw", [128, 2 * CCH * 512], BF16))
                wqkp0 = _es3.enter_context(al("wqkp0", [128, CCH * 256], BF16))
                wqkp1 = _es3.enter_context(al("wqkp1", [128, CCH * 256], BF16))
                qkev = _es3.enter_context(al("qkev", [128, 512], BF16))
                qkev2 = _es3.enter_context(al("qkev2", [128, 512], BF16))
                qkev3 = _es3.enter_context(al("qkev3", [128, 512], BF16))
                qkev4 = _es3.enter_context(al("qkev4", [128, 512], BF16))

                def ldv(sync):
                    for cc in range(CCH):
                        for vg in range(2):
                            src = AP(wqkb_d.tensor,
                                     (8 * vg) * C * 192 + (cc * 128) * 192 + 128,
                                     [[192, 128], [C * 192, 8], [1, 64]])
                            ctx.b16(sync.dma_start(
                                out=wvw[:, (cc * 2 + vg) * 512:(cc * 2 + vg + 1) * 512],
                                in_=src), "dVW")
                B.sync(ldv)

                ev_vev, ev_vst = {}, {}
                k = 0
                for sq in range(SEQ):
                    for vg in range(2):
                        for tt in range(16):
                            pb = ps0 if k % 2 == 0 else ps1

                            def mm(t, sq=sq, vg=vg, tt=tt, pb=pb, k=k):
                                ctx.w(t, "dVW", 256)
                                if k >= 2:
                                    ctx.w(t, "V", ev_vev[k - 2])
                                gt = sq * 16 + tt
                                for cc in range(CCH):
                                    ins = t.matmul(
                                        pb[:, :],
                                        xnT[:, cc * TOK + gt * 128:cc * TOK + (gt + 1) * 128],
                                        wvw[:, (cc * 2 + vg) * 512:(cc * 2 + vg + 1) * 512],
                                        start=(cc == 0), stop=(cc == CCH - 1))
                                ctx.b(ins, "T")
                            B.tensor(mm)

                            def ev(v, pb=pb, k=k):
                                ctx.w(v, "T")
                                qv = qkev if k % 2 == 0 else qkev2
                                if k >= 2:
                                    ctx.w(v, "dVS0" if k % 2 == 0 else "dVS1",
                                          ev_vst[k - 2])
                                ev_vev[k] = ctx.b(v.tensor_copy(qv[:, :], pb[:, :]),
                                                  "V")
                            B.vector(ev)

                            def st(g, sq=sq, vg=vg, tt=tt, k=k):
                                ctx.w(g, "V", ev_vev[k])
                                qv = qkev if k % 2 == 0 else qkev2
                                dst = AP(v_sp.tensor,
                                         sq * 128 * (16 * C) + tt * 1024 + vg * 512,
                                         [[16 * C, 128], [1, 512]])
                                ev_vst[k] = ctx.b16(
                                    g.dma_start(out=dst, in_=qv[:, :]),
                                    "dVS0" if k % 2 == 0 else "dVS1")
                            B.gpsimd(st)
                            next(bg, None)
                            k += 1

                ev_qkev, ev_qkst, ev_qk3, ev_qkT = {}, {}, {}, {}

                def ldqk(sync, spr):
                    sq, pr = divmod(spr, 8)
                    wqkp = wqkp0 if spr % 2 == 0 else wqkp1
                    if spr >= 2:
                        ctx.w(sync, "T", ev_qkT[spr * 4 - 5])
                    elif spr == 0:
                        ctx.w(sync, "T")
                    dsm = "dQW" if spr % 2 == 0 else "dQW1"
                    for cc in range(CCH):
                        src_ = AP(wqkb_d.tensor,
                                  (2 * pr) * C * 192 + (cc * 128) * 192,
                                  [[192, 128], [C * 192, 2], [1, 128]])
                        ctx.b16(sync.dma_start(
                            out=wqkp[:, cc * 256:(cc + 1) * 256],
                            in_=src_), dsm)
                    ev_qkev[(spr, "ld")] = ctx.cnt[dsm]

                kk = 0
                for sq in range(SEQ):
                    for pr in range(8):
                        spr = sq * 8 + pr
                        if spr == 0:
                            B.sync(lambda sync: ldqk(sync, 0))
                        if spr + 1 < 16:
                            B.sync(lambda sync, spr=spr: ldqk(sync, spr + 1))
                        next(bg, None)
                        wqkp = wqkp0 if spr % 2 == 0 else wqkp1

                        for tr in range(4):
                            if kk % 2 == 0:
                                bq0, bq1, bk0 = ps0, ps2, ps1
                                qa, qb = qkev, qkev2
                            else:
                                bq0, bq1, bk0 = ps4, ps5, ps7
                                qa, qb = qkev3, qkev4

                            def mm(t, sq=sq, spr=spr, tr=tr, kk=kk,
                                   wqkp=wqkp, bq0=bq0, bq1=bq1, bk0=bk0):
                                ctx.w(t, "dQW" if spr % 2 == 0 else "dQW1",
                                      ev_qkev[(spr, "ld")])
                                if kk >= 1:
                                    ctx.w(t, "V", ev_qk3[kk - 1])
                                else:
                                    ctx.w(t, "V")
                                if kk >= 2:
                                    ctx.w(t, "V", ev_qkev[kk - 2])
                                gt = sq * 4 + tr
                                for cc in range(CCH):
                                    rhs = xnT[:, cc * TOK + gt * 512:cc * TOK + (gt + 1) * 512]
                                    s_ = (cc == 0)
                                    p_ = (cc == CCH - 1)
                                    t.matmul(bq0[0:64, :],
                                             wqkp[:, cc * 256:cc * 256 + 64],
                                             rhs, start=s_, stop=p_,
                                             tile_position=(0, 0))
                                    t.matmul(bq1[64:128, :],
                                             wqkp[:, cc * 256 + 128:cc * 256 + 192],
                                             rhs, start=s_, stop=p_,
                                             tile_position=(0, 64))
                                    t.matmul(bk0[0:64, :],
                                             wqkp[:, cc * 256 + 64:cc * 256 + 128],
                                             rhs, start=s_, stop=p_,
                                             tile_position=(0, 0))
                                    ins = t.matmul(ps3[64:128, :],
                                                   wqkp[:, cc * 256 + 192:cc * 256 + 256],
                                                   rhs, start=s_, stop=p_,
                                                   tile_position=(0, 64))
                                ev_qkT[kk] = ctx.b(ins, "T")
                            B.tensor(mm)

                            def ev(v, kk=kk, bq0=bq0, bq1=bq1, bk0=bk0,
                                   qa=qa, qb=qb):
                                ctx.w(v, "T")
                                if kk >= 2:
                                    ctx.w(v, "dVS0" if kk % 2 == 0 else "dVS1",
                                          ev_qkst[kk - 2])
                                else:
                                    ctx.w(v, "dVS0")
                                    ctx.w(v, "dVS1")
                                ev_qk3[kk] = ctx.b(
                                    v.tensor_copy(qb[64:128, :],
                                                  ps3[64:128, :]), "V")
                                v.tensor_copy(qa[0:64, :], bq0[0:64, :])
                                v.tensor_copy(qa[64:128, :], bq1[64:128, :])
                                ev_qkev[kk] = ctx.b(
                                    v.tensor_copy(qb[0:64, :],
                                                  bk0[0:64, :]), "V")
                            B.vector(ev)

                            def st(g, sq=sq, pr=pr, tr=tr, kk=kk,
                                   qa=qa, qb=qb):
                                ctx.w(g, "V", ev_qkev[kk])
                                base = ((sq * 8 + pr) * 2) * 128 * T
                                dq = AP(qk_sp.tensor, base + tr * 512,
                                        [[T, 128], [1, 512]])
                                dk = AP(qk_sp.tensor, base + 128 * T + tr * 512,
                                        [[T, 128], [1, 512]])
                                dsm = "dVS0" if kk % 2 == 0 else "dVS1"
                                ctx.b16(g.dma_start(out=dq, in_=qa[:, :]), dsm)
                                ev_qkst[kk] = ctx.b16(
                                    g.dma_start(out=dk, in_=qb[:, :]), dsm)
                            B.gpsimd(st)
                            next(bg, None)
                            kk += 1

                for _ in bg:
                    pass
                ctx.barrier()

        _esP4b = contextlib.ExitStack()
        ub0 = _esP4b.enter_context(al("ub0", [128, C], F32))
        ub1 = _esP4b.enter_context(al("ub1", [128, C], F32))
        ev_i2 = {}
        rx_lams = []
        RX0 = ctx.cnt["S"]
        for j in range(NTT):
            ub = ub0 if j % 2 == 0 else ub1

            def mk_i1(j=j, ub=ub, vprev=ev_i2.get(j - 2, 0)):
                def f(s):
                    if j == 0:
                        ctx.w(s, "G", ev_bc[0])
                    if vprev > 0:
                        s.wait_ge(ctx.sem["V"], vprev)
                    got = ctx.b(s.activation(
                        ub[:, :], xnT[:, j * 1024:(j + 1) * 1024],
                        AF.Copy, bias=MAGIC, scale=bc[:, 0:1]), "S")
                    assert got == RX0 + j + 1
                return f
            rx_lams.append(mk_i1())

            def i2(v, j=j, ub=ub):
                ctx.w(v, "S", RX0 + j + 1)
                ev_i2[j] = ctx.b(v.tensor_single_scalar(
                    xnT[:, j * 1024:(j + 1) * 1024], ub[:, :],
                    -MAGIC, ALU.add), "V")
            B.vector(i2)

        def rx_sbody(s):
            for f in rx_lams:
                f(s)
        B.scalar(rx_sbody)
        ctx.barrier()
        _esP4b.close()

        ffn_state = dict(ke=0, km=0)
        ffn_ev_mm, ffn_ev_ev2, ffn_ev_st2 = {}, {}, {}
        nU = 16 * ICH

        def ffn_pump_ev():
            k = ffn_state["ke"]
            if k >= ffn_state["km"] or k >= nU:
                return
            ffn_state["ke"] += 1
            gw, t8 = divmod(k, 8)
            sp_dst = g_sp if gw < ICH else u_sp
            ic = gw % ICH
            mcol = 32 + k if k < 8 * ICH else 224 + (k - 8 * ICH)

            def ev2(v, gw=gw, t8=t8, k=k, mcol=mcol):
                ctx.w(v, "T", ffn_ev_mm[(gw, t8)])
                sg = stg0 if k % 2 == 0 else stg1
                if k >= 2:
                    ctx.w(v, "dFS0" if k % 2 == 0 else "dFS1",
                          ffn_ev_st2[k - 2])
                v.tensor_copy(sg[:, 0:512], ps0[:, :])
                ffn_ev_ev2[k] = ctx.b(v.reduce_max(
                    macc[:, mcol:mcol + 1],
                    ps0[:, :], AX.X,
                    apply_absolute_value=True), "V")
            B.vector(ev2)

            def st2(g, ic=ic, t8=t8, k=k, sp_dst=sp_dst):
                ctx.w(g, "V", ffn_ev_ev2[k])
                sg = stg0 if k % 2 == 0 else stg1
                dst = AP(sp_dst.tensor,
                         ic * 128 * TOK + t8 * 512,
                         [[TOK, 128], [1, 512]])
                ffn_ev_st2[k] = ctx.b16(
                    g.dma_start(out=dst, in_=sg[:, 0:512]),
                    "dFS0" if k % 2 == 0 else "dFS1")
            B.gpsimd(st2)

        def ffn_pump_mm():
            k = ffn_state["km"]
            if k >= nU or ffn_state["ke"] < k:
                return
            ffn_state["km"] += 1
            gw, t8 = divmod(k, 8)
            wq_src = wgq_d if gw < ICH else wuq_d
            ic = gw % ICH
            wql = wql0 if gw % 2 == 0 else wql1

            if t8 == 0:
                def ldw(sync, gw=gw, ic=ic, wql=wql, wq_src=wq_src):
                    if gw >= 2:
                        ctx.w(sync, "T", ffn_ev_mm[(gw - 2, 7)])
                    dsm = "dFW0" if gw % 2 == 0 else "dFW1"
                    for cc in range(CCH):
                        ctx.b16(sync.dma_start(
                            out=wql[:, cc * 128:(cc + 1) * 128],
                            in_=wq_src[cc * 128:(cc + 1) * 128,
                                       ic * 128:(ic + 1) * 128]), dsm)
                    ffn_ev_mm[(gw, "ld")] = ctx.cnt[dsm]
                B.sync(ldw)

            def mm(t, gw=gw, t8=t8, k=k, wql=wql):
                ctx.w(t, "dFW0" if gw % 2 == 0 else "dFW1",
                      ffn_ev_mm[(gw, "ld")])
                if k >= 1:
                    ctx.w(t, "V", ffn_ev_ev2[k - 1])
                for cc in range(CCH):
                    ins = t.matmul(
                        ps0[:, :], wql[:, cc * 128:(cc + 1) * 128],
                        xnT[:, cc * TOK + t8 * 512:cc * TOK + (t8 + 1) * 512],
                        start=(cc == 0), stop=(cc == CCH - 1))
                ffn_ev_mm[(gw, t8)] = ctx.b(ins, "T")
            B.tensor(mm)


        with contextlib.ExitStack() as _es4:
            v_sb = _es4.enter_context(al("v_sb", [128, 16 * C], BF16))
            atT = _es4.enter_context(al("atT", [128, 8 * T], BF16))
            qp = _es4.enter_context(al("qp", [128, T], BF16))
            kp = _es4.enter_context(al("kp", [128, T], BF16))
            Pa0 = _es4.enter_context(al("Pa0", [128, T], BF16))
            Pb0 = _es4.enter_context(al("Pb0", [128, T], BF16))
            Pa1 = _es4.enter_context(al("Pa1", [128, T], BF16))
            Pb1 = _es4.enter_context(al("Pb1", [128, T], BF16))
            PT0 = _es4.enter_context(al("PT0", [128, 1024], BF16))
            PT1 = _es4.enter_context(al("PT1", [128, 1024], BF16))
            cmx = _es4.enter_context(al("cmx", [128, 32], F32))
            dnp = _es4.enter_context(al("dnp", [128, 32], F32))
            wp_sb = _es4.enter_context(al("wp_sb", [128, CCH * 1024], BF16))
            def ld_wp(sync):
                for ccb in range(CCH):
                    ctx.b16(sync.dma_start(
                        out=wp_sb[:, ccb * 1024:(ccb + 1) * 1024],
                        in_=wpb_d[ccb * 128:(ccb + 1) * 128, :]), "dPW")
            B.sync(ld_wp)

            for sq in range(SEQ):
                def ldv2(sync, sq=sq):
                    ctx.w(sync, "T")
                    src = AP(v_sp.tensor, sq * 128 * 16 * C,
                             [[16 * C, 128], [1, 16 * C]])
                    ctx.b16(sync.dma_start(out=v_sb[:, :], in_=src), "dVB")
                B.sync(ldv2)
                ev_vb = ctx.cnt["dVB"]

                for pr in range(8):
                    def ldqk2(sync, sq=sq, pr=pr):
                        ctx.w(sync, "T")
                        base = ((sq * 8 + pr) * 2) * 128 * T
                        ctx.b16(sync.dma_start(
                            out=qp[:, :],
                            in_=AP(qk_sp.tensor, base,
                                   [[T, 128], [1, T]])), "dQK")
                        ctx.b16(sync.dma_start(
                            out=kp[:, :],
                            in_=AP(qk_sp.tensor, base + 128 * T,
                                   [[T, 128], [1, T]])), "dQK")
                    B.sync(ldqk2)
                    ev_qk2 = ctx.cnt["dQK"]

                    S0 = ctx.cnt["S"]

                    def sprom(qt, hh):
                        return S0 + 2 * qt + hh + 1 if qt >= 0 else S0

                    s_lams = []
                    ev_nrm = {}
                    ev_tp = {}
                    nrm_pend = []
                    for qt in range(17):
                      ffn_pump_ev()
                      if qt < 16:
                        nsr = qt // 4 + 1
                        last_n = (qt % 4 + 1) * 128
                        for hh in range(2):
                            b0 = 64 * hh
                            Ph = (Pa0 if hh == 0 else Pb0) if qt % 2 == 0 \
                                else (Pa1 if hh == 0 else Pb1)
                            slot = (qt % 2) * 2 + hh
                            if nsr <= 2:
                                scbase = 0 if hh == 0 else 1024
                                bankfree = sprom(qt - 1, hh)
                            else:
                                scbase = 0
                                bankfree = sprom(qt, hh - 1) if hh == 1 \
                                    else sprom(qt - 1, 1)
                            qw = (qt + 1) * 128

                            def mm(t, b0=b0, qt=qt, nsr=nsr, last_n=last_n,
                                   ev_qk2=ev_qk2, scbase=scbase,
                                   bankfree=bankfree):
                                ctx.w(t, "dQK", ev_qk2)
                                ctx.w(t, "S", bankfree)
                                for sr in range(nsr):
                                    n = 512 if sr < nsr - 1 else last_n
                                    ins = t.matmul(
                                        scps[:, scbase + sr * 512:scbase + sr * 512 + n],
                                        qp[b0:b0 + 64, qt * 128:(qt + 1) * 128],
                                        kp[b0:b0 + 64, sr * 512:sr * 512 + n],
                                        start=True, stop=True)
                                ctx.b(ins, "T")
                            B.tensor(mm)

                            def msk(v, qt=qt, hh=hh, qw=qw, scbase=scbase,
                                    slot=slot):
                                ctx.w(v, "T")
                                ctx.w(v, "S", sprom(qt - 2, hh))
                                v.tensor_add(scps[:, scbase + qw - 128:scbase + qw],
                                             scps[:, scbase + qw - 128:scbase + qw],
                                             maskneg[:, :])
                                v.drain()
                                ctx.b(v.reduce_max(
                                    cmx[:, slot * 8 + 6:slot * 8 + 7],
                                    scps[:, scbase:scbase + qw],
                                    AX.X, negate=True), "V")
                            B.vector(msk)

                            vev_cap = ctx.cnt["V"]
                            tev_cap = ev_tp.get(qt - 2, 0)

                            def ex(s, qt=qt, hh=hh, qw=qw, Ph=Ph, slot=slot,
                                   scbase=scbase, vev=vev_cap, tev=tev_cap):
                                if vev > 0:
                                    s.wait_ge(ctx.sem["V"], vev)
                                if tev > 0:
                                    s.wait_ge(ctx.sem["T"], tev)
                                ins = s.activation(
                                    Ph[:, 0:qw],
                                    scps[:, scbase:scbase + qw], AF.Exp,
                                    bias=cmx[:, slot * 8 + 6:slot * 8 + 7],
                                    scale=1.0,
                                    accum_out=dnp[:, slot * 8:slot * 8 + 1])
                                got = ctx.b(ins, "S")
                                assert got == sprom(qt, hh), (got, qt, hh)
                            s_lams.append(ex)

                            nrm_pend.append((qt, Ph, hh, slot))

                      ffn_pump_mm()

                      if qt >= 1:
                        while nrm_pend and nrm_pend[0][0] == qt - 1:
                            (nqt, nPh, nhh, nslot) = nrm_pend.pop(0)

                            def nrm(v, qt=nqt, Ph=nPh, hh=nhh, slot=nslot):
                                ctx.w(v, "S", sprom(qt, hh))
                                v.reciprocal(dnp[:, slot * 8 + 7:slot * 8 + 8],
                                             dnp[:, slot * 8:slot * 8 + 1])
                                v.drain()
                                ev_nrm[(qt, hh)] = ctx.b(v.tensor_scalar(
                                    Ph[:, 0:(qt + 1) * 128],
                                    Ph[:, 0:(qt + 1) * 128],
                                    dnp[:, slot * 8 + 7:slot * 8 + 8],
                                    None, ALU.mult), "V")
                            B.vector(nrm)
                        pqt = qt - 1
                        nbat = (pqt + 4) // 4
                        PhA = Pa0 if pqt % 2 == 0 else Pa1
                        PhB = Pb0 if pqt % 2 == 0 else Pb1
                        for bt in range(nbat):
                            scs = list(range(bt * 4, min(pqt + 1, bt * 4 + 4)))
                            PTb = PT0 if bt % 2 == 0 else PT1

                            def tp(t, pqt=pqt, scs=scs, PhA=PhA, PhB=PhB):
                                ctx.w(t, "V")
                                for i, sc in enumerate(scs):
                                    for hh in range(2):
                                        Ph = PhA if hh == 0 else PhB
                                        ins = t.transpose(
                                            ps6[:, (i * 2 + hh) * 128:(i * 2 + hh + 1) * 128],
                                            Ph[:, sc * 128:(sc + 1) * 128],
                                            identb[:, :])
                                ev_tp[pqt] = ctx.b(ins, "T")
                            B.tensor(tp)

                            def pev(v, PTb=PTb, scs=scs):
                                ctx.w(v, "T")
                                n = len(scs) * 256
                                ctx.b(v.tensor_copy(PTb[:, 0:n], ps6[:, 0:n]),
                                      "V")
                            B.vector(pev)

                            def av(t, pr=pr, pqt=pqt, PTb=PTb, scs=scs,
                                   ev_vb=ev_vb, S0=S0):
                                ctx.w(t, "dVB", ev_vb)
                                if pqt == 0 and pr == 0:
                                    ctx.w(t, "S", S0)
                                ctx.w(t, "V")
                                reg = (pqt % 4) * 128
                                for i, sc in enumerate(scs):
                                    for hh in range(2):
                                        h = 2 * pr + hh
                                        avb = ps7 if hh == 0 else ps1
                                        vcol = sc * 1024 + (h // 8) * 512 + (h % 8) * 64
                                        ins = t.matmul(
                                            avb[64 * hh:64 * hh + 64,
                                                reg:reg + 128],
                                            v_sb[:, vcol:vcol + 64],
                                            PTb[:, (i * 2 + hh) * 128:(i * 2 + hh) * 128 + 128],
                                            start=(sc == 0), stop=(sc == pqt),
                                            tile_position=(0, 64 * hh))
                                ev_tp[pqt] = ctx.b(ins, "T")
                            B.tensor(av)

                        if pqt % 4 == 3:
                            qr = pqt // 4

                            def aev(v, pr=pr, qr=qr):
                                ctx.w(v, "T")
                                v.tensor_copy(
                                    atT[0:64, pr * T + qr * 512:pr * T + qr * 512 + 512],
                                    ps7[0:64, :])
                                ctx.b(v.tensor_copy(
                                    atT[64:128, pr * T + qr * 512:pr * T + qr * 512 + 512],
                                    ps1[64:128, :]), "V")
                            B.vector(aev)


                    def sbody(s, s_lams=s_lams):
                        for f in s_lams:
                            f(s)
                    B.scalar(sbody)
                    assert ctx.cnt["S"] == S0 + 32

                ffn_pump_ev()

                ev_pst = {}
                kk = 0
                prj_lams = []
                PS0 = ctx.cnt["S"]
                for tt in range(16):
                    for ch in range(2):
                        pb = ps0 if kk % 2 == 0 else ps1

                        def mm(t, sq=sq, tt=tt, ch=ch, pb=pb, kk=kk):
                            ctx.w(t, "dPW", 128)
                            ctx.w(t, "V")
                            if kk >= 2:
                                ctx.w(t, "S", PS0 + kk - 1)
                            t.matmul(pb[:, :], ones1[0:1, :],
                                     bprojb[0:1, ch * 512:(ch + 1) * 512],
                                     start=True, stop=False)
                            for pr in range(8):
                                ins = t.matmul(
                                    pb[:, :],
                                    atT[:, pr * T + tt * 128:pr * T + (tt + 1) * 128],
                                    wp_sb[:, pr * 1024 + ch * 512:pr * 1024 + ch * 512 + 512],
                                    start=False, stop=(pr == 7))
                            ctx.b(ins, "T")
                        B.tensor(mm)

                        def mk_pev(kk=kk, pb=pb,
                                   tcap=None, pstprev=None):
                            def f(s, tcap=ctx.cnt["T"],
                                  pstprev=ev_pst.get(
                                      kk - 2,
                                      ctx.cnt["dPS0" if kk % 2 == 0
                                              else "dPS1"])):
                                s.wait_ge(ctx.sem["T"], tcap)
                                sh = stgh0 if kk % 2 == 0 else stgh1
                                ctx.w(s, "dPS0" if kk % 2 == 0 else "dPS1",
                                      pstprev)
                                got = ctx.b(s.activation(
                                    sh[:, :], pb[:, :], AF.Copy,
                                    bias=0.0, scale=1.0), "S")
                                assert got == PS0 + kk + 1
                            return f
                        prj_lams.append(mk_pev())

                        def pst(g, sq=sq, tt=tt, ch=ch, kk=kk):
                            ctx.w(g, "S", PS0 + kk + 1)
                            sh = stgh0 if kk % 2 == 0 else stgh1
                            dst = AP(ao_sp.tensor,
                                     (sq * 16 + tt) * 128 * C + ch * 512,
                                     [[C, 128], [1, 512]])
                            ev_pst[kk] = ctx.b16(
                                g.dma_start(out=dst, in_=sh[:, :]),
                                "dPS0" if kk % 2 == 0 else "dPS1")
                        B.gpsimd(pst)
                        kk += 1

                def prj_sbody(s, prj_lams=prj_lams):
                    for f in prj_lams:
                        f(s)
                B.scalar(prj_sbody)
            ctx.barrier()

        with contextlib.ExitStack() as _es5:
            ub2 = _es5.enter_context(al("ub2", [128, C], F32))
            ub6 = _es5.enter_context(al("ub6", [128, C], F32))
            ub7 = _es5.enter_context(al("ub7", [128, C], F32))
            stgc0 = _es5.enter_context(al("stgc0", [128, 1024], BF16))
            stgc1 = _es5.enter_context(al("stgc1", [128, 1024], BF16))
            ub3 = _es5.enter_context(al("ub3", [128, C], F32))
            ub4 = _es5.enter_context(al("ub4", [128, C], F32))
            ub5 = _es5.enter_context(al("ub5", [128, C], F32))
            ldt0 = _es5.enter_context(al("ldt0", [128, 1024], BF16))
            ldt1 = _es5.enter_context(al("ldt1", [128, 1024], BF16))
            ldt2 = _es5.enter_context(al("ldt2", [128, 1024], BF16))
            ldt3 = _es5.enter_context(al("ldt3", [128, 1024], BF16))
            rob = _es5.enter_context(al("rob", [128, ICH * 512], BF16))
            rob2 = _es5.enter_context(al("rob2", [128, ICH * 512], BF16))
            def mfin(col0, ncols, slot, clear):
                def f1(v):
                    v.drain()
                    v.reduce_max(stats[:, 9:10], macc[:, col0:col0 + ncols],
                                 AX.X)
                    ctx.b(v.drain(), "V")
                B.vector(f1)

                def f2(g):
                    ctx.w(g, "V")
                    ctx.b16(g.dma_start(out=gath[:, :], in_=stats[:, 9:10]),
                            "dMG")
                B.gpsimd(f2)

                def f3(v, slot=slot, clear=clear):
                    ctx.w(v, "dMG")
                    if clear:
                        v.memset(scal[:, :], 0.0)
                        v.drain()
                    v.reduce_max(scal[0:1, slot:slot + 1], gath[0:1, :], AX.X)
                    ctx.b(v.drain(), "V")
                B.vector(f3)

            ffn_pump_ev()
            mfin(32, 192, 0, True)
            ctx.barrier()

            def c2_math(v):
                v.tensor_mul(sres[0:1, 10:11], sres[0:1, 0:1], scalg[0:1, 0:1])
                v.drain()
                v.tensor_scalar(sres[0:1, 2:3], sres[0:1, 10:11],
                                1.0 / Q15, 1e-12, ALU.mult, ALU.max)
                v.drain()
                v.reciprocal(sres[0:1, 11:12], sres[0:1, 2:3])
                v.drain()
                v.tensor_mul(sres[0:1, 3:4], sres[0:1, 0:1], sres[0:1, 11:12])

            collective(1, c2_math, [(1, 3), (2, 2)])

            ntile = ICH * 4
            ev_ld3, ev_v1, ev_v2, ev_st3 = {}, {}, {}, {}
            sig_lams = []
            S8 = ctx.cnt["S"]
            for k in range(ntile):
                ffn_pump_ev()
                ffn_pump_mm()
                ic, tc = divmod(k, 4)
                gt_ = ldt0 if k % 2 == 0 else ldt1
                uu = ub2 if k % 2 == 0 else ub3
                rg_ = ub4 if k % 2 == 0 else ub5
                ss = ub6 if k % 2 == 0 else ub7

                def ld3(sync, ic=ic, tc=tc, gt_=gt_, k=k):
                    if k >= 2:
                        ctx.w(sync, "V", ev_v2.get(k - 2, 0))
                    dsm = "dG0" if k % 2 == 0 else "dG1"
                    ctx.b16(sync.dma_start(
                        out=gt_[:, :],
                        in_=AP(g_sp.tensor, ic * 128 * TOK + tc * 1024,
                               [[TOK, 128], [1, 1024]])), dsm)
                    ev_ld3[k] = ctx.cnt[dsm]
                B.sync(ld3)

                def v1f(v, k=k, gt_=gt_, uu=uu, rg_=rg_):
                    if k == 0:
                        ctx.w(v, "G", ev_bc[0])
                    ctx.w(v, "dG0" if k % 2 == 0 else "dG1", ev_ld3[k])
                    if k >= 2:
                        ctx.w(v, "S", S8 + k - 1)
                    v.tensor_scalar(uu[:, :], gt_[:, :], bc[:, 1:2], MAGIC,
                                    ALU.mult, ALU.add)
                    v.drain()
                    ev_v1[k] = ctx.b(v.tensor_single_scalar(
                        rg_[:, :], uu[:, :], -MAGIC, ALU.add), "V")
                B.vector(v1f)

                vcap1 = ctx.cnt["V"]
                vcap2 = ev_v2.get(k - 2, 0)
                sig_lams.append((vcap1, vcap2, k, rg_, ss))

                def v2a(v, k=k, uu=uu, ss=ss):
                    ctx.w(v, "S", S8 + k + 1)
                    v.tensor_scalar(uu[:, :], ss[:, :], 65536.0, MAGIC,
                                    ALU.mult, ALU.add)
                    v.drain()
                    ctx.b(v.tensor_scalar(ss[:, :], uu[:, :], -MAGIC, 65535.0,
                                          ALU.add, ALU.min), "V")
                B.vector(v2a)

                def g2f(g, k=k, ss=ss, rg_=rg_):
                    ctx.w(g, "V")
                    if k >= 2:
                        ctx.w(g, "dTG0" if k % 2 == 0 else "dTG1",
                              ev_st3.get(k - 2, 0))
                    sg = stgc0 if k % 2 == 0 else stgc1
                    ctx.b(g.tensor_mul(sg[:, :], rg_[:, :], ss[:, :]), "G")
                B.gpsimd(g2f)

                def v2b(v, k=k):
                    ctx.w(v, "G")
                    sg = stgc0 if k % 2 == 0 else stgc1
                    ev_v2[k] = ctx.b(v.reduce_max(
                        macc[:, 32 + k:33 + k], sg[:, :], AX.X,
                        apply_absolute_value=True), "V")
                B.vector(v2b)

                def st3(g, k=k, ic=ic, tc=tc):
                    ctx.w(g, "V", ev_v2[k])
                    sg = stgc0 if k % 2 == 0 else stgc1
                    ev_st3[k] = ctx.b16(g.dma_start(
                        out=AP(pt_sp.tensor, ic * 128 * TOK + tc * 1024,
                               [[TOK, 128], [1, 1024]]),
                        in_=sg[:, :]), "dTG0" if k % 2 == 0 else "dTG1")
                B.gpsimd(st3)

            def sig_body(s):
                for (vcap1, vcap2, k, rg_, ss) in sig_lams:
                    s.wait_ge(ctx.sem["V"], vcap1)
                    if vcap2 > 0:
                        s.wait_ge(ctx.sem["V"], vcap2)
                    got = ctx.b(s.activation(ss[:, :], rg_[:, :], AF.Sigmoid,
                                             bias=0.0, scale=bc[:, 2:3]), "S")
                    assert got == S8 + k + 1
            B.scalar(sig_body)
            mfin(32, ntile, 1, True)

            while ffn_state["km"] < nU or ffn_state["ke"] < nU:
                ffn_pump_ev()
                ffn_pump_mm()
            mfin(224, 192, 0, False)
            ctx.barrier()

            def c3_math(v):
                v.tensor_mul(sres[0:1, 10:11], sres[0:1, 0:1], scalg[0:1, 0:1])
                v.drain()
                v.tensor_scalar(sres[0:1, 4:5], sres[0:1, 10:11],
                                1.0 / Q15, 1e-12, ALU.mult, ALU.max)
                v.drain()
                v.reciprocal(sres[0:1, 11:12], sres[0:1, 4:5])
                v.drain()
                v.tensor_mul(sres[0:1, 5:6], sres[0:1, 0:1], sres[0:1, 11:12])
                v.tensor_scalar(sres[0:1, 12:13], sres[0:1, 2:3],
                                1.0 / 65536.0, None, ALU.mult)
                v.drain()
                v.tensor_mul(sres[0:1, 10:11], sres[0:1, 12:13],
                             scalg[0:1, 1:2])
                v.drain()
                v.tensor_scalar(sres[0:1, 6:7], sres[0:1, 10:11],
                                1.0 / Q15, 1e-12, ALU.mult, ALU.max)
                v.drain()
                v.reciprocal(sres[0:1, 11:12], sres[0:1, 6:7])
                v.drain()
                v.tensor_mul(sres[0:1, 7:8], sres[0:1, 12:13],
                             sres[0:1, 11:12])

            collective(2, c3_math, [(3, 5), (4, 7)])

            wdq_tot = [0]

            def wdpre(sync):
                for ic in range(ICH):
                    wdq_tot[0] = ctx.b16(sync.dma_start(
                        out=xnT[:, ic * 1024:(ic + 1) * 1024],
                        in_=wdq_d[ic * 128:(ic + 1) * 128, :]), "dW50")
            B.sync(wdpre)

            ev_ld4, ev_v3, ev_v4, ev_st4 = {}, {}, {}, {}
            p9_lams = []
            S9 = ctx.cnt["S"]
            for k in range(ntile):
                ic, tc = divmod(k, 4)
                ut_ = ldt0 if k % 2 == 0 else ldt1
                ptt = ldt2 if k % 2 == 0 else ldt3
                uu = ub2 if k % 2 == 0 else ub3
                ru_ = ub4 if k % 2 == 0 else ub5
                rt_ = ub6 if k % 2 == 0 else ub7

                def ld4(sync, ic=ic, tc=tc, ut_=ut_, ptt=ptt, k=k):
                    if k >= 2:
                        ctx.w(sync, "V", ev_v4.get(k - 2, 0))
                    dsm = "dG0" if k % 2 == 0 else "dG1"
                    ctx.b16(sync.dma_start(
                        out=ut_[:, :],
                        in_=AP(u_sp.tensor, ic * 128 * TOK + tc * 1024,
                               [[TOK, 128], [1, 1024]])), dsm)
                    ctx.b16(sync.dma_start(
                        out=ptt[:, :],
                        in_=AP(pt_sp.tensor, ic * 128 * TOK + tc * 1024,
                               [[TOK, 128], [1, 1024]])), dsm)
                    ev_ld4[k] = ctx.cnt[dsm]
                B.sync(ld4)

                def mk_s3(k=k, ut_=ut_, uu=uu,
                          tok=("dG0" if k % 2 == 0 else "dG1", 0),
                          vprev=ev_v4.get(k - 2, 0)):
                    def f(s, ldv=ev_ld4[k]):
                        if k == 0:
                            ctx.w(s, "G", ev_bc[0])
                        ctx.w(s, "dG0" if k % 2 == 0 else "dG1", ldv)
                        if vprev > 0:
                            s.wait_ge(ctx.sem["V"], vprev)
                        got = ctx.b(s.activation(
                            uu[:, :], ut_[:, :], AF.Copy,
                            bias=MAGIC, scale=bc[:, 3:4]), "S")
                        assert got == S9 + 2 * k + 1
                    return f
                p9_lams.append(mk_s3())

                def v3f(v, k=k, uu=uu, ru_=ru_):
                    ctx.w(v, "S", S9 + 2 * k + 1)
                    ev_v3[k] = ctx.b(v.tensor_single_scalar(
                        ru_[:, :], uu[:, :], -MAGIC, ALU.add), "V")
                B.vector(v3f)

                def mk_s4(k=k, ptt=ptt, uu=uu, vv=None):
                    def f(s, vv=ev_v3[k]):
                        s.wait_ge(ctx.sem["V"], vv)
                        got = ctx.b(s.activation(
                            uu[:, :], ptt[:, :], AF.Copy,
                            bias=MAGIC, scale=bc[:, 4:5]), "S")
                        assert got == S9 + 2 * k + 2
                    return f
                p9_lams.append(mk_s4())

                def v4a(v, k=k, uu=uu, rt_=rt_):
                    ctx.w(v, "S", S9 + 2 * k + 2)
                    ctx.b(v.tensor_single_scalar(rt_[:, :], uu[:, :],
                                                 -MAGIC, ALU.add), "V")
                B.vector(v4a)

                def g4f(g, k=k, ru_=ru_, rt_=rt_):
                    ctx.w(g, "V")
                    if k >= 2:
                        ctx.w(g, "dTG0" if k % 2 == 0 else "dTG1",
                              ev_st4.get(k - 2, 0))
                    else:
                        ctx.w(g, "dTG0" if k % 2 == 0 else "dTG1")
                    sg = stgc0 if k % 2 == 0 else stgc1
                    ctx.b(g.tensor_mul(sg[:, :], ru_[:, :], rt_[:, :]), "G")
                B.gpsimd(g4f)

                def v4b(v, k=k):
                    ctx.w(v, "G")
                    sg = stgc0 if k % 2 == 0 else stgc1
                    ev_v4[k] = ctx.b(v.reduce_max(
                        macc[:, 32 + k:33 + k], sg[:, :], AX.X,
                        apply_absolute_value=True), "V")
                B.vector(v4b)

                def st4(g, k=k, ic=ic, tc=tc):
                    ctx.w(g, "V", ev_v4[k])
                    sg = stgc0 if k % 2 == 0 else stgc1
                    ev_st4[k] = ctx.b16(g.dma_start(
                        out=AP(pu_sp.tensor, ic * 128 * TOK + tc * 1024,
                               [[TOK, 128], [1, 1024]]),
                        in_=sg[:, :]), "dTG0" if k % 2 == 0 else "dTG1")
                B.gpsimd(st4)

            def p9_sbody(s):
                for f in p9_lams:
                    f(s)
            B.scalar(p9_sbody)
            mfin(32, ntile, 0, True)
            ctx.barrier()

            def c4_math(v):
                v.tensor_mul(sres[0:1, 12:13], sres[0:1, 6:7], sres[0:1, 4:5])
                v.drain()
                v.tensor_mul(sres[0:1, 10:11], sres[0:1, 12:13],
                             scalg[0:1, 0:1])
                v.drain()
                v.tensor_scalar(sres[0:1, 8:9], sres[0:1, 10:11],
                                1.0 / Q15, 1e-12, ALU.mult, ALU.max)
                v.drain()
                v.reciprocal(sres[0:1, 11:12], sres[0:1, 8:9])
                v.drain()
                v.tensor_mul(sres[0:1, 9:10], sres[0:1, 12:13],
                             sres[0:1, 11:12])

            collective(3, c4_math, [(5, 9), (6, 8)])

            ev_ro, ev_x10, ev_ao, ev_dn, ev_cmb, ev_ost = {}, {}, {}, {}, {}, {}
            ev_ost_last = {0: 0, 1: 0}
            p10_lams = []
            S10 = ctx.cnt["S"]
            for tr8 in range(8):
                robc = rob if tr8 % 2 == 0 else rob2
                for ic in range(ICH):
                    kr = tr8 * ICH + ic
                    ldt = ldt0 if kr % 2 == 0 else ldt1
                    uu = ub2 if kr % 2 == 0 else ub3

                    def ld5(sync, tr8=tr8, ic=ic, ldt=ldt, kr=kr):
                        if kr >= 2:
                            ctx.w(sync, "S", ev_ro.get((kr - 2, "s"), 0))
                        if tr8 >= 2:
                            ctx.w(sync, "T", ev_dn.get((tr8 - 2, 1), 0))
                        dsm = "dR0" if kr % 2 == 0 else "dR1"
                        ctx.b16(sync.dma_start(
                            out=ldt[:, 0:512],
                            in_=AP(pu_sp.tensor, ic * 128 * TOK + tr8 * 512,
                                   [[TOK, 128], [1, 512]])), dsm)
                        ev_ro[(kr, "d")] = ctx.cnt[dsm]
                    B.sync(ld5)

                    def mk_s5(kr=kr, ldt=ldt, uu=uu,
                              vprev=ev_ro.get((kr - 2, "v"), 0)):
                        def f(s, ldv=ev_ro[(kr, "d")]):
                            if kr == 0:
                                ctx.w(s, "G", ev_bc[0])
                            ctx.w(s, "dR0" if kr % 2 == 0 else "dR1", ldv)
                            if vprev > 0:
                                s.wait_ge(ctx.sem["V"], vprev)
                            got = ctx.b(s.activation(
                                uu[:, 0:512], ldt[:, 0:512], AF.Copy,
                                bias=MAGIC, scale=bc[:, 5:6]), "S")
                            assert got == S10 + kr + 1
                        return f
                    p10_lams.append(mk_s5())
                    ev_ro[(kr, "s")] = S10 + kr + 1

                    def v5f(v, kr=kr, ic=ic, uu=uu, robc=robc):
                        ctx.w(v, "S", ev_ro[(kr, "s")])
                        ev_ro[(kr, "v")] = ctx.b(v.tensor_single_scalar(
                            robc[:, ic * 512:(ic + 1) * 512], uu[:, 0:512],
                            -MAGIC, ALU.add), "V")
                    B.vector(v5f)

                ro_done = ctx.cnt["V"]

                for half in range(2):
                    banks = [ps0, ps1, ps2, ps3] if half == 0 \
                        else [ps4, ps5, ps7, ps0]

                    def merged(tr8=tr8, half=half, ro_done=ro_done,
                               banks=banks, robc=robc):
                        def mm5(t, tr8=tr8, half=half, robc=robc):
                            ctx.w(t, "dW50", wdq_tot[0])
                            if half == 0:
                                ctx.w(t, "V")
                            else:
                                ctx.w(t, "V", ev_cmb[(tr8 * 4, 0)])
                            for ic in range(ICH):
                                for tj in range(2):
                                    tt = half * 2 + tj
                                    for ch in range(2):
                                        ins = t.matmul(
                                            banks[tj * 2 + ch][:, :],
                                            robc[:, ic * 512 + tt * 128:ic * 512 + (tt + 1) * 128],
                                            xnT[:, ic * 1024 + ch * 512:ic * 1024 + (ch + 1) * 512],
                                            start=(ic == 0),
                                            stop=(ic == ICH - 1))
                            ev_dn[(tr8, half)] = ctx.b(ins, "T")
                        B.tensor(mm5)
                    merged()

                    for tj in range(2):
                        tt = half * 2 + tj
                        gtt = tr8 * 4 + tt
                        xt = xt0 if gtt % 2 == 0 else xt1

                        def ldx(sync, gtt=gtt, xt=xt, tr8=tr8, half=half,
                                tj=tj):
                            if ev_cmb:
                                ctx.w(sync, "V")
                            dsm = "dX0" if gtt % 2 == 0 else "dX1"
                            ctx.b16(sync.dma_start(
                                out=xt[:, :],
                                in_=x_d[gtt * 128:(gtt + 1) * 128, :]), dsm)
                            ev_x10[gtt] = (dsm, ctx.cnt[dsm])
                        B.sync(ldx)

                        for ch in range(2):
                            sh = stgh0 if ch == 0 else stgh1
                            ou = outs0 if ch == 0 else outs1

                            def lda2(sync, gtt=gtt, ch=ch, sh=sh):
                                ctx.w(sync, "V")
                                dsm = "dAO0" if ch == 0 else "dAO1"
                                ctx.b16(sync.dma_start(
                                    out=sh[:, :],
                                    in_=AP(ao_sp.tensor,
                                           gtt * 128 * C + ch * 512,
                                           [[C, 128], [1, 512]])), dsm)
                                ev_ao[(gtt, ch)] = (dsm, ctx.cnt[dsm])
                            B.sync(lda2)

                            def cmb2(v, tr8=tr8, half=half, tj=tj, ch=ch,
                                     gtt=gtt, xt=xt, sh=sh, ou=ou):
                                ctx.w(v, "T", ev_dn[(tr8, half)])
                                ctx.wt(v, ev_ao[(gtt, ch)])
                                ctx.wt(v, ev_x10[gtt])
                                ctx.w(v, "dOUT0" if ch == 0 else "dOUT1",
                                      ev_ost_last[ch])
                                bank = ([ps0, ps1, ps2, ps3]
                                        if half == 0
                                        else [ps4, ps5, ps7, ps0])[tj * 2 + ch]
                                ev_cmb[(gtt, ch)] = ctx.b(
                                    v.scalar_tensor_tensor(
                                        ou[:, :], bank[:, :], bc[:, 6:7],
                                        sh[:, :], ALU.mult, ALU.add), "V")
                            B.vector(cmb2)

                            def gadd(g, gtt=gtt, ch=ch, ou=ou, xt=xt):
                                ctx.w(g, "V", ev_cmb[(gtt, ch)])
                                ev_cmb[(gtt, ch, "g")] = ctx.b(g.tensor_add(
                                    ou[:, :], ou[:, :],
                                    xt[:, ch * 512:(ch + 1) * 512]), "G")
                            B.gpsimd(gadd)

                            def ost(g, gtt=gtt, ch=ch, ou=ou):
                                ctx.w(g, "G", ev_cmb[(gtt, ch, "g")])
                                ev_ost[(gtt, ch)] = ctx.b16(g.dma_start(
                                    out=out_d[gtt * 128:(gtt + 1) * 128,
                                              ch * 512:(ch + 1) * 512],
                                    in_=ou[:, :]),
                                    "dOUT0" if ch == 0 else "dOUT1")
                                ev_ost_last[ch] = ev_ost[(gtt, ch)]
                            B.gpsimd(ost)

            def p10_sbody(s):
                for f in p10_lams:
                    f(s)
            B.scalar(p10_sbody)
            ctx.barrier()

    return nc


_NC_CACHE = [None]


def kernel(L0_attn_x, L0_attn_w_qkv, L0_attn_w_proj, L0_attn_b_proj,
           L0_ffn_w_gate, L0_ffn_w_up, L0_ffn_w_down):
    x = np.ascontiguousarray(np.asarray(L0_attn_x, dtype=np.float32))
    wqkv = np.ascontiguousarray(np.asarray(L0_attn_w_qkv, dtype=np.float32))
    wproj = np.ascontiguousarray(np.asarray(L0_attn_w_proj, dtype=np.float32))
    bproj = np.ascontiguousarray(np.asarray(L0_attn_b_proj, dtype=np.float32))
    wg = np.ascontiguousarray(np.asarray(L0_ffn_w_gate, dtype=np.float32))
    wu = np.ascontiguousarray(np.asarray(L0_ffn_w_up, dtype=np.float32))
    wd = np.ascontiguousarray(np.asarray(L0_ffn_w_down, dtype=np.float32))

    wqkT = np.ascontiguousarray(wqkv.transpose(0, 2, 1))
    wpT = np.ascontiguousarray(wproj.T)
    wgT = np.ascontiguousarray(wg.T)
    wuT = np.ascontiguousarray(wu.T)
    wdT = np.ascontiguousarray(wd.T)

    identb = np.eye(128, dtype=ml_dtypes.bfloat16)
    identf = np.eye(128, dtype=np.float32)
    maskneg = np.triu(np.full((128, 128), -1.0e30, dtype=np.float32), 1)
    ones1 = np.ones((1, 128), dtype=ml_dtypes.bfloat16)

    if _NC_CACHE[0] is None:
        _NC_CACHE[0] = build_nc()
    nc = _NC_CACHE[0]

    shared = dict(wqkT=wqkT, wpT=wpT, bproj=bproj.reshape(1, C),
                  wg=wg, wgT=wgT, wu=wu, wuT=wuT, wd=wd, wdT=wdT,
                  identb_in=identb, identf_in=identf,
                  maskneg_in=maskneg, ones1_in=ones1)
    in_maps = []
    for c in range(NCORE):
        m = dict(shared)
        m["x"] = np.ascontiguousarray(
            x[2 * c:2 * c + 2].reshape(TOK, C))
        in_maps.append(m)

    res = run_bass_kernel_spmd(nc, in_maps, core_ids=list(range(NCORE)))
    out = np.concatenate(
        [r["out"].reshape(2, T, C) for r in res.results], axis=0)
    return out.astype(np.float32)
